# version 03
# baseline (speedup 1.0000x reference)
import numpy as np
from contextlib import ExitStack, contextmanager

import concourse.bass as bass
import concourse.mybir as mybir
from concourse.bass_utils import run_bass_kernel_spmd

F32 = mybir.dt.float32
BF16 = mybir.dt.bfloat16
I16 = mybir.dt.int16
AF = mybir.ActivationFunctionType
ALU = mybir.AluOpType

D = 1024
KC = 8
TLOC = 2048
NCH = 16
NH = 16
HD = 64
QB = 512
MAGIC = 12582912.0
X = mybir.AxisListType.X
ENGINES = ("sync", "scalar", "vector", "tensor", "gpsimd")
DEBUG_ATTN = False
DEBUG_FFN = False
DEBUG_BLOCK = False


class Sem:
    __slots__ = ("h", "n")

    def __init__(self, h):
        self.h = h
        self.n = 0


CHAIN_DEFAULT = {"vector": True, "scalar": True, "gpsimd": True,
                 "sync": False, "tensor": False}


class Prog:
    def __init__(self, nc, ctx):
        self.nc = nc
        self.ctx = ctx
        self.q = {e: [] for e in ENGINES}
        self._sid = 0
        self._chain = {}

    def sem(self, name=None):
        self._sid += 1
        h = self.ctx.enter_context(self.nc.semaphore(name or f"sem{self._sid}"))
        return Sem(h)

    def chain_sem(self, eng):
        if eng not in self._chain:
            self._chain[eng] = self.sem(f"ch_{eng}")
        return self._chain[eng]

    def do(self, eng, fn, waits=(), incs=(), chain=None):
        if chain is None:
            chain = CHAIN_DEFAULT[eng]
        waits = list(waits)
        incs = list(incs)
        if chain:
            cs = self.chain_sem(eng)
            waits.append((cs, cs.n))
            incs.append((cs, 1))
        wl = [(s.h, int(v)) for (s, v) in waits]
        il = []
        for (s, a) in incs:
            s.n += a
            il.append((s.h, a))
        self.q[eng].append((fn, wl, il))

    def dma(self, eng, out, in_, sem, waits=()):
        self.do(eng, (lambda e, o=out, i=in_: e.dma_start(out=o, in_=i)),
                waits=list(waits) + [(sem, sem.n)], incs=[(sem, 16)], chain=False)

    def finalize(self):
        with self.nc.allow_low_precision(reason="bf16 elementwise by design"), \
                self.nc.Block() as block:
            for eng in ENGINES:
                ops = self.q[eng]
                if ops:
                    getattr(block, eng)(lambda e, ops=ops: Prog._run(e, ops))

    @staticmethod
    def _run(e, ops):
        for fn, wl, il in ops:
            byh = {}
            for h, v in wl:
                if byh.get(id(h), (None, -1))[1] < v:
                    byh[id(h)] = (h, v)
            for h, v in byh.values():
                e.wait_ge(h, v)
            inst = fn(e)
            for h, a in il:
                inst = inst.then_inc(h, a, skip_validation=(a == 16))


def cur(sem):
    return (sem, sem.n)



class Build:
    def __init__(self, subs):
        self.subs = list(subs)
        self.nc = bass.Bass("TRN2", target_bir_lowering=False)
        self.ctx = ExitStack()
        self.P = Prog(self.nc, self.ctx)
        self.phase = None
        self.psum_guard = []
        self.uid = 0

    def name(self, pfx):
        self.uid += 1
        return f"{pfx}{self.uid}"

    def _stack(self):
        return self.phase if self.phase is not None else self.ctx

    def sb(self, shape, dtype, pfx="t"):
        return self._stack().enter_context(
            self.nc.sbuf_tensor(self.name(pfx), shape, dtype))

    def ps(self, shape, dtype=F32, pfx="p"):
        return self._stack().enter_context(
            self.nc.psum_tensor(self.name(pfx), shape, dtype))


    def build(self):
        nc = self.nc
        self.xs = nc.dram_tensor("xs", [TLOC, D], F32, kind="ExternalInput").ap()
        self.ys = nc.dram_tensor("ys", [TLOC, D], F32, kind="ExternalOutput").ap()
        self.cconst = nc.dram_tensor("cconst", [1, 8], F32, kind="ExternalInput").ap()
        wnames = [
            ("L0_attn_a_query_proj", [16, 64, 1024]),
            ("L0_attn_a_key_proj", [1024, 64]),
            ("L0_attn_a_value_proj", [1024, 64]),
            ("L0_attn_a_output_proj", [1024, 16, 64]),
            ("L0_attn_b_w_q", [16, 64, 1024]),
            ("L0_attn_b_w_k", [16, 64, 1024]),
            ("L0_attn_b_w_v", [16, 64, 1024]),
            ("L0_attn_b_w_o", [1024, 1024]),
            ("L0_ffn_weight", [1024, 1024]),
            ("L0_ffn_bias", [1024]),
            ("L1_attn_a_Wq", [1024, 16, 64]),
            ("L1_attn_a_Wkv_down", [1024, 256]),
            ("L1_attn_a_Wk_up", [256, 16, 64]),
            ("L1_attn_a_Wv_up", [256, 16, 64]),
            ("L1_attn_a_Wout", [16, 64, 1024]),
            ("L1_attn_b_Q_w", [16, 1024, 64]),
            ("L1_attn_b_Q_b", [16, 64]),
            ("L1_attn_b_K_w", [16, 1024, 64]),
            ("L1_attn_b_K_b", [16, 64]),
            ("L1_attn_b_V_w", [16, 1024, 64]),
            ("L1_attn_b_V_b", [16, 64]),
            ("L1_attn_b_O_w", [16, 64, 1024]),
            ("L1_attn_b_O_b", [1024]),
            ("L1_ffn_router", [1024, 8]),
            ("L1_ffn_experts_gate", [8, 1024, 512]),
            ("L1_ffn_experts_up", [8, 1024, 512]),
            ("L1_ffn_experts_down", [8, 512, 1024]),
            ("L1_ffn_shared_gate", [1024, 1024]),
            ("L1_ffn_shared_up", [1024, 1024]),
            ("L1_ffn_shared_down", [1024, 1024]),
        ]
        self.w = {}
        for n, shp in wnames:
            self.w[n] = nc.dram_tensor(n, shp, F32, kind="ExternalInput").ap()

        self.maskd = nc.dram_tensor("maskd", [128, 32 * QB], mybir.dt.float8e4)

        P = self.P
        self.s_ld = P.sem("s_ld")
        self.s_sc = P.sem("s_sc")
        self.s_gp = P.sem("s_gp")
        self.s_t = P.sem("s_t")
        self.s_v = P.chain_sem("vector")
        self.s_s = P.chain_sem("scalar")
        self.s_g = P.chain_sem("gpsimd")
        self.s_cc = P.sem("s_cc")

        self.r_sb = self.sb([128, NCH * D], F32, "r")
        self.xlnT = self.sb([128, KC * TLOC], BF16, "xlnT")
        self.identb = self.sb([128, 128], BF16, "identb")
        self.identf = self.sb([128, 128], F32, "identf")
        self.ones1b = self.sb([1, 128], BF16, "ones1b")
        self.ccs = self.sb([128, 8], F32, "ccs")
        self.stat_sum = self.sb([128, NCH], F32, "stsum")
        self.stat_sq = self.sb([128, NCH], F32, "stsq")
        self.stat_mean = self.sb([128, NCH], F32, "stmean")
        self.stat_rstd = self.sb([128, NCH], F32, "strstd")
        self.stat_tmp = self.sb([128, NCH], F32, "sttmp")
        self.cq = self.sb([128, D], BF16, "cq")

        self.startup()
        seq = [self.sub_l0a, self.sub_l0ffn, self.sub_l0b, self.sub_l0ffn,
               self.sub_l1a, self.sub_l1moe, self.sub_l1b, self.sub_l1moe]
        for si in self.subs:
            self.emit_barrier()
            with ExitStack() as st:
                self.phase = st
                seq[si]()
            self.phase = None
            self.psum_guard = [cur(self.s_v), cur(self.s_s)]
        self.store_out()
        self.P.finalize()
        self.ctx.close()
        return self.nc

    def emit_barrier(self):
        P = self.P
        waits = [cur(s) for s in (self.s_ld, self.s_sc, self.s_gp, self.s_t,
                                  self.s_cc, self.s_v, self.s_s, self.s_g)]
        for eng in ("tensor", "sync"):
            P.do(eng, lambda e: e.nop(), waits=waits, chain=False)
        for eng in ("vector", "scalar", "gpsimd"):
            P.do(eng, lambda e: e.nop(), waits=waits)


    def startup(self):
        P = self.P
        P.dma("scalar", self.ccs[:], self.cconst.broadcast_to((128, 8)), self.s_sc)
        self.ccs_ready = cur(self.s_sc)
        P.do("gpsimd", lambda g: g.memset(self.identb[:], 0.0))
        P.do("gpsimd", lambda g: g.memset(self.identf[:], 0.0))
        P.do("gpsimd", lambda g: g.memset(self.ones1b[:], 1.0))
        P.do("gpsimd", lambda g: g.affine_select(
            out=self.identb[:], in_=self.identb[:], compare_op=ALU.not_equal,
            fill=1.0, base=0, pattern=[[-1, 128]], channel_multiplier=1))
        P.do("gpsimd", lambda g: g.affine_select(
            out=self.identf[:], in_=self.identf[:], compare_op=ALU.not_equal,
            fill=1.0, base=0, pattern=[[-1, 128]], channel_multiplier=1))
        self.ident_ready = cur(self.s_g)

        xload_waits = []
        if any(s in (2, 4, 6) for s in self.subs):
            mk8c = self.mk8c = self.sb([128, QB], mybir.dt.float8e4, "mk8c")
            P.do("gpsimd", lambda g: g.iota(
                self.r_sb[:, 0:32 * QB].rearrange("p (o q) -> p o q", q=QB),
                [[-128, 32], [1, QB]], base=0, channel_multiplier=-1,
                allow_small_or_imprecise_dtypes=True))
            iota_done = cur(self.s_g)
            for o in range(32):
                wv = [iota_done, self.ccs_ready] if o == 0 else [cur(self.s_gp)]
                P.do("vector", lambda v, o=o: v.tensor_scalar(
                    out=mk8c[:], in0=self.r_sb[:, o * QB:(o + 1) * QB],
                    scalar1=self.ccs[:, 0:1], scalar2=0.0,
                    op0=ALU.add, op1=ALU.is_ge), waits=wv)
                P.dma("gpsimd", self.maskd.ap()[:, o * QB:(o + 1) * QB], mk8c[:],
                      self.s_gp, waits=[cur(self.s_v)])
            self.mask_stored = cur(self.s_gp)
            xload_waits = [cur(self.s_v), self.mask_stored]
        P.dma("sync", self.r_sb[:].rearrange("p (i d) -> p i d", d=D),
              self.xs.rearrange("(i p) d -> p i d", p=128), self.s_ld,
              waits=xload_waits)
        if DEBUG_BLOCK:
            dt_ = self.nc.dram_tensor("dbg_r_postload", [128, NCH * D], F32,
                                      kind="ExternalOutput")
            P.dma("gpsimd", dt_.ap(), self.r_sb[:], self.s_gp,
                  waits=[cur(self.s_ld)])
            snap = cur(self.s_gp)
            for engx in ("vector", "scalar", "tensor", "sync", "gpsimd"):
                P.do(engx, lambda e: e.nop(), waits=[snap], chain=False)

    def store_out(self):
        P = self.P
        P.dma("gpsimd", self.ys.rearrange("(i p) d -> p i d", p=128),
              self.r_sb[:].rearrange("p (i d) -> p i d", d=D), self.s_gp,
              waits=[cur(self.s_ld), cur(self.s_v), cur(self.s_t)])
        P.do("gpsimd", lambda g: g.nop(), waits=[cur(self.s_gp)])


    def emit_ln_stats(self):
        P = self.P
        w0 = [cur(self.s_ld), cur(self.s_v), cur(self.s_t)]
        for i in range(NCH):
            ch = self.r_sb[:, i * D:(i + 1) * D]
            P.do("vector", lambda v, ch=ch, i=i: v.reduce_sum(
                out=self.stat_sum[:, i:i + 1], in_=ch, axis=X),
                waits=w0 if i == 0 else [])
            P.do("vector", lambda v, ch=ch, i=i: v.scalar_tensor_tensor(
                out=self.cq[:], in0=ch, scalar=1.0, in1=ch,
                op0=ALU.mult, op1=ALU.mult, accum_out=self.stat_sq[:, i:i + 1]))
        P.do("vector", lambda v: v.tensor_scalar_mul(
            self.stat_mean[:], self.stat_sum[:], 1.0 / D))
        P.do("vector", lambda v: v.tensor_tensor(
            out=self.stat_tmp[:], in0=self.stat_mean[:], in1=self.stat_mean[:],
            op=ALU.mult))
        P.do("vector", lambda v: v.scalar_tensor_tensor(
            out=self.stat_tmp[:], in0=self.stat_sq[:], scalar=1.0 / D,
            in1=self.stat_tmp[:], op0=ALU.mult, op1=ALU.subtract))
        P.do("scalar", lambda s: s.activation(
            out=self.stat_tmp[:], in_=self.stat_tmp[:], func=AF.Sqrt, bias=self.ccs[:, 2:3]),
            waits=[cur(self.s_v), self.ccs_ready])
        P.do("vector", lambda v: v.reciprocal(
            out=self.stat_rstd[:], in_=self.stat_tmp[:]),
            waits=[cur(self.s_s)])
        self.ln_ready = cur(self.s_v)

    def emit_ln_chunk(self, i, out_ap, waits=(), inc=False):
        P = self.P
        P.do("vector", lambda v, i=i, o=out_ap: v.tensor_scalar(
            out=o, in0=self.r_sb[:, i * D:(i + 1) * D],
            scalar1=self.stat_mean[:, i:i + 1], scalar2=self.stat_rstd[:, i:i + 1],
            op0=ALU.subtract, op1=ALU.mult), waits=list(waits))

    def emit_xlnT(self, quant=False):
        P = self.P
        ptA = self.ps([128, KC * 128], BF16, "ptT")
        ptB = self.ps([128, KC * 128], BF16, "ptT")
        xv = self.xlnT[:].rearrange("p (kc t) -> p kc t", t=TLOC)
        evac_done = [None, None]
        tr_done_prev = None
        for i in range(NCH):
            w = [tr_done_prev] if tr_done_prev is not None else []
            if quant:
                self.emit_ln_chunk(i, self.c32a[:], waits=w)
                self.emit_quant_chunk()
            else:
                self.emit_ln_chunk(i, self.cq[:], waits=w, inc=True)
            src_ready = cur(self.s_v)
            pt = ptA if i % 2 == 0 else ptB
            wa = [src_ready, self.ident_ready] + (self.psum_guard if i < 2 else [])
            if evac_done[i % 2] is not None:
                wa.append(evac_done[i % 2])
            for kc in range(KC):
                P.do("tensor", lambda t, kc=kc, pt=pt: t.transpose(
                    pt[:, kc * 128:(kc + 1) * 128],
                    self.cq[:, kc * 128:(kc + 1) * 128], self.identb[:]),
                    waits=wa if kc == 0 else [],
                    incs=[(self.s_t, 1)] if kc == KC - 1 else [])
            tr_done_prev = cur(self.s_t)
            P.do("vector", lambda v, i=i, pt=pt: v.tensor_copy(
                out=xv[:, :, i * 128:(i + 1) * 128],
                in_=pt[:].rearrange("p (kc t) -> p kc t", t=128)),
                waits=[tr_done_prev])
            evac_done[i % 2] = cur(self.s_v)
        self.xlnT_ready = cur(self.s_v)

    def emit_quant_chunk(self):
        P = self.P
        a, b, c, d = self.c32a, self.c32b, self.c32c, self.c32d
        qs, qr = self.qscale, self.qrec
        a3 = a[:].rearrange("p (blk j) -> p blk j", j=32)
        b3 = b[:].rearrange("p (blk j) -> p blk j", j=32)
        qrb = qr[:].rearrange("p (blk one) -> p blk one", one=1).broadcast_to((128, 32, 32))
        qsb = qs[:].rearrange("p (blk one) -> p blk one", one=1).broadcast_to((128, 32, 32))
        P.do("vector", lambda v: v.reduce_max(
            out=qs[:], in_=a3, axis=X, apply_absolute_value=True))
        P.do("vector", lambda v: v.tensor_scalar(
            out=qs[:], in0=qs[:], scalar1=1.0 / 7.5, scalar2=1e-30,
            op0=ALU.mult, op1=ALU.max))
        P.do("vector", lambda v: v.reciprocal(out=qr[:], in_=qs[:]))
        P.do("vector", lambda v: v.tensor_tensor(out=b3, in0=a3, in1=qrb, op=ALU.mult))
        P.do("vector", lambda v: v.scalar_tensor_tensor(
            out=c[:], in0=b[:], scalar=-1.0, in1=b[:], op0=ALU.mult, op1=ALU.max))
        P.do("vector", lambda v: v.tensor_single_scalar(
            out=d[:], in_=c[:], scalar=2.0, op=ALU.is_ge))
        P.do("vector", lambda v: v.tensor_single_scalar(
            out=c[:], in_=c[:], scalar=4.0, op=ALU.is_ge))
        P.do("vector", lambda v: v.tensor_scalar(
            out=d[:], in0=d[:], scalar1=-4.0, scalar2=8.0, op0=ALU.mult, op1=ALU.add))
        P.do("vector", lambda v: v.scalar_tensor_tensor(
            out=c[:], in0=c[:], scalar=-2.0, in1=d[:], op0=ALU.mult, op1=ALU.add))
        P.do("vector", lambda v: v.tensor_tensor(out=b[:], in0=b[:], in1=c[:], op=ALU.mult))
        P.do("vector", lambda v: v.tensor_scalar(
            out=b[:], in0=b[:], scalar1=MAGIC, scalar2=MAGIC,
            op0=ALU.add, op1=ALU.subtract))
        P.do("vector", lambda v: v.reciprocal(out=c[:], in_=c[:]))
        P.do("vector", lambda v: v.tensor_tensor(out=b[:], in0=b[:], in1=c[:], op=ALU.mult))
        cq3 = self.cq[:].rearrange("p (blk j) -> p blk j", j=32)
        P.do("vector", lambda v: v.tensor_tensor(out=cq3, in0=b3, in1=qsb, op=ALU.mult))

    def alloc_quant_scratch(self):
        self.c32a = self.sb([128, D], F32, "qa")
        self.c32b = self.sb([128, D], F32, "qb")
        self.c32c = self.sb([128, D], F32, "qc")
        self.c32d = self.sb([128, D], F32, "qd")
        self.qscale = self.sb([128, 32], F32, "qsc")
        self.qrec = self.sb([128, 32], F32, "qrc")


    def prep_direct(self, dst, src_hbm, kdim, fdim):
        kcn = kdim // 128
        self.P.dma("gpsimd",
                   dst[:, 0:kcn * fdim].rearrange("p (k f) -> p k f", f=fdim),
                   src_hbm.rearrange("(k p) f -> p k f", p=128), self.s_gp)
        return cur(self.s_gp)

    def prep_transposed(self, dst, src_hbm, fdim, kdim, stage, pst,
                        extra_waits=()):
        P = self.P
        fc = fdim // 128
        kcn = kdim // 128
        ev_done = [None, None]
        tr_prev = None
        dv = dst[:, 0:kcn * fdim].rearrange("p (k f) -> p k f", f=fdim)
        for f in range(fc):
            par = f % 2
            stg = stage[:, 0:kdim]
            wa = ([tr_prev] if tr_prev is not None else []) + \
                (list(extra_waits) if f == 0 else [])
            P.dma("gpsimd", stg,
                  src_hbm.rearrange("(a p) k -> p a k", p=128)[:, f, :],
                  self.s_gp, waits=wa)
            ld = cur(self.s_gp)
            pt = pst[par]
            wt = [ld, self.ident_ready] + \
                ([ev_done[par]] if ev_done[par] is not None else []) + \
                ((list(extra_waits) + self.psum_guard) if f == 0 else [])
            for k in range(kcn):
                P.do("tensor", lambda t, k=k, stg=stg, pt=pt: t.transpose(
                    pt[:, k * 128:(k + 1) * 128], stg[:, k * 128:(k + 1) * 128],
                    self.identb[:]),
                    waits=wt if k == 0 else [],
                    incs=[(self.s_t, 1)] if k == kcn - 1 else [])
            trd = cur(self.s_t)
            tr_prev = trd
            P.do("vector", lambda v, f=f, pt=pt, kcn=kcn: v.tensor_copy(
                out=dv[:, :, f * 128:(f + 1) * 128],
                in_=pt[:, 0:kcn * 128].rearrange("p (k t) -> p k t", t=128)),
                waits=[trd] + (list(extra_waits) if f == 0 else []))
            ev_done[par] = cur(self.s_v)
        return cur(self.s_v)


    def emit_proj_fm(self, warr, outF, out_sb, out_cols, t0, t1, wready,
                     scale=1.0, biascol=None, src=None, src_kc=KC,
                     extra_waits=(), src_ready=None, out_dtype_f32=False):
        P = self.P
        if src is None:
            src = self.xlnT
        if src_ready is None:
            src_ready = self.xlnT_ready
        fcn = outF // 128
        pj = [self.ps([128, 512], F32, "pj"), self.ps([128, 512], F32, "pj")]
        ev = [None, None]
        wv = warr[:].rearrange("p (k f) -> p k f", f=outF)
        sv = src[:].rearrange("p (k t) -> p k t", t=TLOC)
        idx = 0
        for f in range(fcn):
            for tt in range(t0, t1, 512):
                par = idx % 2
                idx += 1
                wa = [wready, src_ready] + list(extra_waits)
                if ev[par] is not None:
                    wa.append(ev[par])
                for k in range(src_kc):
                    P.do("tensor", lambda t, k=k, f=f, tt=tt, par=par: t.matmul(
                        pj[par][:], wv[:, k, f * 128:(f + 1) * 128],
                        sv[:, k, tt:tt + 512],
                        start=(k == 0), stop=(k == src_kc - 1)),
                        waits=wa if k == 0 else [],
                        incs=[(self.s_t, 1)] if k == src_kc - 1 else [])
                mmd = cur(self.s_t)
                o = out_sb[:, f * out_cols + tt - t0:f * out_cols + tt - t0 + 512]
                if biascol is not None:
                    P.do("vector", lambda v, o=o, par=par, f=f: v.tensor_scalar(
                        out=o, in0=pj[par][:], scalar1=scale,
                        scalar2=biascol[:, f:f + 1], op0=ALU.mult, op1=ALU.add),
                        waits=[mmd])
                elif scale != 1.0:
                    P.do("vector", lambda v, o=o, par=par: v.tensor_scalar_mul(
                        o, pj[par][:], scale), waits=[mmd])
                else:
                    P.do("vector", lambda v, o=o, par=par: v.tensor_copy(
                        out=o, in_=pj[par][:]), waits=[mmd])
                ev[par] = cur(self.s_v)
        return cur(self.s_v)

    def emit_tokproj_add(self, lhs_sb, lhs_cols, rhs_sb, rhs_cols, kcn, chunks,
                         lready, rready, bias_ready=None, extra_waits=(),
                         lhs_view=None, rhs_view=None):
        P = self.P
        py = [self.ps([128, D], F32, "py"), self.ps([128, D], F32, "py")]
        ev = [None, None]
        lv = lhs_view if lhs_view is not None else \
            lhs_sb[:].rearrange("p (k t) -> p k t", t=lhs_cols)
        rv = rhs_view if rhs_view is not None else \
            rhs_sb[:].rearrange("p (k f) -> p k f", f=rhs_cols)
        br = self.biasrow if bias_ready is not None else None
        for n, (ci, toff) in enumerate(chunks):
            par = n % 2
            wa = [lready, rready] + list(extra_waits) + \
                (self.psum_guard if n < 2 else [])
            if bias_ready is not None:
                wa.append(bias_ready)
            if ev[par] is not None:
                wa.append(ev[par])
            nb = 1 if bias_ready is not None else 0
            for hh in range(2):
                for k in range(kcn):
                    first = (hh == 0 and k == 0)
                    last = (k == kcn - 1 and nb == 0)
                    P.do("tensor",
                         lambda t, k=k, hh=hh, toff=toff, par=par, first=first, last=last:
                         t.matmul(py[par][:, hh * 512:(hh + 1) * 512],
                                  lv[:, k, toff:toff + 128],
                                  rv[:, k, hh * 512:(hh + 1) * 512],
                                  start=(k == 0), stop=last),
                         waits=wa if first else [],
                         incs=[(self.s_t, 1)] if (last and hh == 1) else [])
                if nb:
                    P.do("tensor", lambda t, hh=hh, par=par, br=br: t.matmul(
                        py[par][:, hh * 512:(hh + 1) * 512], self.ones1b[:],
                        br[0:1, hh * 512:(hh + 1) * 512],
                        start=False, stop=True),
                        incs=[(self.s_t, 1)] if hh == 1 else [])
            mmd = cur(self.s_t)
            P.do("vector", lambda v, ci=ci, par=par: v.scalar_tensor_tensor(
                out=self.r_sb[:, ci * D:(ci + 1) * D], in0=py[par][:], scalar=1.0,
                in1=self.r_sb[:, ci * D:(ci + 1) * D], op0=ALU.mult, op1=ALU.add),
                waits=[mmd])
            ev[par] = cur(self.s_v)
        return cur(self.s_v)


    def sub_l0ffn(self):
        P = self.P
        self.biasrow = self.sb([1, D], BF16, "biasrow")
        self.alloc_quant_scratch()
        wstage32 = self.sb([128, KC * D], F32, "w32")
        wq_sb = self.sb([128, KC * D], BF16, "wq")
        wdT = self.sb([128, KC * D], BF16, "wdT")

        P.dma("sync", wstage32[:].rearrange("p (a k) -> p a k", k=D),
              self.w["L0_ffn_weight"].rearrange("(a p) k -> p a k", p=128), self.s_ld)
        wld = cur(self.s_ld)
        c32a_l, cq_l = self.c32a, self.cq
        for a in range(KC):
            P.do("vector", lambda v, a=a, c32a_l=c32a_l: v.tensor_copy(
                out=c32a_l[:], in_=wstage32[:, a * D:(a + 1) * D]),
                waits=[wld] if a == 0 else [])
            self.emit_quant_chunk()
            P.do("vector", lambda v, a=a, cq_l=cq_l: v.tensor_copy(
                out=wq_sb[:, a * D:(a + 1) * D], in_=cq_l[:]))
        wq_done = cur(self.s_v)
        ptw = self.ps([128, KC * 128], BF16, "pw")
        wv = wdT[:].rearrange("p (k j) -> p k j", j=D)
        prev_ev = None
        for a in range(KC):
            wt = [wq_done, self.ident_ready] + ([prev_ev] if prev_ev is not None else [])
            for k in range(KC):
                P.do("tensor", lambda t, a=a, k=k: t.transpose(
                    ptw[:, k * 128:(k + 1) * 128],
                    wq_sb[:, a * D + k * 128:a * D + (k + 1) * 128], self.identb[:]),
                    waits=wt if k == 0 else [],
                    incs=[(self.s_t, 1)] if k == KC - 1 else [])
            trd = cur(self.s_t)
            P.do("vector", lambda v, a=a: v.tensor_copy(
                out=wv[:, :, a * 128:(a + 1) * 128],
                in_=ptw[:].rearrange("p (k j) -> p k j", j=128)),
                waits=[trd])
            prev_ev = cur(self.s_v)
        wdT_ready = cur(self.s_v)
        P.dma("gpsimd", self.biasrow[:],
              self.w["L0_ffn_bias"].rearrange("(one d) -> one d", one=1), self.s_gp)
        bias_ready = cur(self.s_gp)

        if DEBUG_FFN:
            self.uid += 1
            dt_ = self.nc.dram_tensor(f"dbg_wdT{self.uid}", [128, KC * D], BF16,
                                      kind="ExternalOutput")
            P.dma("gpsimd", dt_.ap(), wdT[:], self.s_gp, waits=[wdT_ready])
            dt2_ = self.nc.dram_tensor(f"dbg_bias{self.uid}", [1, D], BF16,
                                       kind="ExternalOutput")
            P.dma("gpsimd", dt2_.ap(), self.biasrow[:], self.s_gp,
                  waits=[bias_ready])
            dt3_ = self.nc.dram_tensor(f"dbg_wq{self.uid}", [128, KC * D], BF16,
                                       kind="ExternalOutput")
            P.dma("gpsimd", dt3_.ap(), wq_sb[:], self.s_gp, waits=[wq_done])
        self.emit_ln_stats()
        self.emit_xlnT(quant=True)
        if DEBUG_BLOCK:
            self.uid += 1
            wdb = [cur(self.s_v), cur(self.s_t), cur(self.s_s), cur(self.s_g),
                   cur(self.s_ld), cur(self.s_sc), cur(self.s_gp)]
            for nmb, buf in (("rpre", self.r_sb), ("xdT", self.xlnT)):
                dt_ = self.nc.dram_tensor(f"dbg_{nmb}{self.uid}", list(buf.shape),
                                          buf[:].dtype, kind="ExternalOutput")
                P.dma("gpsimd", dt_.ap(), buf[:], self.s_gp, waits=wdb)
            snap = cur(self.s_gp)
            for engx in ("vector", "scalar", "tensor", "sync", "gpsimd"):
                P.do(engx, lambda e: e.nop(), waits=[snap], chain=False)
        self.emit_tokproj_add(self.xlnT, TLOC, wdT, D, KC,
                              [(i, i * 128) for i in range(NCH)],
                              self.xlnT_ready, wdT_ready, bias_ready=bias_ready)
        if DEBUG_BLOCK:
            self.uid += 1
            wdb = [cur(self.s_v), cur(self.s_t)]
            dt_ = self.nc.dram_tensor(f"dbg_rpost{self.uid}", [128, NCH * D], F32,
                                      kind="ExternalOutput")
            P.dma("gpsimd", dt_.ap(), self.r_sb[:], self.s_gp, waits=wdb)
            snap = cur(self.s_gp)
            for engx in ("vector", "scalar", "tensor", "sync", "gpsimd"):
                P.do(engx, lambda e: e.nop(), waits=[snap], chain=False)


    @contextmanager
    def subphase(self):
        old = self.phase
        with ExitStack() as st:
            self.phase = st
            yield
        self.phase = old
        self.psum_guard = [cur(self.s_v), cur(self.s_s)]

    def emit_bias_col(self, src_hbm, dst_col, scale=1.0):
        P = self.P
        if getattr(self, "_tmp8", None) is None or self._tmp8_phase is not self.phase:
            self._tmp8 = self.sb([8, 128], F32, "b8")
            self._tmp8_phase = self.phase
            self._tmp8_busy = []
        tmp8 = self._tmp8
        prev = list(self._tmp8_busy)
        P.dma("sync", tmp8[:], src_hbm, self.s_ld, waits=prev)
        ld = cur(self.s_ld)
        with self.subphase():
            pb = self.ps([128, 8], F32, "pb")
            P.do("tensor", lambda t: t.transpose(pb[:], tmp8[:],
                                                 self.identf[0:8, 0:8]),
                 waits=[ld, self.ident_ready] + self.psum_guard,
                 incs=[(self.s_t, 1)])
            trd = cur(self.s_t)
            self._tmp8_busy = [trd]
            P.do("vector", lambda v: v.tensor_scalar_mul(dst_col[:], pb[:], scale),
                 waits=[trd])
        return cur(self.s_v)

    def emit_v_pipeline(self, wv_view, outF, wready, src_view, src_kc, src_ready,
                        vfm, vstg, vshard, vbias_col, vbias_ready, stride):
        P = self.P
        fw = min(outF, 128)
        fcn = (outF + 127) // 128
        pj = [self.ps([128, 512], F32, "pvj"), self.ps([128, 512], F32, "pvj")]
        pv = self.ps([128, NCH * fw], BF16, "pv")
        vsv = vstg[:, 0:NCH * stride].rearrange("p (mc c) -> p mc c", c=stride)
        P.do("gpsimd", lambda g: g.memset(vstg[:, 0:NCH * stride], 0.0))
        P.do("gpsimd", lambda g: g.memset(vsv[:, :, 64:65], 1.0))
        if stride == 130:
            P.do("gpsimd", lambda g: g.memset(vsv[:, :, 129:130], 1.0))
        ones_done = cur(self.s_g)
        prev_tr = None
        prev_dma = None
        prev_evd = None
        ev = [None, None]
        for hc in range(fcn):
            for tt in range(4):
                par = tt % 2
                wa = [wready, src_ready] + \
                    (self.psum_guard if (hc == 0 and tt < 2) else [])
                if ev[par] is not None:
                    wa.append(ev[par])
                for k in range(src_kc):
                    P.do("tensor", lambda t, k=k, hc=hc, tt=tt, par=par: t.matmul(
                        pj[par][0:fw, :], wv_view[:, k, hc * fw:hc * fw + fw],
                        src_view[:, k, tt * 512:(tt + 1) * 512],
                        start=(k == 0), stop=(k == src_kc - 1)),
                        waits=wa if k == 0 else [],
                        incs=[(self.s_t, 1)] if k == src_kc - 1 else [])
                mmd = cur(self.s_t)
                wv2 = [mmd]
                if prev_tr is not None and tt == 0:
                    wv2.append(prev_tr)
                if vbias_col is not None:
                    if vbias_ready is not None:
                        wv2.append(vbias_ready)
                    P.do("vector", lambda v, tt=tt, par=par, hc=hc: v.tensor_scalar(
                        out=vfm[0:fw, tt * 512:(tt + 1) * 512], in0=pj[par][0:fw, :],
                        scalar1=1.0, scalar2=vbias_col[0:fw, hc:hc + 1],
                        op0=ALU.mult, op1=ALU.add), waits=wv2)
                else:
                    P.do("vector", lambda v, tt=tt, par=par: v.tensor_copy(
                        out=vfm[0:fw, tt * 512:(tt + 1) * 512], in_=pj[par][0:fw, :]),
                        waits=wv2)
                ev[par] = cur(self.s_v)
            vfm_done = cur(self.s_v)
            for mc in range(NCH):
                wt = [vfm_done, self.ident_ready]
                if mc == 0 and prev_evd is not None:
                    wt.append(prev_evd)
                P.do("tensor", lambda t, mc=mc: t.transpose(
                    pv[:, mc * fw:(mc + 1) * fw],
                    vfm[0:fw, mc * 128:(mc + 1) * 128],
                    self.identb[0:fw, 0:fw]),
                    waits=wt if mc == 0 else [],
                    incs=[(self.s_t, 1)] if mc == NCH - 1 else [])
            prev_tr = cur(self.s_t)
            wv3 = [prev_tr, ones_done]
            if prev_dma is not None:
                wv3.append(prev_dma)
            pvv = pv[:].rearrange("p (mc c) -> p mc c", c=fw)
            P.do("vector", lambda v: v.tensor_copy(
                out=vsv[:, :, 0:64], in_=pvv[:, :, 0:64]), waits=wv3)
            if stride == 130:
                P.do("vector", lambda v: v.tensor_copy(
                    out=vsv[:, :, 65:129], in_=pvv[:, :, 64:128]))
            prev_evd = cur(self.s_v)
            P.dma("scalar", vshard.rearrange("p (hc c) -> p hc c",
                                             c=NCH * stride)[:, hc, :],
                  vstg[:, 0:NCH * stride], self.s_sc, waits=[prev_evd])
            prev_dma = cur(self.s_sc)
        return prev_dma, prev_tr

    def emit_attention(self, nm, causal, shared_kv, kdim_rows,
                       prep_q, prep_k, prep_v, prep_o,
                       q_bias=None, k_bias=None, v_bias=None, o_bias=None,
                       latent=None):
        P = self.P
        nc = self.nc
        stride = 65 if shared_kv else 130
        vsh_cols = NCH * stride * (1 if shared_kv else 8)
        kshard = nc.dram_tensor(f"ksh_{nm}", [kdim_rows, TLOC], BF16)
        vshard = nc.dram_tensor(f"vsh_{nm}", [128, vsh_cols], BF16)
        kg = nc.dram_tensor(f"kg_{nm}", [8 * kdim_rows, TLOC], BF16)
        vg = nc.dram_tensor(f"vg_{nm}", [8 * 128, vsh_cols], BF16)

        q_sb = self.sb([128, KC * TLOC], BF16, "q")
        warr = self.sb([128, KC * D], BF16, "wa")
        wstage = self.sb([128, D], BF16, "ws")
        k_tile = self.sb([128, 8192], BF16, "kt")
        vt = self.sb([128, 64 * stride], BF16, "vt")
        pst = self.sb([128, 2048], BF16, "P")
        rec_sb = self.sb([1, 512], BF16, "rec")
        rb_sb = self.sb([128, 512], BF16, "rb")
        if causal:
            mask8 = self.sb([128, 32 * QB], mybir.dt.float8e4, "mk8")
        qcol = self.sb([128, 8], F32, "qc8")
        kcol = self.sb([128, 8], F32, "kc8")
        vcol = self.sb([128, 8], F32, "vc8")
        vfm = k_tile[:, 0:2048]
        vstg = k_tile[:, 2048:2048 + NCH * stride]

        if DEBUG_BLOCK:
            wdb = [cur(self.s_v), cur(self.s_t), cur(self.s_s), cur(self.s_g),
                   cur(self.s_ld), cur(self.s_sc), cur(self.s_gp)]
            dt_ = nc.dram_tensor(f"dbg_r_start_{nm}", [128, NCH * D], F32,
                                 kind="ExternalOutput")
            P.dma("gpsimd", dt_.ap(), self.r_sb[:], self.s_gp, waits=wdb)
            snap = cur(self.s_gp)
            for engx in ("vector", "scalar", "tensor", "sync", "gpsimd"):
                P.do(engx, lambda e: e.nop(), waits=[snap], chain=False)
        self.emit_ln_stats()
        with self.subphase():
            self.emit_xlnT(quant=False)
        xv = self.xlnT[:].rearrange("p (k t) -> p k t", t=TLOC)

        qb_ready = kb_ready = vb_ready = None
        if q_bias is not None:
            qb_ready = self.emit_bias_col(q_bias, qcol, scale=0.125)
        if k_bias is not None:
            kb_ready = self.emit_bias_col(k_bias, kcol)
        if v_bias is not None:
            vb_ready = self.emit_bias_col(v_bias, vcol)
        if DEBUG_ATTN and q_bias is not None:
            dt_ = nc.dram_tensor(f"dbg_qcol_early_{nm}", [128, 8], F32,
                                 kind="ExternalOutput")
            P.dma("gpsimd", dt_.ap(), qcol[:], self.s_gp, waits=[qb_ready])
            dt2_ = nc.dram_tensor(f"dbg_r_postb_{nm}", [128, NCH * D], F32,
                                  kind="ExternalOutput")
            P.dma("gpsimd", dt2_.ap(), self.r_sb[:], self.s_gp,
                  waits=[qb_ready, vb_ready])

        if causal:
            P.dma("scalar", mask8[:], self.maskd.ap(), self.s_sc,
                  waits=[cur(self.s_gp)])
            mask_ready = cur(self.s_sc)

        if latent is not None:
            latT = vt[:, 0:2 * TLOC]
            lat_view = latT.rearrange("p (k t) -> p k t", t=TLOC)
            wkvd = warr[:, 0:2048]
            wku = warr[:, 2048:4096]
            wvu = warr[:, 4096:6144]
            kvd_ready = latent["prep_down"](wkvd)
            kup_ready = latent["prep_kup"](wku)
            vup_ready = latent["prep_vup"](wvu)
            with self.subphase():
                lat_ready = self.emit_proj_fm_view(
                    wkvd.rearrange("p (k f) -> p k f", f=256), 256,
                    latT, TLOC, kvd_ready, xv, KC, self.xlnT_ready)
            ksrc_view, ksrc_kc, ksrc_ready = lat_view, 2, lat_ready
            kw_view = wku.rearrange("p (k f) -> p k f", f=D)
            vw_view = wvu.rearrange("p (k f) -> p k f", f=D)
            kw_ready, vw_ready = kup_ready, vup_ready
        else:
            kw_ready = prep_k(warr)
            ksrc_view, ksrc_kc, ksrc_ready = xv, KC, self.xlnT_ready
            kw_view = warr[:, 0:KC * self.kv_outF].rearrange(
                "p (k f) -> p k f", f=self.kv_outF)

        koutF = 64 if shared_kv else D
        with self.subphase():
            kfm_ready = self.emit_proj_fm_view(
                kw_view, koutF, q_sb, TLOC, kw_ready, ksrc_view, ksrc_kc,
                ksrc_ready, biascol=kcol if k_bias is not None else None,
                bias_ready=kb_ready)
        if shared_kv:
            P.dma("scalar", kshard.ap(), q_sb[0:64, 0:TLOC], self.s_sc,
                  waits=[kfm_ready])
        else:
            P.dma("scalar", kshard.ap().rearrange("(hc p) t -> p hc t", p=128),
                  q_sb[:].rearrange("p (hc t) -> p hc t", t=TLOC), self.s_sc,
                  waits=[kfm_ready])
        kshard_dma = cur(self.s_sc)

        if latent is not None:
            vsrc_view, vsrc_kc, vsrc_ready = lat_view, 2, lat_ready
        else:
            vw_ready = prep_v(warr)
            vw_view = warr[:, 0:KC * self.kv_outF].rearrange(
                "p (k f) -> p k f", f=self.kv_outF)
            vsrc_view, vsrc_kc, vsrc_ready = xv, KC, self.xlnT_ready
        with self.subphase():
            vshard_dma, vtr_done = self.emit_v_pipeline(
                vw_view, koutF, vw_ready, vsrc_view, vsrc_kc, vsrc_ready,
                vfm, vstg, vshard.ap(), vcol if v_bias is not None else None,
                vb_ready, stride)

        P.do("gpsimd", lambda g: g.collective_compute(
            "AllGather", ALU.bypass, replica_groups=[list(range(8))],
            ins=[kshard.ap().opt()], outs=[kg.ap().opt()]),
            waits=[kshard_dma], incs=[(self.s_cc, 1)], chain=False)
        P.do("gpsimd", lambda g: g.collective_compute(
            "AllGather", ALU.bypass, replica_groups=[list(range(8))],
            ins=[vshard.ap().opt()], outs=[vg.ap().opt()]),
            waits=[vshard_dma], incs=[(self.s_cc, 1)], chain=False)
        gather_done = cur(self.s_cc)

        qw_ready = prep_q(warr, extra_waits=[kfm_ready, vtr_done, cur(self.s_v)])
        with self.subphase():
            q_ready = self.emit_proj_fm_view(
                warr[:].rearrange("p (k f) -> p k f", f=D), D, q_sb, TLOC,
                qw_ready, xv, KC, self.xlnT_ready, scale=0.125,
                biascol=qcol if q_bias is not None else None, bias_ready=qb_ready,
                evac_extra=[kshard_dma])
        qv = q_sb[:].rearrange("p (oc t) -> p oc t", t=TLOC)
        if DEBUG_ATTN:
            wdb = [q_ready, cur(self.s_t)]
            for nmb, buf in (("q0", q_sb), ("wq", warr), ("xlt", self.xlnT),
                             ("qcol", qcol), ("resid", self.r_sb)):
                dt_ = nc.dram_tensor(f"dbg_{nmb}_{nm}", list(buf.shape),
                                     buf[:].dtype, kind="ExternalOutput")
                P.dma("gpsimd", dt_.ap(), buf[:], self.s_gp, waits=wdb)
            P.do("gpsimd", lambda g: g.nop(), waits=[cur(self.s_gp)])

        ow_ready = prep_o(warr, extra_waits=[q_ready])
        if o_bias is not None:
            self.biasrow = self.sb([1, D], BF16, "biasrow")
            P.dma("gpsimd", self.biasrow[:], o_bias, self.s_gp)
            ob_ready = cur(self.s_gp)
        else:
            ob_ready = None

        with self.subphase():
            ps_s = [self.ps([128, 1024], F32, "pss"), self.ps([128, 1024], F32, "pss")]
            ps_o = [self.ps([65, 512], F32, "pso"), self.ps([65, 512], F32, "pso")]
            recB = self.ps([128, 512], F32, "prb")
            ktv = k_tile
            vtv = vt[:].rearrange("p (mc c) -> p mc c", c=stride)
            exp_done = [None, None]
            av_done = [None, None]
            opsum_free = None
            kv_free = None
            for b in range(2):
                for pair in range(8):
                    lw = [gather_done]
                    if kv_free is not None:
                        lw.append(kv_free)
                    else:
                        lw += [vshard_dma]
                    xw = NCH // 4 * stride
                    if shared_kv:
                        if pair == 0:
                            src = kg.ap().rearrange(
                                "(r d) (l o) -> d l r o", d=64, o=512)
                            vsrc = vg.ap().rearrange(
                                "(r p) (l x) -> p l r x", p=128, x=xw)
                            for kk in range(2):
                                P.dma("sync", ktv[0:64, :].rearrange(
                                    "d (kk r o) -> d kk r o", kk=2, r=8)[:, kk],
                                    src[:, 2 * b + kk, :, :], self.s_ld,
                                    waits=lw if kk == 0 else ())
                                P.dma("sync", ktv[64:128, :].rearrange(
                                    "d (kk r o) -> d kk r o", kk=2, r=8)[:, kk],
                                    src[:, 2 * b + kk, :, :], self.s_ld)
                                P.dma("sync", vt[:].rearrange(
                                    "p (kk r x) -> p kk r x", kk=2, r=8)[:, kk],
                                    vsrc[:, 2 * b + kk, :, :], self.s_ld)
                            kv_loaded = cur(self.s_ld)
                    else:
                        src = kg.ap().rearrange(
                            "(r hc p) (l o) -> p hc l r o", r=8, p=128, o=512)
                        vsrc = vg.ap().rearrange(
                            "(r p) (pr l x) -> p pr l r x", p=128, pr=8, x=xw)
                        for kk in range(2):
                            P.dma("sync", ktv[:].rearrange(
                                "p (kk r o) -> p kk r o", kk=2, r=8)[:, kk],
                                src[:, pair, 2 * b + kk, :, :], self.s_ld,
                                waits=lw if kk == 0 else ())
                            P.dma("sync", vt[:].rearrange(
                                "p (kk r x) -> p kk r x", kk=2, r=8)[:, kk],
                                vsrc[:, pair, 2 * b + kk, :, :], self.s_ld)
                        kv_loaded = cur(self.s_ld)
                    for l in range(2):
                        M = (32 * (l + 1)) if causal else 64
                        qoff = b * 1024 + l * 512
                        for j in range(M + 1):
                            par = j % 2
                            if j < M:
                                wa = [kv_loaded, q_ready]
                                if b == 0 and pair == 0 and l == 0 and j == 0:
                                    wa.append(ow_ready)
                                    wa += self.psum_guard
                                if exp_done[par] is not None:
                                    wa.append(exp_done[par])
                                for hh in range(2):
                                    P.do("tensor",
                                         lambda t, j=j, hh=hh, pair=pair, par=par, qoff=qoff:
                                         t.matmul(
                                             ps_s[par][:, hh * 512:(hh + 1) * 512],
                                             ktv[hh * 64:hh * 64 + 64,
                                                 j * 128:(j + 1) * 128],
                                             qv[hh * 64:hh * 64 + 64, pair,
                                                qoff:qoff + 512],
                                             start=True, stop=True),
                                         waits=wa if hh == 0 else [],
                                         incs=[(self.s_t, 1)] if hh == 1 else [])
                                sc_done = cur(self.s_t)
                                we = [sc_done]
                                if av_done[par] is not None:
                                    we.append(av_done[par])
                                P.do("scalar", lambda s, par=par: s.activation(
                                    out=pst[:, par * 1024:(par + 1) * 1024],
                                    in_=ps_s[par][:], func=AF.Exp), waits=we)
                                exp_done[par] = cur(self.s_s)
                                masked = causal and j >= M - 32
                                if masked:
                                    o = j - (M - 32)
                                    pvw = pst[:, par * 1024:(par + 1) * 1024]\
                                        .rearrange("p (hh q) -> p hh q", q=512)
                                    mkv = mask8[:, o * 512:(o + 1) * 512]\
                                        .rearrange("p (one q) -> p one q", one=1)\
                                        .broadcast_to((128, 2, 512))
                                    P.do("vector", lambda v, pvw=pvw, mkv=mkv:
                                         v.tensor_tensor(out=pvw, in0=pvw, in1=mkv,
                                                         op=ALU.mult),
                                         waits=[exp_done[par], mask_ready])
                                    pmask_done = cur(self.s_v)
                                else:
                                    pmask_done = None
                            if j >= 1:
                                jj = j - 1
                                pj_ = jj % 2
                                wa2 = [exp_done[pj_], kv_loaded]
                                if causal and jj >= M - 32:
                                    wa2.append(cur(self.s_v))
                                if jj == 0 and opsum_free is not None:
                                    wa2.append(opsum_free)
                                for hh in range(2):
                                    voff = jj * stride + \
                                        (hh * 65 if not shared_kv else 0)
                                    P.do("tensor",
                                         lambda t, jj=jj, hh=hh, pj_=pj_, voff=voff:
                                         t.matmul(
                                             ps_o[hh][:],
                                             vt[:, voff:voff + 65],
                                             pst[:, pj_ * 1024 + hh * 512:
                                                 pj_ * 1024 + (hh + 1) * 512],
                                             start=(jj == 0), stop=(jj == M - 1)),
                                         waits=wa2 if hh == 0 else [],
                                         incs=[(self.s_t, 1)] if hh == 1 else [])
                                av_done[pj_] = cur(self.s_t)
                        avf = cur(self.s_t)
                        for hh in range(2):
                            P.do("vector", lambda v, hh=hh: v.reciprocal(
                                out=rec_sb[0:1, :], in_=ps_o[hh][64:65, :]),
                                waits=[avf] if hh == 0 else [cur(self.s_t)])
                            rd = cur(self.s_v)
                            P.do("tensor", lambda t: t.matmul(
                                recB[:], self.ones1b[:], rec_sb[0:1, :],
                                start=True, stop=True),
                                waits=[rd], incs=[(self.s_t, 1)])
                            bc = cur(self.s_t)
                            P.do("vector", lambda v: v.tensor_copy(
                                out=rb_sb[:], in_=recB[:]), waits=[bc])
                            oco = b * 8192 + pair * 1024 + l * 512
                            P.do("vector", lambda v, hh=hh, oco=oco:
                                 v.tensor_tensor(
                                     out=self.xlnT[hh * 64:(hh + 1) * 64,
                                                   oco:oco + 512],
                                     in0=ps_o[hh][0:64, :], in1=rb_sb[0:64, :],
                                     op=ALU.mult))
                        opsum_free = cur(self.s_v)
                    kv_free = cur(self.s_t)
            ocat_ready = cur(self.s_v)
        if DEBUG_ATTN:
            wdb = [cur(self.s_v), cur(self.s_t), cur(self.s_s)]
            for nmb, buf in (("ocat", self.xlnT), ("qsb", q_sb), ("ktl", k_tile),
                             ("vtl", vt), ("pstd", pst)):
                dt_ = nc.dram_tensor(f"dbg_{nmb}_{nm}", list(buf.shape),
                                     buf[:].dtype, kind="ExternalOutput")
                P.dma("gpsimd", dt_.ap(), buf[:], self.s_gp, waits=wdb)
            if causal:
                dtm = nc.dram_tensor(f"dbg_mask_{nm}", list(mask8.shape),
                                     mybir.dt.float8e4, kind="ExternalOutput")
                P.dma("gpsimd", dtm.ap(), mask8[:], self.s_gp, waits=wdb)
            P.do("gpsimd", lambda g: g.nop(), waits=[cur(self.s_gp)])
        for b in range(2):
            with self.subphase():
                self.emit_tokproj_add(
                    None, 1024, None, D, KC,
                    [(b * 8 + tc, tc * 128) for tc in range(8)],
                    ocat_ready, ow_ready, bias_ready=ob_ready,
                    lhs_view=self.xlnT[:, b * 8192:(b + 1) * 8192]
                        .rearrange("p (k t) -> p k t", t=1024),
                    rhs_view=warr[:].rearrange("p (k f) -> p k f", f=D))

    def emit_proj_fm_view(self, wv, outF, out_sb, out_cols, wready, src_view,
                          src_kc, src_ready, scale=1.0, biascol=None,
                          bias_ready=None, evac_extra=()):
        P = self.P
        fw = min(outF, 128)
        fcn = (outF + 127) // 128
        pj = [self.ps([128, 512], F32, "pj"), self.ps([128, 512], F32, "pj")]
        ev = [None, None]
        idx = 0
        for f in range(fcn):
            for tt in range(0, TLOC, 512):
                par = idx % 2
                idx += 1
                wa = [wready, src_ready] + (self.psum_guard if idx <= 2 else [])
                if ev[par] is not None:
                    wa.append(ev[par])
                for k in range(src_kc):
                    P.do("tensor", lambda t, k=k, f=f, tt=tt, par=par: t.matmul(
                        pj[par][0:fw, :], wv[:, k, f * fw:f * fw + fw],
                        src_view[:, k, tt:tt + 512],
                        start=(k == 0), stop=(k == src_kc - 1)),
                        waits=wa if k == 0 else [],
                        incs=[(self.s_t, 1)] if k == src_kc - 1 else [])
                mmd = cur(self.s_t)
                o = out_sb[0:fw, f * out_cols + tt:f * out_cols + tt + 512]
                wv2 = [mmd] + list(evac_extra)
                if biascol is not None:
                    if bias_ready is not None:
                        wv2.append(bias_ready)
                    P.do("vector", lambda v, o=o, par=par, f=f: v.tensor_scalar(
                        out=o, in0=pj[par][0:fw, :], scalar1=scale,
                        scalar2=biascol[0:fw, f:f + 1], op0=ALU.mult, op1=ALU.add),
                        waits=wv2)
                elif scale != 1.0:
                    P.do("vector", lambda v, o=o, par=par: v.tensor_scalar_mul(
                        o, pj[par][0:fw, :], scale), waits=wv2)
                else:
                    P.do("vector", lambda v, o=o, par=par: v.tensor_copy(
                        out=o, in_=pj[par][0:fw, :]), waits=wv2)
                ev[par] = cur(self.s_v)
        return cur(self.s_v)


    def sub_l0a(self):
        self.kv_outF = 64
        w = self.w

        def prep_q(warr, extra_waits=()):
            ws = self.sb([128, 2 * D], BF16, "wsq") if False else None
            with self.subphase():
                wst = self.sb([128, D], BF16, "wst")
                pstA = self.ps([128, KC * 128], BF16, "pt")
                pstB = self.ps([128, KC * 128], BF16, "pt")
                return self.prep_transposed(
                    warr, w["L0_attn_a_query_proj"].rearrange("h k d -> (h k) d"),
                    D, D, wst, [pstA, pstB], extra_waits)

        def prep_k(warr):
            return self.prep_direct_into(warr[:, 0:KC * 64],
                                         w["L0_attn_a_key_proj"], D, 64)

        def prep_v(warr):
            return self.prep_direct_into(warr[:, 0:KC * 64],
                                         w["L0_attn_a_value_proj"], D, 64,
                                         waits=[cur(self.s_t)])

        def prep_o(warr, extra_waits=()):
            with self.subphase():
                wst = self.sb([128, D], BF16, "wst")
                pstA = self.ps([128, KC * 128], BF16, "pt")
                pstB = self.ps([128, KC * 128], BF16, "pt")
                return self.prep_transposed(
                    warr, w["L0_attn_a_output_proj"].rearrange("d h v -> d (h v)"),
                    D, D, wst, [pstA, pstB], extra_waits)

        self.emit_attention("l0a", causal=False, shared_kv=True, kdim_rows=64,
                            prep_q=prep_q, prep_k=prep_k, prep_v=prep_v,
                            prep_o=prep_o)

    def sub_l0b(self):
        self.kv_outF = D
        w = self.w

        def mk_prep(key, extra=False):
            def prep(warr, extra_waits=()):
                with self.subphase():
                    wst = self.sb([128, D], BF16, "wst")
                    pstA = self.ps([128, KC * 128], BF16, "pt")
                    pstB = self.ps([128, KC * 128], BF16, "pt")
                    src = w[key]
                    if key == "L0_attn_b_w_o":
                        src2 = src
                    else:
                        src2 = src.rearrange("h k d -> (h k) d")
                    return self.prep_transposed(warr, src2, D, D, wst,
                                                [pstA, pstB], extra_waits)
            return prep

        def prep_v(warr):
            return mk_prep("L0_attn_b_w_v")(warr, extra_waits=[cur(self.s_t)])

        self.emit_attention("l0b", causal=True, shared_kv=False, kdim_rows=D,
                            prep_q=mk_prep("L0_attn_b_w_q"),
                            prep_k=mk_prep("L0_attn_b_w_k"),
                            prep_v=prep_v,
                            prep_o=mk_prep("L0_attn_b_w_o"))

    def sub_l1a(self):
        self.kv_outF = D
        w = self.w

        def prep_down(wkvd):
            return self.prep_direct_into(wkvd, w["L1_attn_a_Wkv_down"], D, 256)

        def prep_kup(wku):
            return self.prep_direct_into(
                wku, w["L1_attn_a_Wk_up"].rearrange("c h k -> c (h k)"), 256, D)

        def prep_vup(wvu):
            return self.prep_direct_into(
                wvu, w["L1_attn_a_Wv_up"].rearrange("c h k -> c (h k)"), 256, D)

        def prep_q(warr, extra_waits=()):
            return self.prep_direct_into(
                warr, w["L1_attn_a_Wq"].rearrange("d h k -> d (h k)"), D, D,
                waits=list(extra_waits))

        def prep_o(warr, extra_waits=()):
            return self.prep_direct_into(
                warr, w["L1_attn_a_Wout"].rearrange("h v d -> (h v) d"), D, D,
                waits=list(extra_waits))

        self.emit_attention("l1a", causal=True, shared_kv=False, kdim_rows=D,
                            prep_q=prep_q, prep_k=None, prep_v=None,
                            prep_o=prep_o,
                            latent=dict(prep_down=prep_down, prep_kup=prep_kup,
                                        prep_vup=prep_vup))

    def sub_l1b(self):
        self.kv_outF = D
        w = self.w

        def mk_prep(key):
            def prep(warr, extra_waits=()):
                sems = list(extra_waits)
                for h in range(NH):
                    self.P.dma(
                        "gpsimd",
                        warr[:].rearrange("p (k f) -> p k f", f=D)
                        [:, :, h * HD:(h + 1) * HD],
                        w[key].rearrange("h (k p) f -> h p k f", p=128)[h],
                        self.s_gp, waits=sems if h == 0 else ())
                return cur(self.s_gp)
            return prep

        def prep_v(warr):
            return mk_prep("L1_attn_b_V_w")(warr, extra_waits=[cur(self.s_t)])

        def prep_o(warr, extra_waits=()):
            return self.prep_direct_into(
                warr, w["L1_attn_b_O_w"].rearrange("h v d -> (h v) d"), D, D,
                waits=list(extra_waits))

        self.emit_attention(
            "l1b", causal=True, shared_kv=False, kdim_rows=D,
            prep_q=mk_prep("L1_attn_b_Q_w"),
            prep_k=mk_prep("L1_attn_b_K_w"),
            prep_v=prep_v,
            prep_o=prep_o,
            q_bias=w["L1_attn_b_Q_b"].rearrange("(a hh) k -> a (hh k)", a=8),
            k_bias=w["L1_attn_b_K_b"].rearrange("(a hh) k -> a (hh k)", a=8),
            v_bias=w["L1_attn_b_V_b"].rearrange("(a hh) k -> a (hh k)", a=8),
            o_bias=w["L1_attn_b_O_b"].rearrange("(one d) -> one d", one=1))

    def prep_direct_into(self, dst, src_hbm, kdim, fdim, waits=()):
        kcn = kdim // 128
        self.P.dma("gpsimd",
                   dst[:, 0:kcn * fdim].rearrange("p (k f) -> p k f", f=fdim),
                   src_hbm.rearrange("(k p) f -> p k f", p=128), self.s_gp,
                   waits=list(waits))
        return cur(self.s_gp)

    def emit_glu(self, wg, wu, wready, act_out, dcn, fdim, src_view, src_kc,
                 src_ready, coefT=None, e=None, esel=None, coef_ready=None,
                 gs=None, act_war=()):
        P = self.P
        with self.subphase():
            psA = self.ps([128, 512], F32, "pga")
            psB = self.ps([128, 512], F32, "pgb")
            pcb = self.ps([128, 512], F32, "pcb") if coefT is not None else None
            wgv = wg[:, 0:src_kc * fdim].rearrange("p (k f) -> p k f", f=fdim)
            wuv = wu[:, 0:src_kc * fdim].rearrange("p (k f) -> p k f", f=fdim)
            silu_prev = None
            ttA_prev = None
            ttB_prev = None
            scale_prev = None
            first = True
            for tt in range(4):
                if coefT is not None:
                    wv = [coef_ready]
                    if scale_prev is not None:
                        wv.append(scale_prev)
                    if first:
                        wv += self.psum_guard
                    P.do("tensor", lambda t, tt=tt: t.matmul(
                        pcb[:], esel[:, e * 128:(e + 1) * 128],
                        coefT[:, tt * 512:(tt + 1) * 512],
                        start=True, stop=True),
                        waits=wv, incs=[(self.s_t, 1)])
                    pcb_mmd = cur(self.s_t)
                for dc in range(dcn):
                    wa = [wready, src_ready]
                    if silu_prev is not None:
                        wa.append(silu_prev)
                    if first:
                        wa += self.psum_guard
                        first = False
                    for k in range(src_kc):
                        P.do("tensor", lambda t, k=k, dc=dc, tt=tt: t.matmul(
                            psA[:], wgv[:, k, dc * 128:(dc + 1) * 128],
                            src_view[:, k, tt * 512:(tt + 1) * 512],
                            start=(k == 0), stop=(k == src_kc - 1)),
                            waits=wa if k == 0 else [],
                            incs=[(self.s_t, 1)] if k == src_kc - 1 else [])
                    gmmd = cur(self.s_t)
                    P.do("scalar", lambda s: s.activation(
                        out=gs[:], in_=psA[:], func=AF.Silu),
                        waits=[gmmd] + ([ttA_prev] if ttA_prev else []))
                    silu_prev = cur(self.s_s)
                    wb = [ttB_prev] if ttB_prev is not None else []
                    for k in range(src_kc):
                        P.do("tensor", lambda t, k=k, dc=dc, tt=tt: t.matmul(
                            psB[:], wuv[:, k, dc * 128:(dc + 1) * 128],
                            src_view[:, k, tt * 512:(tt + 1) * 512],
                            start=(k == 0), stop=(k == src_kc - 1)),
                            waits=wb if k == 0 else [],
                            incs=[(self.s_t, 1)] if k == src_kc - 1 else [])
                    ummd = cur(self.s_t)
                    o = act_out[:, dc * TLOC + tt * 512:dc * TLOC + tt * 512 + 512]
                    P.do("vector", lambda v, o=o: v.tensor_tensor(
                        out=o, in0=gs[:], in1=psB[:], op=ALU.mult),
                        waits=[ummd, silu_prev] + list(act_war))
                    act_war = ()
                    ttA_prev = cur(self.s_v)
                    ttB_prev = cur(self.s_v)
                    if coefT is not None:
                        P.do("vector", lambda v, o=o: v.tensor_tensor(
                            out=o, in0=o, in1=pcb[:], op=ALU.mult),
                            waits=[pcb_mmd])
                        scale_prev = cur(self.s_v)
                        ttB_prev = scale_prev
        return cur(self.s_v)

    def sub_l1moe(self):
        P = self.P
        w = self.w
        rt32 = self.sb([128, D], F32, "rt")
        c32T = self.sb([128, D], F32, "rtT")
        rsb = self.sb([128, KC * 8], F32, "rw")
        coefT = self.sb([8, TLOC], BF16, "coefT")
        esel = self.sb([8, 1024], BF16, "esel")
        gs = self.sb([128, 512], BF16, "gs")
        act = self.sb([128, 8 * TLOC], BF16, "act")
        wbuf = self.sb([128, 24576], BF16, "wb")
        pr1 = self.sb([128, 8], F32, "pr1")
        pr2 = self.sb([128, 8], F32, "pr2")
        pr3 = self.sb([128, 8], F32, "pr3")
        st1 = self.sb([128, 1], F32, "st1")

        P.do("gpsimd", lambda g: g.memset(esel[:], 0.0))
        P.do("gpsimd", lambda g: g.affine_select(
            out=esel[:].rearrange("j (e p) -> j e p", p=128),
            in_=esel[:].rearrange("j (e p) -> j e p", p=128),
            compare_op=ALU.not_equal, fill=1.0,
            base=0, pattern=[[1, 8], [0, 128]], channel_multiplier=-1),
            waits=[cur(self.s_g)])
        esel_ready = cur(self.s_g)
        P.dma("sync", rsb[:].rearrange("p (k f) -> p k f", f=8),
              w["L1_ffn_router"].rearrange("(k p) f -> p k f", p=128), self.s_ld)
        rw_ready = cur(self.s_ld)

        self.emit_ln_stats()
        with self.subphase():
            self.emit_xlnT(quant=False)
        xv = self.xlnT[:].rearrange("p (k t) -> p k t", t=TLOC)

        with self.subphase():
            pt32 = self.ps([128, D], F32, "pt32")
            plg = self.ps([128, 8], F32, "plg")
            ptc = self.ps([8, 128], F32, "ptc")
            prev_tr = None
            prev_lg = None
            prev_ctc = None
            for i in range(NCH):
                self.emit_ln_chunk(i, rt32[:],
                                   waits=([prev_tr] if prev_tr else
                                          [self.ln_ready]))
                lnr = cur(self.s_v)
                for kc in range(KC):
                    P.do("tensor", lambda t, kc=kc: t.transpose(
                        pt32[:, kc * 128:(kc + 1) * 128],
                        rt32[:, kc * 128:(kc + 1) * 128], self.identf[:]),
                        waits=([lnr, self.ident_ready, cur(self.s_v)] +
                               self.psum_guard) if kc == 0 else [],
                        incs=[(self.s_t, 1)] if kc == KC - 1 else [])
                prev_tr = cur(self.s_t)
                P.do("vector", lambda v: v.tensor_copy(out=c32T[:], in_=pt32[:]),
                     waits=[prev_tr])
                evd = cur(self.s_v)
                wlg = [evd, rw_ready] + ([prev_lg] if prev_lg else [])
                for kc in range(KC):
                    P.do("tensor", lambda t, kc=kc: t.matmul(
                        plg[:], c32T[:, kc * 128:(kc + 1) * 128],
                        rsb[:, kc * 8:(kc + 1) * 8],
                        start=(kc == 0), stop=(kc == KC - 1)),
                        waits=wlg if kc == 0 else [],
                        incs=[(self.s_t, 1)] if kc == KC - 1 else [])
                mmd = cur(self.s_t)
                P.do("vector", lambda v: v.reduce_max(
                    out=st1[:], in_=plg[:], axis=X, negate=True), waits=[mmd])
                P.do("scalar", lambda s: s.activation(
                    out=pr1[:], in_=plg[:], func=AF.Exp, bias=st1[:, 0:1],
                    accum_out=pr3[:, 0:1]), waits=[cur(self.s_v)])
                prev_lg = cur(self.s_s)
                P.do("vector", lambda v: v.reciprocal(
                    out=st1[:], in_=pr3[:, 0:1]), waits=[prev_lg])
                P.do("vector", lambda v: v.tensor_scalar_mul(
                    pr1[:], pr1[:], st1[:, 0:1]))
                P.do("vector", lambda v: v.reduce_max(out=st1[:], in_=pr1[:], axis=X))
                P.do("vector", lambda v: v.tensor_single_scalar(
                    out=pr2[:], in_=pr1[:], scalar=st1[:, 0:1], op=ALU.is_ge))
                P.do("vector", lambda v: v.tensor_tensor(
                    out=pr3[:], in0=pr1[:], in1=pr2[:], op=ALU.mult))
                P.do("vector", lambda v: v.tensor_tensor(
                    out=pr3[:], in0=pr1[:], in1=pr3[:], op=ALU.subtract))
                P.do("vector", lambda v: v.reduce_max(out=st1[:], in_=pr3[:], axis=X))
                P.do("vector", lambda v: v.tensor_single_scalar(
                    out=pr3[:], in_=pr3[:], scalar=st1[:, 0:1], op=ALU.is_ge))
                P.do("vector", lambda v: v.tensor_tensor(
                    out=pr2[:], in0=pr2[:], in1=pr3[:], op=ALU.add))
                P.do("vector", lambda v: v.tensor_tensor(
                    out=pr1[:], in0=pr1[:], in1=pr2[:], op=ALU.mult))
                P.do("vector", lambda v: v.tensor_copy(
                    out=rt32[:, 0:8], in_=pr1[:]),
                    waits=[prev_ctc] if prev_ctc else [])
                cpd = cur(self.s_v)
                P.do("tensor", lambda t: t.transpose(
                    ptc[:], rt32[:, 0:8], self.identf[:]),
                    waits=[cpd] + ([prev_ctc] if prev_ctc else []),
                    incs=[(self.s_t, 1)])
                trc = cur(self.s_t)
                P.do("vector", lambda v, i=i: v.tensor_copy(
                    out=coefT[:, i * 128:(i + 1) * 128], in_=ptc[:]),
                    waits=[trc])
                prev_ctc = cur(self.s_v)
                prev_tr = trc
        coef_ready = cur(self.s_v)

        wsg = wbuf[:, 0:8192]
        wsu = wbuf[:, 8192:16384]
        wsd = wbuf[:, 16384:24576]
        self.prep_direct_into(wsg, w["L1_ffn_shared_gate"], D, D)
        g2 = self.prep_direct_into(wsu, w["L1_ffn_shared_up"], D, D)
        g3 = self.prep_direct_into(wsd, w["L1_ffn_shared_down"], D, D)
        act_ready = self.emit_glu(wsg, wsu, g2, act, 8, D, xv, KC,
                                  self.xlnT_ready, gs=gs)
        with self.subphase():
            self.emit_tokproj_add(act, TLOC, None, D, KC,
                                  [(i, i * 128) for i in range(NCH)],
                                  act_ready, g3,
                                  rhs_view=wsd.rearrange("p (k f) -> p k f", f=D))
        shared_t = cur(self.s_t)
        shared_v = cur(self.s_v)

        we_done = [(shared_t, shared_v), (shared_t, shared_v)]
        wready = [None, None]
        prev_down_t = shared_t
        for e in range(8):
            par = e % 2
            we = wbuf[:, par * 12288:(par + 1) * 12288]
            lw = list(we_done[par])
            self.P.dma("gpsimd",
                       we[:, 0:4096].rearrange("p (k f) -> p k f", f=512),
                       w["L1_ffn_experts_gate"][e].rearrange(
                           "(k p) f -> p k f", p=128), self.s_gp, waits=lw)
            self.P.dma("gpsimd",
                       we[:, 4096:8192].rearrange("p (k f) -> p k f", f=512),
                       w["L1_ffn_experts_up"][e].rearrange(
                           "(k p) f -> p k f", p=128), self.s_gp)
            self.P.dma("gpsimd",
                       we[:, 8192:12288].rearrange("p (k f) -> p k f", f=1024),
                       w["L1_ffn_experts_down"][e].rearrange(
                           "(k p) f -> p k f", p=128), self.s_gp)
            wready[par] = cur(self.s_gp)
            act_ready = self.emit_glu(
                we[:, 0:4096], we[:, 4096:8192], wready[par], act, 4, 512,
                xv, KC, self.xlnT_ready, coefT=coefT, e=e, esel=esel,
                coef_ready=coef_ready, gs=gs, act_war=[prev_down_t])
            with self.subphase():
                self.emit_tokproj_add(
                    act, TLOC, None, D, 4,
                    [(i, i * 128) for i in range(NCH)],
                    act_ready, wready[par],
                    rhs_view=we[:, 8192:12288].rearrange(
                        "p (k f) -> p k f", f=D))
            we_done[par] = (cur(self.s_t), cur(self.s_v))
            prev_down_t = cur(self.s_t)



def shard_x(x):
    xs = x.reshape(2, 16, 512, D)
    out = []
    for c in range(8):
        out.append(np.ascontiguousarray(np.concatenate(
            [xs[0, c], xs[0, 8 + c], xs[1, c], xs[1, 8 + c]], axis=0)))
    return out


def unshard_y(shards):
    y = np.empty((2, 16, 512, D), dtype=shards[0].dtype)
    for c in range(8):
        s = shards[c].reshape(4, 512, D)
        y[0, c] = s[0]
        y[0, 8 + c] = s[1]
        y[1, c] = s[2]
        y[1, 8 + c] = s[3]
    return y.reshape(2, 8192, D)


def build_nc(subs=tuple(range(8))):
    return Build(subs).build()


def make_in_maps(inputs):
    xsh = shard_x(np.ascontiguousarray(inputs["L0_attn_a_x"], dtype=np.float32))
    in_maps = []
    for c in range(8):
        m = {"xs": xsh[c],
             "cconst": np.array([[512.0 * c, float(c), 1e-5, 0, 0, 0, 0, 0]],
                                dtype=np.float32)}
        for k, v in inputs.items():
            if k == "L0_attn_a_x":
                continue
            m[k] = np.ascontiguousarray(v, dtype=np.float32)
        in_maps.append(m)
    return in_maps


def kernel(**inputs):
    nc = build_nc(tuple(range(8)))
    in_maps = make_in_maps(inputs)
    res = run_bass_kernel_spmd(nc, in_maps, core_ids=list(range(8)))
    return unshard_y([r["ys"] for r in res.results]).astype(np.float32)


# version 07
# speedup vs baseline: 1.3878x; 1.3878x over previous
import numpy as np
from contextlib import ExitStack, contextmanager

import concourse.bass as bass
import concourse.mybir as mybir
from concourse.bass_utils import run_bass_kernel_spmd

F32 = mybir.dt.float32
BF16 = mybir.dt.bfloat16
I16 = mybir.dt.int16
AF = mybir.ActivationFunctionType
ALU = mybir.AluOpType

D = 1024
KC = 8
TLOC = 2048
NCH = 16
NH = 16
HD = 64
QB = 512
MAGIC = 12582912.0
X = mybir.AxisListType.X
ENGINES = ("sync", "scalar", "vector", "tensor", "gpsimd")
DEBUG_ATTN = False
DEBUG_FFN = False
DEBUG_BLOCK = False


class Sem:
    __slots__ = ("h", "n")

    def __init__(self, h):
        self.h = h
        self.n = 0


CHAIN_DEFAULT = {"vector": True, "scalar": True, "gpsimd": True,
                 "sync": False, "tensor": False}


class Prog:
    def __init__(self, nc, ctx):
        self.nc = nc
        self.ctx = ctx
        self.q = {e: [] for e in ENGINES}
        self._sid = 0
        self._chain = {}

    def sem(self, name=None):
        self._sid += 1
        h = self.ctx.enter_context(self.nc.semaphore(name or f"sem{self._sid}"))
        return Sem(h)

    def chain_sem(self, eng):
        if eng not in self._chain:
            self._chain[eng] = self.sem(f"ch_{eng}")
        return self._chain[eng]

    def do(self, eng, fn, waits=(), incs=(), chain=None):
        if chain is None:
            chain = CHAIN_DEFAULT[eng]
        waits = list(waits)
        incs = list(incs)
        if chain:
            cs = self.chain_sem(eng)
            waits.append((cs, cs.n))
            incs.append((cs, 1))
        wl = [(s.h, int(v)) for (s, v) in waits]
        il = []
        for (s, a) in incs:
            s.n += a
            il.append((s.h, a))
        self.q[eng].append((fn, wl, il))

    def dma(self, eng, out, in_, sem, waits=()):
        self.do(eng, (lambda e, o=out, i=in_: e.dma_start(out=o, in_=i)),
                waits=list(waits) + [(sem, sem.n)], incs=[(sem, 16)], chain=False)

    def finalize(self):
        with self.nc.allow_low_precision(reason="bf16 elementwise by design"), \
                self.nc.Block() as block:
            for eng in ENGINES:
                ops = self.q[eng]
                if ops:
                    getattr(block, eng)(lambda e, ops=ops: Prog._run(e, ops))

    @staticmethod
    def _run(e, ops):
        for fn, wl, il in ops:
            byh = {}
            for h, v in wl:
                if byh.get(id(h), (None, -1))[1] < v:
                    byh[id(h)] = (h, v)
            for h, v in byh.values():
                e.wait_ge(h, v)
            inst = fn(e)
            for h, a in il:
                inst = inst.then_inc(h, a, skip_validation=(a == 16))


def cur(sem):
    return (sem, sem.n)



class Build:
    def __init__(self, subs):
        self.subs = list(subs)
        self.nc = bass.Bass("TRN2", target_bir_lowering=False)
        self.ctx = ExitStack()
        self.P = Prog(self.nc, self.ctx)
        self.phase = None
        self.psum_guard = []
        self.uid = 0

    def name(self, pfx):
        self.uid += 1
        return f"{pfx}{self.uid}"

    def _stack(self):
        return self.phase if self.phase is not None else self.ctx

    def sb(self, shape, dtype, pfx="t"):
        return self._stack().enter_context(
            self.nc.sbuf_tensor(self.name(pfx), shape, dtype))

    def ps(self, shape, dtype=F32, pfx="p"):
        return self._stack().enter_context(
            self.nc.psum_tensor(self.name(pfx), shape, dtype))


    def build(self):
        nc = self.nc
        self.xs = nc.dram_tensor("xs", [TLOC, D], F32, kind="ExternalInput").ap()
        self.ys = nc.dram_tensor("ys", [TLOC, D], F32, kind="ExternalOutput").ap()
        self.cconst = nc.dram_tensor("cconst", [1, 8], F32, kind="ExternalInput").ap()
        wnames = [
            ("L0_attn_a_query_proj", [16, 64, 1024]),
            ("L0_attn_a_key_proj", [1024, 64]),
            ("L0_attn_a_value_proj", [1024, 64]),
            ("L0_attn_a_output_proj", [1024, 16, 64]),
            ("L0_attn_b_w_q", [16, 64, 1024]),
            ("L0_attn_b_w_k", [16, 64, 1024]),
            ("L0_attn_b_w_v", [16, 64, 1024]),
            ("L0_attn_b_w_o", [1024, 1024]),
            ("L0_ffn_weight", [1024, 1024]),
            ("L0_ffn_bias", [1024]),
            ("L1_attn_a_Wq", [1024, 16, 64]),
            ("L1_attn_a_Wkv_down", [1024, 256]),
            ("L1_attn_a_Wk_up", [256, 16, 64]),
            ("L1_attn_a_Wv_up", [256, 16, 64]),
            ("L1_attn_a_Wout", [16, 64, 1024]),
            ("L1_attn_b_Q_w", [16, 1024, 64]),
            ("L1_attn_b_Q_b", [16, 64]),
            ("L1_attn_b_K_w", [16, 1024, 64]),
            ("L1_attn_b_K_b", [16, 64]),
            ("L1_attn_b_V_w", [16, 1024, 64]),
            ("L1_attn_b_V_b", [16, 64]),
            ("L1_attn_b_O_w", [16, 64, 1024]),
            ("L1_attn_b_O_b", [1024]),
            ("L1_ffn_router", [1024, 8]),
            ("L1_ffn_experts_gate", [8, 1024, 512]),
            ("L1_ffn_experts_up", [8, 1024, 512]),
            ("L1_ffn_experts_down", [8, 512, 1024]),
            ("L1_ffn_shared_gate", [1024, 1024]),
            ("L1_ffn_shared_up", [1024, 1024]),
            ("L1_ffn_shared_down", [1024, 1024]),
        ]
        self.w = {}
        for n, shp in wnames:
            self.w[n] = nc.dram_tensor(n, shp, F32, kind="ExternalInput").ap()

        self.maskd = nc.dram_tensor("maskd", [128, 32 * QB], mybir.dt.float8e4)

        P = self.P
        self.s_ld = P.sem("s_ld")
        self.s_sc = P.sem("s_sc")
        self.s_gp = P.sem("s_gp")
        self.s_t = P.sem("s_t")
        self.s_v = P.chain_sem("vector")
        self.s_s = P.chain_sem("scalar")
        self.s_g = P.chain_sem("gpsimd")
        self.s_cc = P.sem("s_cc")

        self.r_sb = self.sb([128, NCH * D], F32, "r")
        self.xlnT = self.sb([128, KC * TLOC], BF16, "xlnT")
        self.identb = self.sb([128, 128], BF16, "identb")
        self.identf = self.sb([128, 128], F32, "identf")
        self.ones1b = self.sb([1, 128], BF16, "ones1b")
        self.ccs = self.sb([128, 8], F32, "ccs")
        self.stat_sum = self.sb([128, NCH], F32, "stsum")
        self.stat_sq = self.sb([128, NCH], F32, "stsq")
        self.stat_mean = self.sb([128, NCH], F32, "stmean")
        self.stat_rstd = self.sb([128, NCH], F32, "strstd")
        self.stat_tmp = self.sb([128, NCH], F32, "sttmp")
        self.cq = self.sb([128, D], BF16, "cq")

        self.startup()
        seq = [self.sub_l0a, self.sub_l0ffn, self.sub_l0b, self.sub_l0ffn,
               self.sub_l1a, self.sub_l1moe, self.sub_l1b, self.sub_l1moe]
        for si in self.subs:
            self.emit_barrier()
            with ExitStack() as st:
                self.phase = st
                seq[si]()
            self.phase = None
            self.psum_guard = [cur(self.s_v), cur(self.s_s)]
        self.store_out()
        self.P.finalize()
        self.ctx.close()
        return self.nc

    def emit_barrier(self):
        P = self.P
        waits = [cur(s) for s in (self.s_ld, self.s_sc, self.s_gp, self.s_t,
                                  self.s_cc, self.s_v, self.s_s, self.s_g)]
        for eng in ("tensor", "sync"):
            P.do(eng, lambda e: e.nop(), waits=waits, chain=False)
        for eng in ("vector", "scalar", "gpsimd"):
            P.do(eng, lambda e: e.nop(), waits=waits)


    def startup(self):
        P = self.P
        P.dma("scalar", self.ccs[:], self.cconst.broadcast_to((128, 8)), self.s_sc)
        self.ccs_ready = cur(self.s_sc)
        P.do("gpsimd", lambda g: g.memset(self.identb[:], 0.0))
        P.do("gpsimd", lambda g: g.memset(self.identf[:], 0.0))
        P.do("gpsimd", lambda g: g.memset(self.ones1b[:], 1.0))
        P.do("gpsimd", lambda g: g.affine_select(
            out=self.identb[:], in_=self.identb[:], compare_op=ALU.not_equal,
            fill=1.0, base=0, pattern=[[-1, 128]], channel_multiplier=1))
        P.do("gpsimd", lambda g: g.affine_select(
            out=self.identf[:], in_=self.identf[:], compare_op=ALU.not_equal,
            fill=1.0, base=0, pattern=[[-1, 128]], channel_multiplier=1))
        self.ident_ready = cur(self.s_g)

        xload_waits = []
        if any(s in (2, 4, 6) for s in self.subs):
            mk8c = self.mk8c = self.sb([128, QB], mybir.dt.float8e4, "mk8c")
            P.do("gpsimd", lambda g: g.iota(
                self.r_sb[:, 0:32 * QB].rearrange("p (o q) -> p o q", q=QB),
                [[-128, 32], [1, QB]], base=0, channel_multiplier=-1,
                allow_small_or_imprecise_dtypes=True))
            iota_done = cur(self.s_g)
            for o in range(32):
                wv = [iota_done, self.ccs_ready] if o == 0 else [cur(self.s_gp)]
                P.do("vector", lambda v, o=o: v.tensor_scalar(
                    out=mk8c[:], in0=self.r_sb[:, o * QB:(o + 1) * QB],
                    scalar1=self.ccs[:, 0:1], scalar2=0.0,
                    op0=ALU.add, op1=ALU.is_ge), waits=wv)
                P.dma("gpsimd", self.maskd.ap()[:, o * QB:(o + 1) * QB], mk8c[:],
                      self.s_gp, waits=[cur(self.s_v)])
            self.mask_stored = cur(self.s_gp)
            xload_waits = [cur(self.s_v), self.mask_stored]
        P.dma("sync", self.r_sb[:].rearrange("p (i d) -> p i d", d=D),
              self.xs.rearrange("(i p) d -> p i d", p=128), self.s_ld,
              waits=xload_waits)
        if DEBUG_BLOCK:
            dt_ = self.nc.dram_tensor("dbg_r_postload", [128, NCH * D], F32,
                                      kind="ExternalOutput")
            P.dma("gpsimd", dt_.ap(), self.r_sb[:], self.s_gp,
                  waits=[cur(self.s_ld)])
            snap = cur(self.s_gp)
            for engx in ("vector", "scalar", "tensor", "sync", "gpsimd"):
                P.do(engx, lambda e: e.nop(), waits=[snap], chain=False)

    def store_out(self):
        P = self.P
        P.dma("gpsimd", self.ys.rearrange("(i p) d -> p i d", p=128),
              self.r_sb[:].rearrange("p (i d) -> p i d", d=D), self.s_gp,
              waits=[cur(self.s_ld), cur(self.s_v), cur(self.s_t)])
        P.do("gpsimd", lambda g: g.nop(), waits=[cur(self.s_gp)])


    def emit_ln_stats(self):
        P = self.P
        w0 = [cur(self.s_ld), cur(self.s_v), cur(self.s_t)]
        for i in range(NCH):
            ch = self.r_sb[:, i * D:(i + 1) * D]
            P.do("vector", lambda v, ch=ch, i=i: v.reduce_sum(
                out=self.stat_sum[:, i:i + 1], in_=ch, axis=X),
                waits=w0 if i == 0 else [])
            P.do("vector", lambda v, ch=ch, i=i: v.scalar_tensor_tensor(
                out=self.cq[:], in0=ch, scalar=1.0, in1=ch,
                op0=ALU.mult, op1=ALU.mult, accum_out=self.stat_sq[:, i:i + 1]))
        P.do("vector", lambda v: v.tensor_scalar_mul(
            self.stat_mean[:], self.stat_sum[:], 1.0 / D))
        P.do("vector", lambda v: v.tensor_tensor(
            out=self.stat_tmp[:], in0=self.stat_mean[:], in1=self.stat_mean[:],
            op=ALU.mult))
        P.do("vector", lambda v: v.scalar_tensor_tensor(
            out=self.stat_tmp[:], in0=self.stat_sq[:], scalar=1.0 / D,
            in1=self.stat_tmp[:], op0=ALU.mult, op1=ALU.subtract))
        P.do("scalar", lambda s: s.activation(
            out=self.stat_tmp[:], in_=self.stat_tmp[:], func=AF.Sqrt, bias=self.ccs[:, 2:3]),
            waits=[cur(self.s_v), self.ccs_ready])
        P.do("vector", lambda v: v.reciprocal(
            out=self.stat_rstd[:], in_=self.stat_tmp[:]),
            waits=[cur(self.s_s)])
        self.ln_ready = cur(self.s_v)

    def emit_ln_chunk(self, i, out_ap, waits=(), inc=False):
        P = self.P
        P.do("vector", lambda v, i=i, o=out_ap: v.tensor_scalar(
            out=o, in0=self.r_sb[:, i * D:(i + 1) * D],
            scalar1=self.stat_mean[:, i:i + 1], scalar2=self.stat_rstd[:, i:i + 1],
            op0=ALU.subtract, op1=ALU.mult), waits=list(waits))

    def emit_xlnT(self, quant=False):
        P = self.P
        ptA = self.ps([128, KC * 128], BF16, "ptT")
        ptB = self.ps([128, KC * 128], BF16, "ptT")
        xv = self.xlnT[:].rearrange("p (kc t) -> p kc t", t=TLOC)
        evac_done = [None, None]
        tr_done_prev = None
        for i in range(NCH):
            w = [tr_done_prev] if tr_done_prev is not None else []
            if quant:
                self.emit_ln_chunk(i, self.c32a[:], waits=w)
                self.emit_quant_chunk()
            else:
                self.emit_ln_chunk(i, self.cq[:], waits=w, inc=True)
            src_ready = cur(self.s_v)
            pt = ptA if i % 2 == 0 else ptB
            wa = [src_ready, self.ident_ready] + (self.psum_guard if i < 2 else [])
            if evac_done[i % 2] is not None:
                wa.append(evac_done[i % 2])
            for kc in range(KC):
                P.do("tensor", lambda t, kc=kc, pt=pt: t.transpose(
                    pt[:, kc * 128:(kc + 1) * 128],
                    self.cq[:, kc * 128:(kc + 1) * 128], self.identb[:]),
                    waits=wa if kc == 0 else [],
                    incs=[(self.s_t, 1)] if kc == KC - 1 else [])
            tr_done_prev = cur(self.s_t)
            P.do("vector", lambda v, i=i, pt=pt: v.tensor_copy(
                out=xv[:, :, i * 128:(i + 1) * 128],
                in_=pt[:].rearrange("p (kc t) -> p kc t", t=128)),
                waits=[tr_done_prev])
            evac_done[i % 2] = cur(self.s_v)
        self.xlnT_ready = cur(self.s_v)

    def emit_quant_chunk(self):
        P = self.P
        a, b, c, d = self.c32a, self.c32b, self.c32c, self.c32d
        qs, qr = self.qscale, self.qrec
        a3 = a[:].rearrange("p (blk j) -> p blk j", j=32)
        b3 = b[:].rearrange("p (blk j) -> p blk j", j=32)
        qrb = qr[:].rearrange("p (blk one) -> p blk one", one=1).broadcast_to((128, 32, 32))
        qsb = qs[:].rearrange("p (blk one) -> p blk one", one=1).broadcast_to((128, 32, 32))
        P.do("vector", lambda v: v.reduce_max(
            out=qs[:], in_=a3, axis=X, apply_absolute_value=True))
        P.do("vector", lambda v: v.tensor_scalar(
            out=qs[:], in0=qs[:], scalar1=1.0 / 7.5, scalar2=1e-30,
            op0=ALU.mult, op1=ALU.max))
        P.do("vector", lambda v: v.reciprocal(out=qr[:], in_=qs[:]))
        P.do("vector", lambda v: v.tensor_tensor(out=b3, in0=a3, in1=qrb, op=ALU.mult))
        P.do("vector", lambda v: v.scalar_tensor_tensor(
            out=c[:], in0=b[:], scalar=-1.0, in1=b[:], op0=ALU.mult, op1=ALU.max))
        P.do("vector", lambda v: v.tensor_single_scalar(
            out=d[:], in_=c[:], scalar=2.0, op=ALU.is_ge))
        P.do("vector", lambda v: v.tensor_single_scalar(
            out=c[:], in_=c[:], scalar=4.0, op=ALU.is_ge))
        P.do("vector", lambda v: v.tensor_scalar(
            out=d[:], in0=d[:], scalar1=-4.0, scalar2=8.0, op0=ALU.mult, op1=ALU.add))
        P.do("vector", lambda v: v.scalar_tensor_tensor(
            out=c[:], in0=c[:], scalar=-2.0, in1=d[:], op0=ALU.mult, op1=ALU.add))
        P.do("vector", lambda v: v.tensor_tensor(out=b[:], in0=b[:], in1=c[:], op=ALU.mult))
        P.do("vector", lambda v: v.tensor_scalar(
            out=b[:], in0=b[:], scalar1=MAGIC, scalar2=MAGIC,
            op0=ALU.add, op1=ALU.subtract))
        P.do("vector", lambda v: v.reciprocal(out=c[:], in_=c[:]))
        P.do("vector", lambda v: v.tensor_tensor(out=b[:], in0=b[:], in1=c[:], op=ALU.mult))
        cq3 = self.cq[:].rearrange("p (blk j) -> p blk j", j=32)
        P.do("vector", lambda v: v.tensor_tensor(out=cq3, in0=b3, in1=qsb, op=ALU.mult))

    def alloc_quant_scratch(self):
        self.c32a = self.sb([128, D], F32, "qa")
        self.c32b = self.sb([128, D], F32, "qb")
        self.c32c = self.sb([128, D], F32, "qc")
        self.c32d = self.sb([128, D], F32, "qd")
        self.qscale = self.sb([128, 32], F32, "qsc")
        self.qrec = self.sb([128, 32], F32, "qrc")


    def prep_direct(self, dst, src_hbm, kdim, fdim):
        kcn = kdim // 128
        self.P.dma("gpsimd",
                   dst[:, 0:kcn * fdim].rearrange("p (k f) -> p k f", f=fdim),
                   src_hbm.rearrange("(k p) f -> p k f", p=128), self.s_gp)
        return cur(self.s_gp)

    def prep_transposed(self, dst, src_hbm, fdim, kdim, stage, pst,
                        extra_waits=()):
        P = self.P
        fc = fdim // 128
        kcn = kdim // 128
        ev_done = [None, None]
        tr_prev = None
        dv = dst[:, 0:kcn * fdim].rearrange("p (k f) -> p k f", f=fdim)
        for f in range(fc):
            par = f % 2
            stg = stage[:, 0:kdim]
            wa = ([tr_prev] if tr_prev is not None else []) + \
                (list(extra_waits) if f == 0 else [])
            P.dma("gpsimd", stg,
                  src_hbm.rearrange("(a p) k -> p a k", p=128)[:, f, :],
                  self.s_gp, waits=wa)
            ld = cur(self.s_gp)
            pt = pst[par]
            wt = [ld, self.ident_ready] + \
                ([ev_done[par]] if ev_done[par] is not None else []) + \
                ((list(extra_waits) + self.psum_guard) if f == 0 else [])
            for k in range(kcn):
                P.do("tensor", lambda t, k=k, stg=stg, pt=pt: t.transpose(
                    pt[:, k * 128:(k + 1) * 128], stg[:, k * 128:(k + 1) * 128],
                    self.identb[:]),
                    waits=wt if k == 0 else [],
                    incs=[(self.s_t, 1)] if k == kcn - 1 else [])
            trd = cur(self.s_t)
            tr_prev = trd
            P.do("vector", lambda v, f=f, pt=pt, kcn=kcn: v.tensor_copy(
                out=dv[:, :, f * 128:(f + 1) * 128],
                in_=pt[:, 0:kcn * 128].rearrange("p (k t) -> p k t", t=128)),
                waits=[trd] + (list(extra_waits) if f == 0 else []))
            ev_done[par] = cur(self.s_v)
        return cur(self.s_v)


    def emit_proj_fm(self, warr, outF, out_sb, out_cols, t0, t1, wready,
                     scale=1.0, biascol=None, src=None, src_kc=KC,
                     extra_waits=(), src_ready=None, out_dtype_f32=False):
        P = self.P
        if src is None:
            src = self.xlnT
        if src_ready is None:
            src_ready = self.xlnT_ready
        fcn = outF // 128
        pj = [self.ps([128, 512], F32, "pj"), self.ps([128, 512], F32, "pj")]
        ev = [None, None]
        wv = warr[:].rearrange("p (k f) -> p k f", f=outF)
        sv = src[:].rearrange("p (k t) -> p k t", t=TLOC)
        idx = 0
        for f in range(fcn):
            for tt in range(t0, t1, 512):
                par = idx % 2
                idx += 1
                wa = ([wready, src_ready] + list(extra_waits)) if idx == 1 else []
                if ev[par] is not None:
                    wa.append(ev[par])
                for k in range(src_kc):
                    P.do("tensor", lambda t, k=k, f=f, tt=tt, par=par: t.matmul(
                        pj[par][:], wv[:, k, f * 128:(f + 1) * 128],
                        sv[:, k, tt:tt + 512],
                        start=(k == 0), stop=(k == src_kc - 1)),
                        waits=wa if k == 0 else [],
                        incs=[(self.s_t, 1)] if k == src_kc - 1 else [])
                mmd = cur(self.s_t)
                o = out_sb[:, f * out_cols + tt - t0:f * out_cols + tt - t0 + 512]
                if biascol is not None:
                    P.do("vector", lambda v, o=o, par=par, f=f: v.tensor_scalar(
                        out=o, in0=pj[par][:], scalar1=scale,
                        scalar2=biascol[:, f:f + 1], op0=ALU.mult, op1=ALU.add),
                        waits=[mmd])
                elif scale != 1.0:
                    P.do("vector", lambda v, o=o, par=par: v.tensor_scalar_mul(
                        o, pj[par][:], scale), waits=[mmd])
                else:
                    P.do("vector", lambda v, o=o, par=par: v.tensor_copy(
                        out=o, in_=pj[par][:]), waits=[mmd])
                ev[par] = cur(self.s_v)
        return cur(self.s_v)

    def emit_tokproj_add(self, lhs_sb, lhs_cols, rhs_sb, rhs_cols, kcn, chunks,
                         lready, rready, bias_ready=None, extra_waits=(),
                         lhs_view=None, rhs_view=None):
        P = self.P
        py = [self.ps([128, D], F32, "py"), self.ps([128, D], F32, "py")]
        ev = [None, None]
        lv = lhs_view if lhs_view is not None else \
            lhs_sb[:].rearrange("p (k t) -> p k t", t=lhs_cols)
        rv = rhs_view if rhs_view is not None else \
            rhs_sb[:].rearrange("p (k f) -> p k f", f=rhs_cols)
        br = self.biasrow if bias_ready is not None else None
        for n, (ci, toff) in enumerate(chunks):
            par = n % 2
            if n == 0:
                wa = [lready, rready] + list(extra_waits) + self.psum_guard
                if bias_ready is not None:
                    wa.append(bias_ready)
            else:
                wa = []
            if ev[par] is not None:
                wa.append(ev[par])
            nb = 1 if bias_ready is not None else 0
            for hh in range(2):
                for k in range(kcn):
                    first = (hh == 0 and k == 0)
                    last = (k == kcn - 1 and nb == 0)
                    P.do("tensor",
                         lambda t, k=k, hh=hh, toff=toff, par=par, first=first, last=last:
                         t.matmul(py[par][:, hh * 512:(hh + 1) * 512],
                                  lv[:, k, toff:toff + 128],
                                  rv[:, k, hh * 512:(hh + 1) * 512],
                                  start=(k == 0), stop=last),
                         waits=wa if first else [],
                         incs=[(self.s_t, 1)] if (last and hh == 1) else [])
                if nb:
                    P.do("tensor", lambda t, hh=hh, par=par, br=br: t.matmul(
                        py[par][:, hh * 512:(hh + 1) * 512], self.ones1b[:],
                        br[0:1, hh * 512:(hh + 1) * 512],
                        start=False, stop=True),
                        incs=[(self.s_t, 1)] if hh == 1 else [])
            mmd = cur(self.s_t)
            P.do("vector", lambda v, ci=ci, par=par: v.scalar_tensor_tensor(
                out=self.r_sb[:, ci * D:(ci + 1) * D], in0=py[par][:], scalar=1.0,
                in1=self.r_sb[:, ci * D:(ci + 1) * D], op0=ALU.mult, op1=ALU.add),
                waits=[mmd])
            ev[par] = cur(self.s_v)
        return cur(self.s_v)


    def sub_l0ffn(self):
        P = self.P
        self.biasrow = self.sb([1, D], BF16, "biasrow")
        self.alloc_quant_scratch()
        wstage32 = self.sb([128, KC * D], F32, "w32")
        wq_sb = self.sb([128, KC * D], BF16, "wq")
        wdT = self.sb([128, KC * D], BF16, "wdT")

        P.dma("sync", wstage32[:].rearrange("p (a k) -> p a k", k=D),
              self.w["L0_ffn_weight"].rearrange("(a p) k -> p a k", p=128), self.s_ld)
        wld = cur(self.s_ld)
        c32a_l, cq_l = self.c32a, self.cq
        for a in range(KC):
            P.do("vector", lambda v, a=a, c32a_l=c32a_l: v.tensor_copy(
                out=c32a_l[:], in_=wstage32[:, a * D:(a + 1) * D]),
                waits=[wld] if a == 0 else [])
            self.emit_quant_chunk()
            P.do("vector", lambda v, a=a, cq_l=cq_l: v.tensor_copy(
                out=wq_sb[:, a * D:(a + 1) * D], in_=cq_l[:]))
        wq_done = cur(self.s_v)
        ptw = self.ps([128, KC * 128], BF16, "pw")
        wv = wdT[:].rearrange("p (k j) -> p k j", j=D)
        prev_ev = None
        for a in range(KC):
            wt = [wq_done, self.ident_ready] + ([prev_ev] if prev_ev is not None else [])
            for k in range(KC):
                P.do("tensor", lambda t, a=a, k=k: t.transpose(
                    ptw[:, k * 128:(k + 1) * 128],
                    wq_sb[:, a * D + k * 128:a * D + (k + 1) * 128], self.identb[:]),
                    waits=wt if k == 0 else [],
                    incs=[(self.s_t, 1)] if k == KC - 1 else [])
            trd = cur(self.s_t)
            P.do("vector", lambda v, a=a: v.tensor_copy(
                out=wv[:, :, a * 128:(a + 1) * 128],
                in_=ptw[:].rearrange("p (k j) -> p k j", j=128)),
                waits=[trd])
            prev_ev = cur(self.s_v)
        wdT_ready = cur(self.s_v)
        P.dma("gpsimd", self.biasrow[:],
              self.w["L0_ffn_bias"].rearrange("(one d) -> one d", one=1), self.s_gp)
        bias_ready = cur(self.s_gp)

        if DEBUG_FFN:
            self.uid += 1
            dt_ = self.nc.dram_tensor(f"dbg_wdT{self.uid}", [128, KC * D], BF16,
                                      kind="ExternalOutput")
            P.dma("gpsimd", dt_.ap(), wdT[:], self.s_gp, waits=[wdT_ready])
            dt2_ = self.nc.dram_tensor(f"dbg_bias{self.uid}", [1, D], BF16,
                                       kind="ExternalOutput")
            P.dma("gpsimd", dt2_.ap(), self.biasrow[:], self.s_gp,
                  waits=[bias_ready])
            dt3_ = self.nc.dram_tensor(f"dbg_wq{self.uid}", [128, KC * D], BF16,
                                       kind="ExternalOutput")
            P.dma("gpsimd", dt3_.ap(), wq_sb[:], self.s_gp, waits=[wq_done])
        self.emit_ln_stats()
        self.emit_xlnT(quant=True)
        if DEBUG_BLOCK:
            self.uid += 1
            wdb = [cur(self.s_v), cur(self.s_t), cur(self.s_s), cur(self.s_g),
                   cur(self.s_ld), cur(self.s_sc), cur(self.s_gp)]
            for nmb, buf in (("rpre", self.r_sb), ("xdT", self.xlnT)):
                dt_ = self.nc.dram_tensor(f"dbg_{nmb}{self.uid}", list(buf.shape),
                                          buf[:].dtype, kind="ExternalOutput")
                P.dma("gpsimd", dt_.ap(), buf[:], self.s_gp, waits=wdb)
            snap = cur(self.s_gp)
            for engx in ("vector", "scalar", "tensor", "sync", "gpsimd"):
                P.do(engx, lambda e: e.nop(), waits=[snap], chain=False)
        self.emit_tokproj_add(self.xlnT, TLOC, wdT, D, KC,
                              [(i, i * 128) for i in range(NCH)],
                              self.xlnT_ready, wdT_ready, bias_ready=bias_ready)
        if DEBUG_BLOCK:
            self.uid += 1
            wdb = [cur(self.s_v), cur(self.s_t)]
            dt_ = self.nc.dram_tensor(f"dbg_rpost{self.uid}", [128, NCH * D], F32,
                                      kind="ExternalOutput")
            P.dma("gpsimd", dt_.ap(), self.r_sb[:], self.s_gp, waits=wdb)
            snap = cur(self.s_gp)
            for engx in ("vector", "scalar", "tensor", "sync", "gpsimd"):
                P.do(engx, lambda e: e.nop(), waits=[snap], chain=False)


    @contextmanager
    def subphase(self):
        old = self.phase
        with ExitStack() as st:
            self.phase = st
            yield
        self.phase = old
        self.psum_guard = [cur(self.s_v), cur(self.s_s)]

    def emit_bias_col(self, src_hbm, dst_col, scale=1.0):
        P = self.P
        if getattr(self, "_tmp8", None) is None or self._tmp8_phase is not self.phase:
            self._tmp8 = self.sb([8, 128], F32, "b8")
            self._tmp8_phase = self.phase
            self._tmp8_busy = []
        tmp8 = self._tmp8
        prev = list(self._tmp8_busy)
        P.dma("sync", tmp8[:], src_hbm, self.s_ld, waits=prev)
        ld = cur(self.s_ld)
        with self.subphase():
            pb = self.ps([128, 8], F32, "pb")
            P.do("tensor", lambda t: t.transpose(pb[:], tmp8[:],
                                                 self.identf[0:8, 0:8]),
                 waits=[ld, self.ident_ready] + self.psum_guard,
                 incs=[(self.s_t, 1)])
            trd = cur(self.s_t)
            self._tmp8_busy = [trd]
            P.do("vector", lambda v: v.tensor_scalar_mul(dst_col[:], pb[:], scale),
                 waits=[trd])
        return cur(self.s_v)

    def emit_v_pipeline(self, wv_view, outF, wready, src_view, src_kc, src_ready,
                        vfm, vstg, vshard, vbias_col, vbias_ready, stride):
        P = self.P
        fw = min(outF, 128)
        fcn = (outF + 127) // 128
        pj = [self.ps([128, 512], F32, "pvj"), self.ps([128, 512], F32, "pvj")]
        pv = self.ps([128, NCH * fw], BF16, "pv")
        vsv = vstg[:, 0:NCH * stride].rearrange("p (mc c) -> p mc c", c=stride)
        P.do("gpsimd", lambda g: g.memset(vstg[:, 0:NCH * stride], 0.0))
        P.do("gpsimd", lambda g: g.memset(vsv[:, :, 64:65], 1.0))
        if stride == 130:
            P.do("gpsimd", lambda g: g.memset(vsv[:, :, 129:130], 1.0))
        ones_done = cur(self.s_g)
        prev_tr = None
        prev_dma = None
        prev_evd = None
        ev = [None, None]
        for hc in range(fcn):
            for tt in range(4):
                par = tt % 2
                wa = ([wready, src_ready] + self.psum_guard) \
                    if (hc == 0 and tt == 0) else []
                if ev[par] is not None:
                    wa.append(ev[par])
                for k in range(src_kc):
                    P.do("tensor", lambda t, k=k, hc=hc, tt=tt, par=par: t.matmul(
                        pj[par][0:fw, :], wv_view[:, k, hc * fw:hc * fw + fw],
                        src_view[:, k, tt * 512:(tt + 1) * 512],
                        start=(k == 0), stop=(k == src_kc - 1)),
                        waits=wa if k == 0 else [],
                        incs=[(self.s_t, 1)] if k == src_kc - 1 else [])
                mmd = cur(self.s_t)
                wv2 = [mmd]
                if prev_tr is not None and tt == 0:
                    wv2.append(prev_tr)
                if vbias_col is not None:
                    if vbias_ready is not None:
                        wv2.append(vbias_ready)
                    P.do("vector", lambda v, tt=tt, par=par, hc=hc: v.tensor_scalar(
                        out=vfm[0:fw, tt * 512:(tt + 1) * 512], in0=pj[par][0:fw, :],
                        scalar1=1.0, scalar2=vbias_col[0:fw, hc:hc + 1],
                        op0=ALU.mult, op1=ALU.add), waits=wv2)
                else:
                    P.do("vector", lambda v, tt=tt, par=par: v.tensor_copy(
                        out=vfm[0:fw, tt * 512:(tt + 1) * 512], in_=pj[par][0:fw, :]),
                        waits=wv2)
                ev[par] = cur(self.s_v)
            vfm_done = cur(self.s_v)
            for mc in range(NCH):
                wt = [vfm_done, self.ident_ready]
                if mc == 0 and prev_evd is not None:
                    wt.append(prev_evd)
                P.do("tensor", lambda t, mc=mc: t.transpose(
                    pv[:, mc * fw:(mc + 1) * fw],
                    vfm[0:fw, mc * 128:(mc + 1) * 128],
                    self.identb[0:fw, 0:fw]),
                    waits=wt if mc == 0 else [],
                    incs=[(self.s_t, 1)] if mc == NCH - 1 else [])
            prev_tr = cur(self.s_t)
            wv3 = [prev_tr, ones_done]
            if prev_dma is not None:
                wv3.append(prev_dma)
            pvv = pv[:].rearrange("p (mc c) -> p mc c", c=fw)
            P.do("vector", lambda v: v.tensor_copy(
                out=vsv[:, :, 0:64], in_=pvv[:, :, 0:64]), waits=wv3)
            if stride == 130:
                P.do("vector", lambda v: v.tensor_copy(
                    out=vsv[:, :, 65:129], in_=pvv[:, :, 64:128]))
            prev_evd = cur(self.s_v)
            P.dma("scalar", vshard.rearrange("p (hc c) -> p hc c",
                                             c=NCH * stride)[:, hc, :],
                  vstg[:, 0:NCH * stride], self.s_sc, waits=[prev_evd])
            prev_dma = cur(self.s_sc)
        return prev_dma, prev_tr

    def emit_attention(self, nm, causal, shared_kv, kdim_rows,
                       prep_q, prep_k, prep_v, prep_o,
                       q_bias=None, k_bias=None, v_bias=None, o_bias=None,
                       latent=None):
        P = self.P
        nc = self.nc
        stride = 65 if shared_kv else 130
        vsh_cols = NCH * stride * (1 if shared_kv else 8)
        kshard = nc.dram_tensor(f"ksh_{nm}", [kdim_rows, TLOC], BF16)
        vshard = nc.dram_tensor(f"vsh_{nm}", [128, vsh_cols], BF16)
        kg = nc.dram_tensor(f"kg_{nm}", [8 * kdim_rows, TLOC], BF16)
        vg = nc.dram_tensor(f"vg_{nm}", [8 * 128, vsh_cols], BF16)

        q_sb = self.sb([128, KC * TLOC], BF16, "q")
        warr = self.sb([128, KC * D], BF16, "wa")
        wstage = self.sb([128, D], BF16, "ws")
        k_tile = self.sb([128, 8192], BF16, "kt")
        vt = self.sb([128, 64 * stride], BF16, "vt")
        pst = self.sb([128, 2048], BF16, "P")
        rec_sb = self.sb([1, 512], BF16, "rec")
        rb_sb = self.sb([128, 512], BF16, "rb")
        if causal:
            mask8 = self.sb([128, 32 * QB], mybir.dt.float8e4, "mk8")
        qcol = self.sb([128, 8], F32, "qc8")
        kcol = self.sb([128, 8], F32, "kc8")
        vcol = self.sb([128, 8], F32, "vc8")
        vfm = k_tile[:, 0:2048]
        vstg = k_tile[:, 2048:2048 + NCH * stride]

        if DEBUG_BLOCK:
            wdb = [cur(self.s_v), cur(self.s_t), cur(self.s_s), cur(self.s_g),
                   cur(self.s_ld), cur(self.s_sc), cur(self.s_gp)]
            dt_ = nc.dram_tensor(f"dbg_r_start_{nm}", [128, NCH * D], F32,
                                 kind="ExternalOutput")
            P.dma("gpsimd", dt_.ap(), self.r_sb[:], self.s_gp, waits=wdb)
            snap = cur(self.s_gp)
            for engx in ("vector", "scalar", "tensor", "sync", "gpsimd"):
                P.do(engx, lambda e: e.nop(), waits=[snap], chain=False)
        self.emit_ln_stats()
        with self.subphase():
            self.emit_xlnT(quant=False)
        xv = self.xlnT[:].rearrange("p (k t) -> p k t", t=TLOC)

        qb_ready = kb_ready = vb_ready = None
        if q_bias is not None:
            qb_ready = self.emit_bias_col(q_bias, qcol, scale=0.125)
        if k_bias is not None:
            kb_ready = self.emit_bias_col(k_bias, kcol)
        if v_bias is not None:
            vb_ready = self.emit_bias_col(v_bias, vcol)
        if DEBUG_ATTN and q_bias is not None:
            dt_ = nc.dram_tensor(f"dbg_qcol_early_{nm}", [128, 8], F32,
                                 kind="ExternalOutput")
            P.dma("gpsimd", dt_.ap(), qcol[:], self.s_gp, waits=[qb_ready])
            dt2_ = nc.dram_tensor(f"dbg_r_postb_{nm}", [128, NCH * D], F32,
                                  kind="ExternalOutput")
            P.dma("gpsimd", dt2_.ap(), self.r_sb[:], self.s_gp,
                  waits=[qb_ready, vb_ready])

        if causal:
            P.dma("scalar", mask8[:], self.maskd.ap(), self.s_sc,
                  waits=[cur(self.s_gp)])
            mask_ready = cur(self.s_sc)

        if latent is not None:
            latT = vt[:, 0:2 * TLOC]
            lat_view = latT.rearrange("p (k t) -> p k t", t=TLOC)
            wkvd = warr[:, 0:2048]
            wku = warr[:, 2048:4096]
            wvu = warr[:, 4096:6144]
            kvd_ready = latent["prep_down"](wkvd)
            kup_ready = latent["prep_kup"](wku)
            vup_ready = latent["prep_vup"](wvu)
            with self.subphase():
                lat_ready = self.emit_proj_fm_view(
                    wkvd.rearrange("p (k f) -> p k f", f=256), 256,
                    latT, TLOC, kvd_ready, xv, KC, self.xlnT_ready)
            ksrc_view, ksrc_kc, ksrc_ready = lat_view, 2, lat_ready
            kw_view = wku.rearrange("p (k f) -> p k f", f=D)
            vw_view = wvu.rearrange("p (k f) -> p k f", f=D)
            kw_ready, vw_ready = kup_ready, vup_ready
        else:
            kw_ready = prep_k(warr)
            ksrc_view, ksrc_kc, ksrc_ready = xv, KC, self.xlnT_ready
            kw_view = warr[:, 0:KC * self.kv_outF].rearrange(
                "p (k f) -> p k f", f=self.kv_outF)

        koutF = 64 if shared_kv else D
        with self.subphase():
            kfm_ready = self.emit_proj_fm_view(
                kw_view, koutF, q_sb, TLOC, kw_ready, ksrc_view, ksrc_kc,
                ksrc_ready, biascol=kcol if k_bias is not None else None,
                bias_ready=kb_ready)
        if shared_kv:
            P.dma("scalar", kshard.ap(), q_sb[0:64, 0:TLOC], self.s_sc,
                  waits=[kfm_ready])
        else:
            P.dma("scalar", kshard.ap().rearrange("(hc p) t -> p hc t", p=128),
                  q_sb[:].rearrange("p (hc t) -> p hc t", t=TLOC), self.s_sc,
                  waits=[kfm_ready])
        kshard_dma = cur(self.s_sc)

        if latent is not None:
            vsrc_view, vsrc_kc, vsrc_ready = lat_view, 2, lat_ready
        else:
            vw_ready = prep_v(warr)
            vw_view = warr[:, 0:KC * self.kv_outF].rearrange(
                "p (k f) -> p k f", f=self.kv_outF)
            vsrc_view, vsrc_kc, vsrc_ready = xv, KC, self.xlnT_ready
        with self.subphase():
            vshard_dma, vtr_done = self.emit_v_pipeline(
                vw_view, koutF, vw_ready, vsrc_view, vsrc_kc, vsrc_ready,
                vfm, vstg, vshard.ap(), vcol if v_bias is not None else None,
                vb_ready, stride)

        P.do("gpsimd", lambda g: g.collective_compute(
            "AllGather", ALU.bypass, replica_groups=[list(range(8))],
            ins=[kshard.ap().opt()], outs=[kg.ap().opt()]),
            waits=[kshard_dma], incs=[(self.s_cc, 1)], chain=False)
        P.do("gpsimd", lambda g: g.collective_compute(
            "AllGather", ALU.bypass, replica_groups=[list(range(8))],
            ins=[vshard.ap().opt()], outs=[vg.ap().opt()]),
            waits=[vshard_dma], incs=[(self.s_cc, 1)], chain=False)
        gather_done = cur(self.s_cc)

        qw_ready = prep_q(warr, extra_waits=[kfm_ready, vtr_done, cur(self.s_v)])
        with self.subphase():
            q_ready = self.emit_proj_fm_view(
                warr[:].rearrange("p (k f) -> p k f", f=D), D, q_sb, TLOC,
                qw_ready, xv, KC, self.xlnT_ready, scale=0.125,
                biascol=qcol if q_bias is not None else None, bias_ready=qb_ready,
                evac_extra=[kshard_dma])
        qv = q_sb[:].rearrange("p (oc t) -> p oc t", t=TLOC)
        if DEBUG_ATTN:
            wdb = [q_ready, cur(self.s_t)]
            for nmb, buf in (("q0", q_sb), ("wq", warr), ("xlt", self.xlnT),
                             ("qcol", qcol), ("resid", self.r_sb)):
                dt_ = nc.dram_tensor(f"dbg_{nmb}_{nm}", list(buf.shape),
                                     buf[:].dtype, kind="ExternalOutput")
                P.dma("gpsimd", dt_.ap(), buf[:], self.s_gp, waits=wdb)
            P.do("gpsimd", lambda g: g.nop(), waits=[cur(self.s_gp)])

        ow_ready = prep_o(warr, extra_waits=[q_ready])
        if o_bias is not None:
            self.biasrow = self.sb([1, D], BF16, "biasrow")
            P.dma("gpsimd", self.biasrow[:], o_bias, self.s_gp)
            ob_ready = cur(self.s_gp)
        else:
            ob_ready = None

        with self.subphase():
            ps_s = [self.ps([128, 1024], F32, "pss"), self.ps([128, 1024], F32, "pss")]
            ps_o = [self.ps([65, 512], F32, "pso"), self.ps([65, 512], F32, "pso")]
            recB = self.ps([128, 512], F32, "prb")
            ktv = k_tile
            vtv = vt[:].rearrange("p (mc c) -> p mc c", c=stride)
            exp0_done = [None, None]
            exp1_done = [None, None]
            av_done = [None, None]
            pmask = [None, None]
            opsum_free = None
            kv_free = None
            kv_free0 = None
            kv_half = [None, None]
            fresh_load = False
            for b in range(2):
                for pair in range(8):
                    xw = NCH // 4 * stride
                    fresh_load = (not shared_kv) or pair == 0
                    if shared_kv:
                        if pair == 0:
                            src = kg.ap().rearrange(
                                "(r d) (l o) -> d l r o", d=64, o=512)
                            vsrc = vg.ap().rearrange(
                                "(r p) (l x) -> p l r x", p=128, x=xw)
                            for kk in range(2):
                                if kk == 0:
                                    lw = [kv_free0] if kv_free0 is not None \
                                        else [gather_done, vshard_dma]
                                else:
                                    lw = [kv_free] if kv_free is not None else []
                                P.dma("sync", ktv[0:64, :].rearrange(
                                    "d (kk r o) -> d kk r o", kk=2, r=8)[:, kk],
                                    src[:, 2 * b + kk, :, :], self.s_ld,
                                    waits=lw)
                                P.dma("sync", ktv[64:128, :].rearrange(
                                    "d (kk r o) -> d kk r o", kk=2, r=8)[:, kk],
                                    src[:, 2 * b + kk, :, :], self.s_ld)
                                P.dma("sync", vt[:].rearrange(
                                    "p (kk r x) -> p kk r x", kk=2, r=8)[:, kk],
                                    vsrc[:, 2 * b + kk, :, :], self.s_ld)
                                kv_half[kk] = cur(self.s_ld)
                    else:
                        src = kg.ap().rearrange(
                            "(r hc p) (l o) -> p hc l r o", r=8, p=128, o=512)
                        vsrc = vg.ap().rearrange(
                            "(r p) (pr l x) -> p pr l r x", p=128, pr=8, x=xw)
                        for kk in range(2):
                            if kk == 0:
                                lw = [kv_free0] if kv_free0 is not None \
                                    else [gather_done, vshard_dma]
                            else:
                                lw = [kv_free] if kv_free is not None else []
                            P.dma("sync", ktv[:].rearrange(
                                "p (kk r o) -> p kk r o", kk=2, r=8)[:, kk],
                                src[:, pair, 2 * b + kk, :, :], self.s_ld,
                                waits=lw)
                            P.dma("sync", vt[:].rearrange(
                                "p (kk r x) -> p kk r x", kk=2, r=8)[:, kk],
                                vsrc[:, pair, 2 * b + kk, :, :], self.s_ld)
                            kv_half[kk] = cur(self.s_ld)
                    for l in range(2):
                        M = (32 * (l + 1)) if causal else 64
                        qoff = b * 1024 + l * 512
                        for j in range(M + 1):
                            par = j % 2
                            if j < M:
                                wa = []
                                if exp1_done[par] is not None:
                                    wa.append(exp1_done[par])
                                if b == 0 and pair == 0 and l == 0 and j == 0:
                                    wa += [ow_ready, q_ready]
                                    wa += self.psum_guard
                                if j == 0 and l == 0 and fresh_load:
                                    wa.append(kv_half[0])
                                if j == 32 and fresh_load and \
                                        (l == (1 if causal else 0)):
                                    wa.append(kv_half[1])
                                for hh in range(2):
                                    P.do("tensor",
                                         lambda t, j=j, hh=hh, pair=pair, par=par, qoff=qoff:
                                         t.matmul(
                                             ps_s[par][:, hh * 512:(hh + 1) * 512],
                                             ktv[hh * 64:hh * 64 + 64,
                                                 j * 128:(j + 1) * 128],
                                             qv[hh * 64:hh * 64 + 64, pair,
                                                qoff:qoff + 512],
                                             start=True, stop=True),
                                         waits=wa if hh == 0 else [],
                                         incs=[(self.s_t, 1)] if hh == 1 else [])
                                sc_done = cur(self.s_t)
                                P.do("scalar", lambda s, par=par: s.activation(
                                    out=pst[:, par * 1024:(par + 1) * 1024],
                                    in_=ps_s[par][:], func=AF.Exp),
                                    waits=[sc_done])
                                exp0_done[par] = cur(self.s_s)
                                exp1_done[par] = exp0_done[par]
                                masked = causal and j >= M - 32
                                if masked:
                                    o = j - (M - 32)
                                    mk = mask8[:, o * 512:(o + 1) * 512]
                                    P.do("vector", lambda v, par=par, mk=mk:
                                         v.tensor_tensor(
                                             out=pst[:, par * 1024:par * 1024 + 512],
                                             in0=pst[:, par * 1024:par * 1024 + 512],
                                             in1=mk, op=ALU.mult),
                                         waits=[exp0_done[par], mask_ready])
                                    pm0 = cur(self.s_v)
                                    P.do("gpsimd", lambda g, par=par, mk=mk:
                                         g.tensor_tensor(
                                             out=pst[:, par * 1024 + 512:
                                                     par * 1024 + 1024],
                                             in0=pst[:, par * 1024 + 512:
                                                     par * 1024 + 1024],
                                             in1=mk, op=ALU.mult),
                                         waits=[exp1_done[par], mask_ready])
                                    pmask[par] = (pm0, cur(self.s_g))
                                else:
                                    pmask[par] = None
                            if j >= 1:
                                jj = j - 1
                                pj_ = jj % 2
                                for hh in range(2):
                                    if pmask[pj_] is not None:
                                        wa2 = [pmask[pj_][hh]]
                                    elif hh == 0:
                                        wa2 = [exp0_done[pj_]]
                                    else:
                                        wa2 = []
                                    if hh == 0 and jj == 0 and \
                                            opsum_free is not None:
                                        wa2.append(opsum_free)
                                    voff = jj * stride + \
                                        (hh * 65 if not shared_kv else 0)
                                    P.do("tensor",
                                         lambda t, jj=jj, hh=hh, pj_=pj_, voff=voff:
                                         t.matmul(
                                             ps_o[hh][:],
                                             vt[:, voff:voff + 65],
                                             pst[:, pj_ * 1024 + hh * 512:
                                                 pj_ * 1024 + (hh + 1) * 512],
                                             start=(jj == 0), stop=(jj == M - 1)),
                                         waits=wa2,
                                         incs=[(self.s_t, 1)] if hh == 1 else [])
                                av_done[pj_] = cur(self.s_t)
                                if l == 1 and jj == 31:
                                    kv_free0 = cur(self.s_t)
                        avf = cur(self.s_t)
                        for hh in range(2):
                            P.do("vector", lambda v, hh=hh: v.reciprocal(
                                out=rec_sb[0:1, :], in_=ps_o[hh][64:65, :]),
                                waits=[avf] if hh == 0 else [cur(self.s_t)])
                            rd = cur(self.s_v)
                            P.do("tensor", lambda t: t.matmul(
                                recB[:], self.ones1b[:], rec_sb[0:1, :],
                                start=True, stop=True),
                                waits=[rd], incs=[(self.s_t, 1)])
                            bc = cur(self.s_t)
                            P.do("vector", lambda v: v.tensor_copy(
                                out=rb_sb[:], in_=recB[:]), waits=[bc])
                            oco = b * 8192 + pair * 1024 + l * 512
                            P.do("vector", lambda v, hh=hh, oco=oco:
                                 v.tensor_tensor(
                                     out=self.xlnT[hh * 64:(hh + 1) * 64,
                                                   oco:oco + 512],
                                     in0=ps_o[hh][0:64, :], in1=rb_sb[0:64, :],
                                     op=ALU.mult))
                        opsum_free = cur(self.s_v)
                    kv_free = cur(self.s_t)
            ocat_ready = cur(self.s_v)
        if DEBUG_ATTN:
            wdb = [cur(self.s_v), cur(self.s_t), cur(self.s_s)]
            for nmb, buf in (("ocat", self.xlnT), ("qsb", q_sb), ("ktl", k_tile),
                             ("vtl", vt), ("pstd", pst)):
                dt_ = nc.dram_tensor(f"dbg_{nmb}_{nm}", list(buf.shape),
                                     buf[:].dtype, kind="ExternalOutput")
                P.dma("gpsimd", dt_.ap(), buf[:], self.s_gp, waits=wdb)
            if causal:
                dtm = nc.dram_tensor(f"dbg_mask_{nm}", list(mask8.shape),
                                     mybir.dt.float8e4, kind="ExternalOutput")
                P.dma("gpsimd", dtm.ap(), mask8[:], self.s_gp, waits=wdb)
            P.do("gpsimd", lambda g: g.nop(), waits=[cur(self.s_gp)])
        for b in range(2):
            with self.subphase():
                self.emit_tokproj_add(
                    None, 1024, None, D, KC,
                    [(b * 8 + tc, tc * 128) for tc in range(8)],
                    ocat_ready, ow_ready, bias_ready=ob_ready,
                    lhs_view=self.xlnT[:, b * 8192:(b + 1) * 8192]
                        .rearrange("p (k t) -> p k t", t=1024),
                    rhs_view=warr[:].rearrange("p (k f) -> p k f", f=D))

    def emit_proj_fm_view(self, wv, outF, out_sb, out_cols, wready, src_view,
                          src_kc, src_ready, scale=1.0, biascol=None,
                          bias_ready=None, evac_extra=()):
        P = self.P
        fw = min(outF, 128)
        fcn = (outF + 127) // 128
        pj = [self.ps([128, 512], F32, "pj"), self.ps([128, 512], F32, "pj")]
        ev = [None, None]
        idx = 0
        for f in range(fcn):
            for tt in range(0, TLOC, 512):
                par = idx % 2
                idx += 1
                wa = ([wready, src_ready] + self.psum_guard) if idx == 1 else []
                if ev[par] is not None:
                    wa.append(ev[par])
                for k in range(src_kc):
                    P.do("tensor", lambda t, k=k, f=f, tt=tt, par=par: t.matmul(
                        pj[par][0:fw, :], wv[:, k, f * fw:f * fw + fw],
                        src_view[:, k, tt:tt + 512],
                        start=(k == 0), stop=(k == src_kc - 1)),
                        waits=wa if k == 0 else [],
                        incs=[(self.s_t, 1)] if k == src_kc - 1 else [])
                mmd = cur(self.s_t)
                o = out_sb[0:fw, f * out_cols + tt:f * out_cols + tt + 512]
                wv2 = [mmd] + list(evac_extra)
                if biascol is not None:
                    if bias_ready is not None:
                        wv2.append(bias_ready)
                    P.do("vector", lambda v, o=o, par=par, f=f: v.tensor_scalar(
                        out=o, in0=pj[par][0:fw, :], scalar1=scale,
                        scalar2=biascol[0:fw, f:f + 1], op0=ALU.mult, op1=ALU.add),
                        waits=wv2)
                elif scale != 1.0:
                    P.do("vector", lambda v, o=o, par=par: v.tensor_scalar_mul(
                        o, pj[par][0:fw, :], scale), waits=wv2)
                else:
                    P.do("vector", lambda v, o=o, par=par: v.tensor_copy(
                        out=o, in_=pj[par][0:fw, :]), waits=wv2)
                ev[par] = cur(self.s_v)
        return cur(self.s_v)


    def sub_l0a(self):
        self.kv_outF = 64
        w = self.w

        def prep_q(warr, extra_waits=()):
            ws = self.sb([128, 2 * D], BF16, "wsq") if False else None
            with self.subphase():
                wst = self.sb([128, D], BF16, "wst")
                pstA = self.ps([128, KC * 128], BF16, "pt")
                pstB = self.ps([128, KC * 128], BF16, "pt")
                return self.prep_transposed(
                    warr, w["L0_attn_a_query_proj"].rearrange("h k d -> (h k) d"),
                    D, D, wst, [pstA, pstB], extra_waits)

        def prep_k(warr):
            return self.prep_direct_into(warr[:, 0:KC * 64],
                                         w["L0_attn_a_key_proj"], D, 64)

        def prep_v(warr):
            return self.prep_direct_into(warr[:, 0:KC * 64],
                                         w["L0_attn_a_value_proj"], D, 64,
                                         waits=[cur(self.s_t)])

        def prep_o(warr, extra_waits=()):
            with self.subphase():
                wst = self.sb([128, D], BF16, "wst")
                pstA = self.ps([128, KC * 128], BF16, "pt")
                pstB = self.ps([128, KC * 128], BF16, "pt")
                return self.prep_transposed(
                    warr, w["L0_attn_a_output_proj"].rearrange("d h v -> d (h v)"),
                    D, D, wst, [pstA, pstB], extra_waits)

        self.emit_attention("l0a", causal=False, shared_kv=True, kdim_rows=64,
                            prep_q=prep_q, prep_k=prep_k, prep_v=prep_v,
                            prep_o=prep_o)

    def sub_l0b(self):
        self.kv_outF = D
        w = self.w

        def mk_prep(key, extra=False):
            def prep(warr, extra_waits=()):
                with self.subphase():
                    wst = self.sb([128, D], BF16, "wst")
                    pstA = self.ps([128, KC * 128], BF16, "pt")
                    pstB = self.ps([128, KC * 128], BF16, "pt")
                    src = w[key]
                    if key == "L0_attn_b_w_o":
                        src2 = src
                    else:
                        src2 = src.rearrange("h k d -> (h k) d")
                    return self.prep_transposed(warr, src2, D, D, wst,
                                                [pstA, pstB], extra_waits)
            return prep

        def prep_v(warr):
            return mk_prep("L0_attn_b_w_v")(warr, extra_waits=[cur(self.s_t)])

        self.emit_attention("l0b", causal=True, shared_kv=False, kdim_rows=D,
                            prep_q=mk_prep("L0_attn_b_w_q"),
                            prep_k=mk_prep("L0_attn_b_w_k"),
                            prep_v=prep_v,
                            prep_o=mk_prep("L0_attn_b_w_o"))

    def sub_l1a(self):
        self.kv_outF = D
        w = self.w

        def prep_down(wkvd):
            return self.prep_direct_into(wkvd, w["L1_attn_a_Wkv_down"], D, 256)

        def prep_kup(wku):
            return self.prep_direct_into(
                wku, w["L1_attn_a_Wk_up"].rearrange("c h k -> c (h k)"), 256, D)

        def prep_vup(wvu):
            return self.prep_direct_into(
                wvu, w["L1_attn_a_Wv_up"].rearrange("c h k -> c (h k)"), 256, D)

        def prep_q(warr, extra_waits=()):
            return self.prep_direct_into(
                warr, w["L1_attn_a_Wq"].rearrange("d h k -> d (h k)"), D, D,
                waits=list(extra_waits))

        def prep_o(warr, extra_waits=()):
            return self.prep_direct_into(
                warr, w["L1_attn_a_Wout"].rearrange("h v d -> (h v) d"), D, D,
                waits=list(extra_waits))

        self.emit_attention("l1a", causal=True, shared_kv=False, kdim_rows=D,
                            prep_q=prep_q, prep_k=None, prep_v=None,
                            prep_o=prep_o,
                            latent=dict(prep_down=prep_down, prep_kup=prep_kup,
                                        prep_vup=prep_vup))

    def sub_l1b(self):
        self.kv_outF = D
        w = self.w

        def mk_prep(key):
            def prep(warr, extra_waits=()):
                sems = list(extra_waits)
                for h in range(NH):
                    self.P.dma(
                        "gpsimd",
                        warr[:].rearrange("p (k f) -> p k f", f=D)
                        [:, :, h * HD:(h + 1) * HD],
                        w[key].rearrange("h (k p) f -> h p k f", p=128)[h],
                        self.s_gp, waits=sems if h == 0 else ())
                return cur(self.s_gp)
            return prep

        def prep_v(warr):
            return mk_prep("L1_attn_b_V_w")(warr, extra_waits=[cur(self.s_t)])

        def prep_o(warr, extra_waits=()):
            return self.prep_direct_into(
                warr, w["L1_attn_b_O_w"].rearrange("h v d -> (h v) d"), D, D,
                waits=list(extra_waits))

        self.emit_attention(
            "l1b", causal=True, shared_kv=False, kdim_rows=D,
            prep_q=mk_prep("L1_attn_b_Q_w"),
            prep_k=mk_prep("L1_attn_b_K_w"),
            prep_v=prep_v,
            prep_o=prep_o,
            q_bias=w["L1_attn_b_Q_b"].rearrange("(a hh) k -> a (hh k)", a=8),
            k_bias=w["L1_attn_b_K_b"].rearrange("(a hh) k -> a (hh k)", a=8),
            v_bias=w["L1_attn_b_V_b"].rearrange("(a hh) k -> a (hh k)", a=8),
            o_bias=w["L1_attn_b_O_b"].rearrange("(one d) -> one d", one=1))

    def prep_direct_into(self, dst, src_hbm, kdim, fdim, waits=()):
        kcn = kdim // 128
        self.P.dma("gpsimd",
                   dst[:, 0:kcn * fdim].rearrange("p (k f) -> p k f", f=fdim),
                   src_hbm.rearrange("(k p) f -> p k f", p=128), self.s_gp,
                   waits=list(waits))
        return cur(self.s_gp)

    def emit_glu(self, wg, wu, wready, act_out, dcn, fdim, src_view, src_kc,
                 src_ready, coefT=None, e=None, esel=None, coef_ready=None,
                 gs=None, act_war=()):
        P = self.P
        with self.subphase():
            psA = self.ps([128, 512], F32, "pga")
            psB = self.ps([128, 512], F32, "pgb")
            pcb = self.ps([128, 512], F32, "pcb") if coefT is not None else None
            wgv = wg[:, 0:src_kc * fdim].rearrange("p (k f) -> p k f", f=fdim)
            wuv = wu[:, 0:src_kc * fdim].rearrange("p (k f) -> p k f", f=fdim)
            silu_prev = None
            ttA_prev = None
            ttB_prev = None
            scale_prev = None
            first = True
            for tt in range(4):
                if coefT is not None:
                    wv = [coef_ready]
                    if scale_prev is not None:
                        wv.append(scale_prev)
                    if first:
                        wv += self.psum_guard
                    P.do("tensor", lambda t, tt=tt: t.matmul(
                        pcb[:], esel[:, e * 128:(e + 1) * 128],
                        coefT[:, tt * 512:(tt + 1) * 512],
                        start=True, stop=True),
                        waits=wv, incs=[(self.s_t, 1)])
                    pcb_mmd = cur(self.s_t)
                for dc in range(dcn):
                    wa = [wready, src_ready]
                    if silu_prev is not None:
                        wa.append(silu_prev)
                    if first:
                        wa += self.psum_guard
                        first = False
                    for k in range(src_kc):
                        P.do("tensor", lambda t, k=k, dc=dc, tt=tt: t.matmul(
                            psA[:], wgv[:, k, dc * 128:(dc + 1) * 128],
                            src_view[:, k, tt * 512:(tt + 1) * 512],
                            start=(k == 0), stop=(k == src_kc - 1)),
                            waits=wa if k == 0 else [],
                            incs=[(self.s_t, 1)] if k == src_kc - 1 else [])
                    gmmd = cur(self.s_t)
                    P.do("scalar", lambda s: s.activation(
                        out=gs[:], in_=psA[:], func=AF.Silu),
                        waits=[gmmd] + ([ttA_prev] if ttA_prev else []))
                    silu_prev = cur(self.s_s)
                    wb = [ttB_prev] if ttB_prev is not None else []
                    for k in range(src_kc):
                        P.do("tensor", lambda t, k=k, dc=dc, tt=tt: t.matmul(
                            psB[:], wuv[:, k, dc * 128:(dc + 1) * 128],
                            src_view[:, k, tt * 512:(tt + 1) * 512],
                            start=(k == 0), stop=(k == src_kc - 1)),
                            waits=wb if k == 0 else [],
                            incs=[(self.s_t, 1)] if k == src_kc - 1 else [])
                    ummd = cur(self.s_t)
                    o = act_out[:, dc * TLOC + tt * 512:dc * TLOC + tt * 512 + 512]
                    P.do("vector", lambda v, o=o: v.tensor_tensor(
                        out=o, in0=gs[:], in1=psB[:], op=ALU.mult),
                        waits=[ummd, silu_prev] + list(act_war))
                    act_war = ()
                    ttA_prev = cur(self.s_v)
                    ttB_prev = cur(self.s_v)
                    if coefT is not None:
                        P.do("vector", lambda v, o=o: v.tensor_tensor(
                            out=o, in0=o, in1=pcb[:], op=ALU.mult),
                            waits=[pcb_mmd])
                        scale_prev = cur(self.s_v)
                        ttB_prev = scale_prev
        return cur(self.s_v)

    def sub_l1moe(self):
        P = self.P
        w = self.w
        rt32 = self.sb([128, D], F32, "rt")
        c32T = self.sb([128, D], F32, "rtT")
        rsb = self.sb([128, KC * 8], F32, "rw")
        coefT = self.sb([8, TLOC], BF16, "coefT")
        esel = self.sb([8, 1024], BF16, "esel")
        gs = self.sb([128, 512], BF16, "gs")
        act = self.sb([128, 8 * TLOC], BF16, "act")
        wbuf = self.sb([128, 24576], BF16, "wb")
        pr1 = self.sb([128, 8], F32, "pr1")
        pr2 = self.sb([128, 8], F32, "pr2")
        pr3 = self.sb([128, 8], F32, "pr3")
        st1 = self.sb([128, 1], F32, "st1")

        P.do("gpsimd", lambda g: g.memset(esel[:], 0.0))
        P.do("gpsimd", lambda g: g.affine_select(
            out=esel[:].rearrange("j (e p) -> j e p", p=128),
            in_=esel[:].rearrange("j (e p) -> j e p", p=128),
            compare_op=ALU.not_equal, fill=1.0,
            base=0, pattern=[[1, 8], [0, 128]], channel_multiplier=-1),
            waits=[cur(self.s_g)])
        esel_ready = cur(self.s_g)
        P.dma("sync", rsb[:].rearrange("p (k f) -> p k f", f=8),
              w["L1_ffn_router"].rearrange("(k p) f -> p k f", p=128), self.s_ld)
        rw_ready = cur(self.s_ld)

        self.emit_ln_stats()
        with self.subphase():
            self.emit_xlnT(quant=False)
        xv = self.xlnT[:].rearrange("p (k t) -> p k t", t=TLOC)

        with self.subphase():
            pt32 = self.ps([128, D], F32, "pt32")
            plg = self.ps([128, 8], F32, "plg")
            ptc = self.ps([8, 128], F32, "ptc")
            prev_tr = None
            prev_lg = None
            prev_ctc = None
            for i in range(NCH):
                self.emit_ln_chunk(i, rt32[:],
                                   waits=([prev_tr] if prev_tr else
                                          [self.ln_ready]))
                lnr = cur(self.s_v)
                for kc in range(KC):
                    P.do("tensor", lambda t, kc=kc: t.transpose(
                        pt32[:, kc * 128:(kc + 1) * 128],
                        rt32[:, kc * 128:(kc + 1) * 128], self.identf[:]),
                        waits=([lnr, self.ident_ready, cur(self.s_v)] +
                               self.psum_guard) if kc == 0 else [],
                        incs=[(self.s_t, 1)] if kc == KC - 1 else [])
                prev_tr = cur(self.s_t)
                P.do("vector", lambda v: v.tensor_copy(out=c32T[:], in_=pt32[:]),
                     waits=[prev_tr])
                evd = cur(self.s_v)
                wlg = [evd, rw_ready] + ([prev_lg] if prev_lg else [])
                for kc in range(KC):
                    P.do("tensor", lambda t, kc=kc: t.matmul(
                        plg[:], c32T[:, kc * 128:(kc + 1) * 128],
                        rsb[:, kc * 8:(kc + 1) * 8],
                        start=(kc == 0), stop=(kc == KC - 1)),
                        waits=wlg if kc == 0 else [],
                        incs=[(self.s_t, 1)] if kc == KC - 1 else [])
                mmd = cur(self.s_t)
                P.do("vector", lambda v: v.reduce_max(
                    out=st1[:], in_=plg[:], axis=X, negate=True), waits=[mmd])
                P.do("scalar", lambda s: s.activation(
                    out=pr1[:], in_=plg[:], func=AF.Exp, bias=st1[:, 0:1],
                    accum_out=pr3[:, 0:1]), waits=[cur(self.s_v)])
                prev_lg = cur(self.s_s)
                P.do("vector", lambda v: v.reciprocal(
                    out=st1[:], in_=pr3[:, 0:1]), waits=[prev_lg])
                P.do("vector", lambda v: v.tensor_scalar_mul(
                    pr1[:], pr1[:], st1[:, 0:1]))
                P.do("vector", lambda v: v.reduce_max(out=st1[:], in_=pr1[:], axis=X))
                P.do("vector", lambda v: v.tensor_single_scalar(
                    out=pr2[:], in_=pr1[:], scalar=st1[:, 0:1], op=ALU.is_ge))
                P.do("vector", lambda v: v.tensor_tensor(
                    out=pr3[:], in0=pr1[:], in1=pr2[:], op=ALU.mult))
                P.do("vector", lambda v: v.tensor_tensor(
                    out=pr3[:], in0=pr1[:], in1=pr3[:], op=ALU.subtract))
                P.do("vector", lambda v: v.reduce_max(out=st1[:], in_=pr3[:], axis=X))
                P.do("vector", lambda v: v.tensor_single_scalar(
                    out=pr3[:], in_=pr3[:], scalar=st1[:, 0:1], op=ALU.is_ge))
                P.do("vector", lambda v: v.tensor_tensor(
                    out=pr2[:], in0=pr2[:], in1=pr3[:], op=ALU.add))
                P.do("vector", lambda v: v.tensor_tensor(
                    out=pr1[:], in0=pr1[:], in1=pr2[:], op=ALU.mult))
                P.do("vector", lambda v: v.tensor_copy(
                    out=rt32[:, 0:8], in_=pr1[:]),
                    waits=[prev_ctc] if prev_ctc else [])
                cpd = cur(self.s_v)
                P.do("tensor", lambda t: t.transpose(
                    ptc[:], rt32[:, 0:8], self.identf[:]),
                    waits=[cpd] + ([prev_ctc] if prev_ctc else []),
                    incs=[(self.s_t, 1)])
                trc = cur(self.s_t)
                P.do("vector", lambda v, i=i: v.tensor_copy(
                    out=coefT[:, i * 128:(i + 1) * 128], in_=ptc[:]),
                    waits=[trc])
                prev_ctc = cur(self.s_v)
                prev_tr = trc
        coef_ready = cur(self.s_v)

        wsg = wbuf[:, 0:8192]
        wsu = wbuf[:, 8192:16384]
        wsd = wbuf[:, 16384:24576]
        self.prep_direct_into(wsg, w["L1_ffn_shared_gate"], D, D)
        g2 = self.prep_direct_into(wsu, w["L1_ffn_shared_up"], D, D)
        g3 = self.prep_direct_into(wsd, w["L1_ffn_shared_down"], D, D)
        act_ready = self.emit_glu(wsg, wsu, g2, act, 8, D, xv, KC,
                                  self.xlnT_ready, gs=gs)
        with self.subphase():
            self.emit_tokproj_add(act, TLOC, None, D, KC,
                                  [(i, i * 128) for i in range(NCH)],
                                  act_ready, g3,
                                  rhs_view=wsd.rearrange("p (k f) -> p k f", f=D))
        shared_t = cur(self.s_t)
        shared_v = cur(self.s_v)

        we_done = [(shared_t, shared_v), (shared_t, shared_v)]
        wready = [None, None]
        prev_down_t = shared_t
        for e in range(8):
            par = e % 2
            we = wbuf[:, par * 12288:(par + 1) * 12288]
            lw = list(we_done[par])
            self.P.dma("gpsimd",
                       we[:, 0:4096].rearrange("p (k f) -> p k f", f=512),
                       w["L1_ffn_experts_gate"][e].rearrange(
                           "(k p) f -> p k f", p=128), self.s_gp, waits=lw)
            self.P.dma("gpsimd",
                       we[:, 4096:8192].rearrange("p (k f) -> p k f", f=512),
                       w["L1_ffn_experts_up"][e].rearrange(
                           "(k p) f -> p k f", p=128), self.s_gp)
            self.P.dma("gpsimd",
                       we[:, 8192:12288].rearrange("p (k f) -> p k f", f=1024),
                       w["L1_ffn_experts_down"][e].rearrange(
                           "(k p) f -> p k f", p=128), self.s_gp)
            wready[par] = cur(self.s_gp)
            act_ready = self.emit_glu(
                we[:, 0:4096], we[:, 4096:8192], wready[par], act, 4, 512,
                xv, KC, self.xlnT_ready, coefT=coefT, e=e, esel=esel,
                coef_ready=coef_ready, gs=gs, act_war=[prev_down_t])
            with self.subphase():
                self.emit_tokproj_add(
                    act, TLOC, None, D, 4,
                    [(i, i * 128) for i in range(NCH)],
                    act_ready, wready[par],
                    rhs_view=we[:, 8192:12288].rearrange(
                        "p (k f) -> p k f", f=D))
            we_done[par] = (cur(self.s_t), cur(self.s_v))
            prev_down_t = cur(self.s_t)



def shard_x(x):
    xs = x.reshape(2, 16, 512, D)
    out = []
    for c in range(8):
        out.append(np.ascontiguousarray(np.concatenate(
            [xs[0, c], xs[0, 8 + c], xs[1, c], xs[1, 8 + c]], axis=0)))
    return out


def unshard_y(shards):
    y = np.empty((2, 16, 512, D), dtype=shards[0].dtype)
    for c in range(8):
        s = shards[c].reshape(4, 512, D)
        y[0, c] = s[0]
        y[0, 8 + c] = s[1]
        y[1, c] = s[2]
        y[1, 8 + c] = s[3]
    return y.reshape(2, 8192, D)


def build_nc(subs=tuple(range(8))):
    return Build(subs).build()


def make_in_maps(inputs):
    xsh = shard_x(np.ascontiguousarray(inputs["L0_attn_a_x"], dtype=np.float32))
    in_maps = []
    for c in range(8):
        m = {"xs": xsh[c],
             "cconst": np.array([[512.0 * c, float(c), 1e-5, 0, 0, 0, 0, 0]],
                                dtype=np.float32)}
        for k, v in inputs.items():
            if k == "L0_attn_a_x":
                continue
            m[k] = np.ascontiguousarray(v, dtype=np.float32)
        in_maps.append(m)
    return in_maps


def kernel(**inputs):
    nc = build_nc(tuple(range(8)))
    in_maps = make_in_maps(inputs)
    res = run_bass_kernel_spmd(nc, in_maps, core_ids=list(range(8)))
    return unshard_y([r["ys"] for r in res.results]).astype(np.float32)


# version 08 (final)
# speedup vs baseline: 1.3975x; 1.0070x over previous
import numpy as np
from contextlib import ExitStack, contextmanager

import concourse.bass as bass
import concourse.mybir as mybir
from concourse.bass_utils import run_bass_kernel_spmd

F32 = mybir.dt.float32
BF16 = mybir.dt.bfloat16
I16 = mybir.dt.int16
AF = mybir.ActivationFunctionType
ALU = mybir.AluOpType

D = 1024
KC = 8
TLOC = 2048
NCH = 16
NH = 16
HD = 64
QB = 512
MAGIC = 12582912.0
X = mybir.AxisListType.X
ENGINES = ("sync", "scalar", "vector", "tensor", "gpsimd")
DEBUG_ATTN = False
DEBUG_FFN = False
DEBUG_BLOCK = False


class Sem:
    __slots__ = ("h", "n")

    def __init__(self, h):
        self.h = h
        self.n = 0


CHAIN_DEFAULT = {"vector": True, "scalar": True, "gpsimd": True,
                 "sync": False, "tensor": False}


class Prog:
    def __init__(self, nc, ctx):
        self.nc = nc
        self.ctx = ctx
        self.q = {e: [] for e in ENGINES}
        self._sid = 0
        self._chain = {}

    def sem(self, name=None):
        self._sid += 1
        h = self.ctx.enter_context(self.nc.semaphore(name or f"sem{self._sid}"))
        return Sem(h)

    def chain_sem(self, eng):
        if eng not in self._chain:
            self._chain[eng] = self.sem(f"ch_{eng}")
        return self._chain[eng]

    def do(self, eng, fn, waits=(), incs=(), chain=None):
        if chain is None:
            chain = CHAIN_DEFAULT[eng]
        waits = list(waits)
        incs = list(incs)
        if chain:
            cs = self.chain_sem(eng)
            waits.append((cs, cs.n))
            incs.append((cs, 1))
        wl = [(s.h, int(v)) for (s, v) in waits]
        il = []
        for (s, a) in incs:
            s.n += a
            il.append((s.h, a))
        self.q[eng].append((fn, wl, il))

    def dma(self, eng, out, in_, sem, waits=()):
        self.do(eng, (lambda e, o=out, i=in_: e.dma_start(out=o, in_=i)),
                waits=list(waits) + [(sem, sem.n)], incs=[(sem, 16)], chain=False)

    def finalize(self):
        with self.nc.allow_low_precision(reason="bf16 elementwise by design"), \
                self.nc.Block() as block:
            for eng in ENGINES:
                ops = self.q[eng]
                if ops:
                    getattr(block, eng)(lambda e, ops=ops: Prog._run(e, ops))

    @staticmethod
    def _run(e, ops):
        for fn, wl, il in ops:
            byh = {}
            for h, v in wl:
                if byh.get(id(h), (None, -1))[1] < v:
                    byh[id(h)] = (h, v)
            for h, v in byh.values():
                e.wait_ge(h, v)
            inst = fn(e)
            for h, a in il:
                inst = inst.then_inc(h, a, skip_validation=(a == 16))


def cur(sem):
    return (sem, sem.n)



class Build:
    def __init__(self, subs):
        self.subs = list(subs)
        self.nc = bass.Bass("TRN2", target_bir_lowering=False)
        self.ctx = ExitStack()
        self.P = Prog(self.nc, self.ctx)
        self.phase = None
        self.psum_guard = []
        self.uid = 0

    def name(self, pfx):
        self.uid += 1
        return f"{pfx}{self.uid}"

    def _stack(self):
        return self.phase if self.phase is not None else self.ctx

    def sb(self, shape, dtype, pfx="t"):
        return self._stack().enter_context(
            self.nc.sbuf_tensor(self.name(pfx), shape, dtype))

    def ps(self, shape, dtype=F32, pfx="p"):
        return self._stack().enter_context(
            self.nc.psum_tensor(self.name(pfx), shape, dtype))


    def build(self):
        nc = self.nc
        self.xs = nc.dram_tensor("xs", [TLOC, D], F32, kind="ExternalInput").ap()
        self.ys = nc.dram_tensor("ys", [TLOC, D], F32, kind="ExternalOutput").ap()
        self.cconst = nc.dram_tensor("cconst", [1, 8], F32, kind="ExternalInput").ap()
        wnames = [
            ("L0_attn_a_query_proj", [16, 64, 1024]),
            ("L0_attn_a_key_proj", [1024, 64]),
            ("L0_attn_a_value_proj", [1024, 64]),
            ("L0_attn_a_output_proj", [1024, 16, 64]),
            ("L0_attn_b_w_q", [16, 64, 1024]),
            ("L0_attn_b_w_k", [16, 64, 1024]),
            ("L0_attn_b_w_v", [16, 64, 1024]),
            ("L0_attn_b_w_o", [1024, 1024]),
            ("L0_ffn_weight", [1024, 1024]),
            ("L0_ffn_bias", [1024]),
            ("L1_attn_a_Wq", [1024, 16, 64]),
            ("L1_attn_a_Wkv_down", [1024, 256]),
            ("L1_attn_a_Wk_up", [256, 16, 64]),
            ("L1_attn_a_Wv_up", [256, 16, 64]),
            ("L1_attn_a_Wout", [16, 64, 1024]),
            ("L1_attn_b_Q_w", [16, 1024, 64]),
            ("L1_attn_b_Q_b", [16, 64]),
            ("L1_attn_b_K_w", [16, 1024, 64]),
            ("L1_attn_b_K_b", [16, 64]),
            ("L1_attn_b_V_w", [16, 1024, 64]),
            ("L1_attn_b_V_b", [16, 64]),
            ("L1_attn_b_O_w", [16, 64, 1024]),
            ("L1_attn_b_O_b", [1024]),
            ("L1_ffn_router", [1024, 8]),
            ("L1_ffn_experts_gate", [8, 1024, 512]),
            ("L1_ffn_experts_up", [8, 1024, 512]),
            ("L1_ffn_experts_down", [8, 512, 1024]),
            ("L1_ffn_shared_gate", [1024, 1024]),
            ("L1_ffn_shared_up", [1024, 1024]),
            ("L1_ffn_shared_down", [1024, 1024]),
        ]
        self.w = {}
        for n, shp in wnames:
            self.w[n] = nc.dram_tensor(n, shp, F32, kind="ExternalInput").ap()

        self.maskd = nc.dram_tensor("maskd", [128, 32 * QB], mybir.dt.float8e4)

        P = self.P
        self.s_ld = P.sem("s_ld")
        self.s_sc = P.sem("s_sc")
        self.s_gp = P.sem("s_gp")
        self.s_t = P.sem("s_t")
        self.s_v = P.chain_sem("vector")
        self.s_s = P.chain_sem("scalar")
        self.s_g = P.chain_sem("gpsimd")
        self.s_cc = P.sem("s_cc")

        self.r_sb = self.sb([128, NCH * D], F32, "r")
        self.xlnT = self.sb([128, KC * TLOC], BF16, "xlnT")
        self.identb = self.sb([128, 128], BF16, "identb")
        self.identf = self.sb([128, 128], F32, "identf")
        self.ones1b = self.sb([1, 128], BF16, "ones1b")
        self.ccs = self.sb([128, 8], F32, "ccs")
        self.stat_sum = self.sb([128, NCH], F32, "stsum")
        self.stat_sq = self.sb([128, NCH], F32, "stsq")
        self.stat_mean = self.sb([128, NCH], F32, "stmean")
        self.stat_rstd = self.sb([128, NCH], F32, "strstd")
        self.stat_tmp = self.sb([128, NCH], F32, "sttmp")
        self.cq = self.sb([128, D], BF16, "cq")

        self.startup()
        seq = [self.sub_l0a, self.sub_l0ffn, self.sub_l0b, self.sub_l0ffn,
               self.sub_l1a, self.sub_l1moe, self.sub_l1b, self.sub_l1moe]
        for si in self.subs:
            self.emit_barrier()
            with ExitStack() as st:
                self.phase = st
                seq[si]()
            self.phase = None
            self.psum_guard = [cur(self.s_v), cur(self.s_s)]
        self.store_out()
        self.P.finalize()
        self.ctx.close()
        return self.nc

    def emit_barrier(self):
        P = self.P
        waits = [cur(s) for s in (self.s_ld, self.s_sc, self.s_gp, self.s_t,
                                  self.s_cc, self.s_v, self.s_s, self.s_g)]
        for eng in ("tensor", "sync"):
            P.do(eng, lambda e: e.nop(), waits=waits, chain=False)
        for eng in ("vector", "scalar", "gpsimd"):
            P.do(eng, lambda e: e.nop(), waits=waits)


    def startup(self):
        P = self.P
        P.dma("scalar", self.ccs[:], self.cconst.broadcast_to((128, 8)), self.s_sc)
        self.ccs_ready = cur(self.s_sc)
        P.do("gpsimd", lambda g: g.memset(self.identb[:], 0.0))
        P.do("gpsimd", lambda g: g.memset(self.identf[:], 0.0))
        P.do("gpsimd", lambda g: g.memset(self.ones1b[:], 1.0))
        P.do("gpsimd", lambda g: g.affine_select(
            out=self.identb[:], in_=self.identb[:], compare_op=ALU.not_equal,
            fill=1.0, base=0, pattern=[[-1, 128]], channel_multiplier=1))
        P.do("gpsimd", lambda g: g.affine_select(
            out=self.identf[:], in_=self.identf[:], compare_op=ALU.not_equal,
            fill=1.0, base=0, pattern=[[-1, 128]], channel_multiplier=1))
        self.ident_ready = cur(self.s_g)

        xload_waits = []
        if any(s in (2, 4, 6) for s in self.subs):
            mk8c = self.mk8c = self.sb([128, QB], mybir.dt.float8e4, "mk8c")
            P.do("gpsimd", lambda g: g.iota(
                self.r_sb[:, 0:32 * QB].rearrange("p (o q) -> p o q", q=QB),
                [[-128, 32], [1, QB]], base=0, channel_multiplier=-1,
                allow_small_or_imprecise_dtypes=True))
            iota_done = cur(self.s_g)
            for o in range(32):
                wv = [iota_done, self.ccs_ready] if o == 0 else [cur(self.s_gp)]
                P.do("vector", lambda v, o=o: v.tensor_scalar(
                    out=mk8c[:], in0=self.r_sb[:, o * QB:(o + 1) * QB],
                    scalar1=self.ccs[:, 0:1], scalar2=0.0,
                    op0=ALU.add, op1=ALU.is_ge), waits=wv)
                P.dma("gpsimd", self.maskd.ap()[:, o * QB:(o + 1) * QB], mk8c[:],
                      self.s_gp, waits=[cur(self.s_v)])
            self.mask_stored = cur(self.s_gp)
            xload_waits = [cur(self.s_v), self.mask_stored]
        P.dma("sync", self.r_sb[:].rearrange("p (i d) -> p i d", d=D),
              self.xs.rearrange("(i p) d -> p i d", p=128), self.s_ld,
              waits=xload_waits)
        if DEBUG_BLOCK:
            dt_ = self.nc.dram_tensor("dbg_r_postload", [128, NCH * D], F32,
                                      kind="ExternalOutput")
            P.dma("gpsimd", dt_.ap(), self.r_sb[:], self.s_gp,
                  waits=[cur(self.s_ld)])
            snap = cur(self.s_gp)
            for engx in ("vector", "scalar", "tensor", "sync", "gpsimd"):
                P.do(engx, lambda e: e.nop(), waits=[snap], chain=False)

    def store_out(self):
        P = self.P
        P.dma("gpsimd", self.ys.rearrange("(i p) d -> p i d", p=128),
              self.r_sb[:].rearrange("p (i d) -> p i d", d=D), self.s_gp,
              waits=[cur(self.s_ld), cur(self.s_v), cur(self.s_t)])
        P.do("gpsimd", lambda g: g.nop(), waits=[cur(self.s_gp)])


    def emit_ln_stats(self):
        P = self.P
        w0 = [cur(self.s_ld), cur(self.s_v), cur(self.s_t)]
        for i in range(NCH):
            ch = self.r_sb[:, i * D:(i + 1) * D]
            P.do("vector", lambda v, ch=ch, i=i: v.reduce_sum(
                out=self.stat_sum[:, i:i + 1], in_=ch, axis=X),
                waits=w0 if i == 0 else [])
            P.do("vector", lambda v, ch=ch, i=i: v.scalar_tensor_tensor(
                out=self.cq[:], in0=ch, scalar=1.0, in1=ch,
                op0=ALU.mult, op1=ALU.mult, accum_out=self.stat_sq[:, i:i + 1]))
        P.do("vector", lambda v: v.tensor_scalar_mul(
            self.stat_mean[:], self.stat_sum[:], 1.0 / D))
        P.do("vector", lambda v: v.tensor_tensor(
            out=self.stat_tmp[:], in0=self.stat_mean[:], in1=self.stat_mean[:],
            op=ALU.mult))
        P.do("vector", lambda v: v.scalar_tensor_tensor(
            out=self.stat_tmp[:], in0=self.stat_sq[:], scalar=1.0 / D,
            in1=self.stat_tmp[:], op0=ALU.mult, op1=ALU.subtract))
        P.do("scalar", lambda s: s.activation(
            out=self.stat_tmp[:], in_=self.stat_tmp[:], func=AF.Sqrt, bias=self.ccs[:, 2:3]),
            waits=[cur(self.s_v), self.ccs_ready])
        P.do("vector", lambda v: v.reciprocal(
            out=self.stat_rstd[:], in_=self.stat_tmp[:]),
            waits=[cur(self.s_s)])
        self.ln_ready = cur(self.s_v)

    def emit_ln_chunk(self, i, out_ap, waits=(), inc=False):
        P = self.P
        P.do("vector", lambda v, i=i, o=out_ap: v.tensor_scalar(
            out=o, in0=self.r_sb[:, i * D:(i + 1) * D],
            scalar1=self.stat_mean[:, i:i + 1], scalar2=self.stat_rstd[:, i:i + 1],
            op0=ALU.subtract, op1=ALU.mult), waits=list(waits))

    def emit_xlnT(self, quant=False):
        P = self.P
        ptA = self.ps([128, KC * 128], BF16, "ptT")
        ptB = self.ps([128, KC * 128], BF16, "ptT")
        xv = self.xlnT[:].rearrange("p (kc t) -> p kc t", t=TLOC)
        evac_done = [None, None]
        tr_done_prev = None
        for i in range(NCH):
            w = [tr_done_prev] if tr_done_prev is not None else []
            if quant:
                self.emit_ln_chunk(i, self.c32a[:], waits=w)
                self.emit_quant_chunk()
            else:
                self.emit_ln_chunk(i, self.cq[:], waits=w, inc=True)
            src_ready = cur(self.s_v)
            pt = ptA if i % 2 == 0 else ptB
            wa = [src_ready, self.ident_ready] + (self.psum_guard if i < 2 else [])
            if evac_done[i % 2] is not None:
                wa.append(evac_done[i % 2])
            for kc in range(KC):
                P.do("tensor", lambda t, kc=kc, pt=pt: t.transpose(
                    pt[:, kc * 128:(kc + 1) * 128],
                    self.cq[:, kc * 128:(kc + 1) * 128], self.identb[:]),
                    waits=wa if kc == 0 else [],
                    incs=[(self.s_t, 1)] if kc == KC - 1 else [])
            tr_done_prev = cur(self.s_t)
            P.do("vector", lambda v, i=i, pt=pt: v.tensor_copy(
                out=xv[:, :, i * 128:(i + 1) * 128],
                in_=pt[:].rearrange("p (kc t) -> p kc t", t=128)),
                waits=[tr_done_prev])
            evac_done[i % 2] = cur(self.s_v)
        self.xlnT_ready = cur(self.s_v)

    def emit_quant_chunk(self):
        P = self.P
        a, b, c, d = self.c32a, self.c32b, self.c32c, self.c32d
        qs, qr = self.qscale, self.qrec
        a3 = a[:].rearrange("p (blk j) -> p blk j", j=32)
        b3 = b[:].rearrange("p (blk j) -> p blk j", j=32)
        qrb = qr[:].rearrange("p (blk one) -> p blk one", one=1).broadcast_to((128, 32, 32))
        qsb = qs[:].rearrange("p (blk one) -> p blk one", one=1).broadcast_to((128, 32, 32))
        P.do("vector", lambda v: v.reduce_max(
            out=qs[:], in_=a3, axis=X, apply_absolute_value=True))
        P.do("vector", lambda v: v.tensor_scalar(
            out=qs[:], in0=qs[:], scalar1=1.0 / 7.5, scalar2=1e-30,
            op0=ALU.mult, op1=ALU.max))
        P.do("vector", lambda v: v.reciprocal(out=qr[:], in_=qs[:]))
        P.do("vector", lambda v: v.tensor_tensor(out=b3, in0=a3, in1=qrb, op=ALU.mult))
        P.do("vector", lambda v: v.scalar_tensor_tensor(
            out=c[:], in0=b[:], scalar=-1.0, in1=b[:], op0=ALU.mult, op1=ALU.max))
        P.do("vector", lambda v: v.tensor_single_scalar(
            out=d[:], in_=c[:], scalar=2.0, op=ALU.is_ge))
        P.do("vector", lambda v: v.tensor_single_scalar(
            out=c[:], in_=c[:], scalar=4.0, op=ALU.is_ge))
        P.do("vector", lambda v: v.tensor_scalar(
            out=d[:], in0=d[:], scalar1=-4.0, scalar2=8.0, op0=ALU.mult, op1=ALU.add))
        P.do("vector", lambda v: v.scalar_tensor_tensor(
            out=c[:], in0=c[:], scalar=-2.0, in1=d[:], op0=ALU.mult, op1=ALU.add))
        P.do("vector", lambda v: v.tensor_tensor(out=b[:], in0=b[:], in1=c[:], op=ALU.mult))
        P.do("vector", lambda v: v.tensor_scalar(
            out=b[:], in0=b[:], scalar1=MAGIC, scalar2=MAGIC,
            op0=ALU.add, op1=ALU.subtract))
        P.do("vector", lambda v: v.reciprocal(out=c[:], in_=c[:]))
        P.do("vector", lambda v: v.tensor_tensor(out=b[:], in0=b[:], in1=c[:], op=ALU.mult))
        cq3 = self.cq[:].rearrange("p (blk j) -> p blk j", j=32)
        P.do("vector", lambda v: v.tensor_tensor(out=cq3, in0=b3, in1=qsb, op=ALU.mult))

    def alloc_quant_scratch(self):
        self.c32a = self.sb([128, D], F32, "qa")
        self.c32b = self.sb([128, D], F32, "qb")
        self.c32c = self.sb([128, D], F32, "qc")
        self.c32d = self.sb([128, D], F32, "qd")
        self.qscale = self.sb([128, 32], F32, "qsc")
        self.qrec = self.sb([128, 32], F32, "qrc")


    def prep_direct(self, dst, src_hbm, kdim, fdim):
        kcn = kdim // 128
        self.P.dma("gpsimd",
                   dst[:, 0:kcn * fdim].rearrange("p (k f) -> p k f", f=fdim),
                   src_hbm.rearrange("(k p) f -> p k f", p=128), self.s_gp)
        return cur(self.s_gp)

    def prep_transposed(self, dst, src_hbm, fdim, kdim, stage, pst,
                        extra_waits=()):
        P = self.P
        fc = fdim // 128
        kcn = kdim // 128
        ev_done = [None, None]
        tr_prev = None
        dv = dst[:, 0:kcn * fdim].rearrange("p (k f) -> p k f", f=fdim)
        for f in range(fc):
            par = f % 2
            stg = stage[:, 0:kdim]
            wa = ([tr_prev] if tr_prev is not None else []) + \
                (list(extra_waits) if f == 0 else [])
            P.dma("gpsimd", stg,
                  src_hbm.rearrange("(a p) k -> p a k", p=128)[:, f, :],
                  self.s_gp, waits=wa)
            ld = cur(self.s_gp)
            pt = pst[par]
            wt = [ld, self.ident_ready] + \
                ([ev_done[par]] if ev_done[par] is not None else []) + \
                ((list(extra_waits) + self.psum_guard) if f == 0 else [])
            for k in range(kcn):
                P.do("tensor", lambda t, k=k, stg=stg, pt=pt: t.transpose(
                    pt[:, k * 128:(k + 1) * 128], stg[:, k * 128:(k + 1) * 128],
                    self.identb[:]),
                    waits=wt if k == 0 else [],
                    incs=[(self.s_t, 1)] if k == kcn - 1 else [])
            trd = cur(self.s_t)
            tr_prev = trd
            P.do("vector", lambda v, f=f, pt=pt, kcn=kcn: v.tensor_copy(
                out=dv[:, :, f * 128:(f + 1) * 128],
                in_=pt[:, 0:kcn * 128].rearrange("p (k t) -> p k t", t=128)),
                waits=[trd] + (list(extra_waits) if f == 0 else []))
            ev_done[par] = cur(self.s_v)
        return cur(self.s_v)


    def emit_proj_fm(self, warr, outF, out_sb, out_cols, t0, t1, wready,
                     scale=1.0, biascol=None, src=None, src_kc=KC,
                     extra_waits=(), src_ready=None, out_dtype_f32=False):
        P = self.P
        if src is None:
            src = self.xlnT
        if src_ready is None:
            src_ready = self.xlnT_ready
        fcn = outF // 128
        pj = [self.ps([128, 512], F32, "pj"), self.ps([128, 512], F32, "pj")]
        ev = [None, None]
        wv = warr[:].rearrange("p (k f) -> p k f", f=outF)
        sv = src[:].rearrange("p (k t) -> p k t", t=TLOC)
        idx = 0
        for f in range(fcn):
            for tt in range(t0, t1, 512):
                par = idx % 2
                idx += 1
                wa = ([wready, src_ready] + list(extra_waits)) if idx == 1 else []
                if ev[par] is not None:
                    wa.append(ev[par])
                for k in range(src_kc):
                    P.do("tensor", lambda t, k=k, f=f, tt=tt, par=par: t.matmul(
                        pj[par][:], wv[:, k, f * 128:(f + 1) * 128],
                        sv[:, k, tt:tt + 512],
                        start=(k == 0), stop=(k == src_kc - 1)),
                        waits=wa if k == 0 else [],
                        incs=[(self.s_t, 1)] if k == src_kc - 1 else [])
                mmd = cur(self.s_t)
                o = out_sb[:, f * out_cols + tt - t0:f * out_cols + tt - t0 + 512]
                if biascol is not None:
                    P.do("vector", lambda v, o=o, par=par, f=f: v.tensor_scalar(
                        out=o, in0=pj[par][:], scalar1=scale,
                        scalar2=biascol[:, f:f + 1], op0=ALU.mult, op1=ALU.add),
                        waits=[mmd])
                elif scale != 1.0:
                    P.do("vector", lambda v, o=o, par=par: v.tensor_scalar_mul(
                        o, pj[par][:], scale), waits=[mmd])
                else:
                    P.do("vector", lambda v, o=o, par=par: v.tensor_copy(
                        out=o, in_=pj[par][:]), waits=[mmd])
                ev[par] = cur(self.s_v)
        return cur(self.s_v)

    def emit_tokproj_add(self, lhs_sb, lhs_cols, rhs_sb, rhs_cols, kcn, chunks,
                         lready, rready, bias_ready=None, extra_waits=(),
                         lhs_view=None, rhs_view=None):
        P = self.P
        py = [self.ps([128, D], F32, "py"), self.ps([128, D], F32, "py")]
        ev = [None, None]
        lv = lhs_view if lhs_view is not None else \
            lhs_sb[:].rearrange("p (k t) -> p k t", t=lhs_cols)
        rv = rhs_view if rhs_view is not None else \
            rhs_sb[:].rearrange("p (k f) -> p k f", f=rhs_cols)
        br = self.biasrow if bias_ready is not None else None
        for n, (ci, toff) in enumerate(chunks):
            par = n % 2
            if n == 0:
                wa = [lready, rready] + list(extra_waits) + self.psum_guard
                if bias_ready is not None:
                    wa.append(bias_ready)
            else:
                wa = []
            if ev[par] is not None:
                wa.append(ev[par])
            nb = 1 if bias_ready is not None else 0
            for hh in range(2):
                for k in range(kcn):
                    first = (hh == 0 and k == 0)
                    last = (k == kcn - 1 and nb == 0)
                    P.do("tensor",
                         lambda t, k=k, hh=hh, toff=toff, par=par, first=first, last=last:
                         t.matmul(py[par][:, hh * 512:(hh + 1) * 512],
                                  lv[:, k, toff:toff + 128],
                                  rv[:, k, hh * 512:(hh + 1) * 512],
                                  start=(k == 0), stop=last),
                         waits=wa if first else [],
                         incs=[(self.s_t, 1)] if (last and hh == 1) else [])
                if nb:
                    P.do("tensor", lambda t, hh=hh, par=par, br=br: t.matmul(
                        py[par][:, hh * 512:(hh + 1) * 512], self.ones1b[:],
                        br[0:1, hh * 512:(hh + 1) * 512],
                        start=False, stop=True),
                        incs=[(self.s_t, 1)] if hh == 1 else [])
            mmd = cur(self.s_t)
            P.do("vector", lambda v, ci=ci, par=par: v.scalar_tensor_tensor(
                out=self.r_sb[:, ci * D:(ci + 1) * D], in0=py[par][:], scalar=1.0,
                in1=self.r_sb[:, ci * D:(ci + 1) * D], op0=ALU.mult, op1=ALU.add),
                waits=[mmd])
            ev[par] = cur(self.s_v)
        return cur(self.s_v)


    def sub_l0ffn(self):
        P = self.P
        self.biasrow = self.sb([1, D], BF16, "biasrow")
        self.alloc_quant_scratch()
        wstage32 = self.sb([128, KC * D], F32, "w32")
        wq_sb = self.sb([128, KC * D], BF16, "wq")
        wdT = self.sb([128, KC * D], BF16, "wdT")

        P.dma("sync", wstage32[:].rearrange("p (a k) -> p a k", k=D),
              self.w["L0_ffn_weight"].rearrange("(a p) k -> p a k", p=128), self.s_ld)
        wld = cur(self.s_ld)
        c32a_l, cq_l = self.c32a, self.cq
        for a in range(KC):
            P.do("vector", lambda v, a=a, c32a_l=c32a_l: v.tensor_copy(
                out=c32a_l[:], in_=wstage32[:, a * D:(a + 1) * D]),
                waits=[wld] if a == 0 else [])
            self.emit_quant_chunk()
            P.do("vector", lambda v, a=a, cq_l=cq_l: v.tensor_copy(
                out=wq_sb[:, a * D:(a + 1) * D], in_=cq_l[:]))
        wq_done = cur(self.s_v)
        ptw = self.ps([128, KC * 128], BF16, "pw")
        wv = wdT[:].rearrange("p (k j) -> p k j", j=D)
        prev_ev = None
        for a in range(KC):
            wt = [wq_done, self.ident_ready] + ([prev_ev] if prev_ev is not None else [])
            for k in range(KC):
                P.do("tensor", lambda t, a=a, k=k: t.transpose(
                    ptw[:, k * 128:(k + 1) * 128],
                    wq_sb[:, a * D + k * 128:a * D + (k + 1) * 128], self.identb[:]),
                    waits=wt if k == 0 else [],
                    incs=[(self.s_t, 1)] if k == KC - 1 else [])
            trd = cur(self.s_t)
            P.do("vector", lambda v, a=a: v.tensor_copy(
                out=wv[:, :, a * 128:(a + 1) * 128],
                in_=ptw[:].rearrange("p (k j) -> p k j", j=128)),
                waits=[trd])
            prev_ev = cur(self.s_v)
        wdT_ready = cur(self.s_v)
        P.dma("gpsimd", self.biasrow[:],
              self.w["L0_ffn_bias"].rearrange("(one d) -> one d", one=1), self.s_gp)
        bias_ready = cur(self.s_gp)

        if DEBUG_FFN:
            self.uid += 1
            dt_ = self.nc.dram_tensor(f"dbg_wdT{self.uid}", [128, KC * D], BF16,
                                      kind="ExternalOutput")
            P.dma("gpsimd", dt_.ap(), wdT[:], self.s_gp, waits=[wdT_ready])
            dt2_ = self.nc.dram_tensor(f"dbg_bias{self.uid}", [1, D], BF16,
                                       kind="ExternalOutput")
            P.dma("gpsimd", dt2_.ap(), self.biasrow[:], self.s_gp,
                  waits=[bias_ready])
            dt3_ = self.nc.dram_tensor(f"dbg_wq{self.uid}", [128, KC * D], BF16,
                                       kind="ExternalOutput")
            P.dma("gpsimd", dt3_.ap(), wq_sb[:], self.s_gp, waits=[wq_done])
        self.emit_ln_stats()
        self.emit_xlnT(quant=True)
        if DEBUG_BLOCK:
            self.uid += 1
            wdb = [cur(self.s_v), cur(self.s_t), cur(self.s_s), cur(self.s_g),
                   cur(self.s_ld), cur(self.s_sc), cur(self.s_gp)]
            for nmb, buf in (("rpre", self.r_sb), ("xdT", self.xlnT)):
                dt_ = self.nc.dram_tensor(f"dbg_{nmb}{self.uid}", list(buf.shape),
                                          buf[:].dtype, kind="ExternalOutput")
                P.dma("gpsimd", dt_.ap(), buf[:], self.s_gp, waits=wdb)
            snap = cur(self.s_gp)
            for engx in ("vector", "scalar", "tensor", "sync", "gpsimd"):
                P.do(engx, lambda e: e.nop(), waits=[snap], chain=False)
        self.emit_tokproj_add(self.xlnT, TLOC, wdT, D, KC,
                              [(i, i * 128) for i in range(NCH)],
                              self.xlnT_ready, wdT_ready, bias_ready=bias_ready)
        if DEBUG_BLOCK:
            self.uid += 1
            wdb = [cur(self.s_v), cur(self.s_t)]
            dt_ = self.nc.dram_tensor(f"dbg_rpost{self.uid}", [128, NCH * D], F32,
                                      kind="ExternalOutput")
            P.dma("gpsimd", dt_.ap(), self.r_sb[:], self.s_gp, waits=wdb)
            snap = cur(self.s_gp)
            for engx in ("vector", "scalar", "tensor", "sync", "gpsimd"):
                P.do(engx, lambda e: e.nop(), waits=[snap], chain=False)


    @contextmanager
    def subphase(self):
        old = self.phase
        with ExitStack() as st:
            self.phase = st
            yield
        self.phase = old
        self.psum_guard = [cur(self.s_v), cur(self.s_s)]

    def emit_bias_col(self, src_hbm, dst_col, scale=1.0):
        P = self.P
        if getattr(self, "_tmp8", None) is None or self._tmp8_phase is not self.phase:
            self._tmp8 = self.sb([8, 128], F32, "b8")
            self._tmp8_phase = self.phase
            self._tmp8_busy = []
        tmp8 = self._tmp8
        prev = list(self._tmp8_busy)
        P.dma("sync", tmp8[:], src_hbm, self.s_ld, waits=prev)
        ld = cur(self.s_ld)
        with self.subphase():
            pb = self.ps([128, 8], F32, "pb")
            P.do("tensor", lambda t: t.transpose(pb[:], tmp8[:],
                                                 self.identf[0:8, 0:8]),
                 waits=[ld, self.ident_ready] + self.psum_guard,
                 incs=[(self.s_t, 1)])
            trd = cur(self.s_t)
            self._tmp8_busy = [trd]
            P.do("vector", lambda v: v.tensor_scalar_mul(dst_col[:], pb[:], scale),
                 waits=[trd])
        return cur(self.s_v)

    def emit_v_pipeline(self, wv_view, outF, wready, src_view, src_kc, src_ready,
                        vfm, vstg, vshard, vbias_col, vbias_ready, stride):
        P = self.P
        fw = min(outF, 128)
        fcn = (outF + 127) // 128
        pj = [self.ps([128, 512], F32, "pvj"), self.ps([128, 512], F32, "pvj")]
        pv = self.ps([128, NCH * fw], BF16, "pv")
        vsv = vstg[:, 0:NCH * stride].rearrange("p (mc c) -> p mc c", c=stride)
        P.do("gpsimd", lambda g: g.memset(vstg[:, 0:NCH * stride], 0.0))
        P.do("gpsimd", lambda g: g.memset(vsv[:, :, 64:65], 1.0))
        if stride == 130:
            P.do("gpsimd", lambda g: g.memset(vsv[:, :, 129:130], 1.0))
        ones_done = cur(self.s_g)
        prev_tr = None
        prev_dma = None
        prev_evd = None
        ev = [None, None]
        for hc in range(fcn):
            for tt in range(4):
                par = tt % 2
                wa = ([wready, src_ready] + self.psum_guard) \
                    if (hc == 0 and tt == 0) else []
                if ev[par] is not None:
                    wa.append(ev[par])
                for k in range(src_kc):
                    P.do("tensor", lambda t, k=k, hc=hc, tt=tt, par=par: t.matmul(
                        pj[par][0:fw, :], wv_view[:, k, hc * fw:hc * fw + fw],
                        src_view[:, k, tt * 512:(tt + 1) * 512],
                        start=(k == 0), stop=(k == src_kc - 1)),
                        waits=wa if k == 0 else [],
                        incs=[(self.s_t, 1)] if k == src_kc - 1 else [])
                mmd = cur(self.s_t)
                wv2 = [mmd]
                if prev_tr is not None and tt == 0:
                    wv2.append(prev_tr)
                if vbias_col is not None:
                    if vbias_ready is not None:
                        wv2.append(vbias_ready)
                    P.do("vector", lambda v, tt=tt, par=par, hc=hc: v.tensor_scalar(
                        out=vfm[0:fw, tt * 512:(tt + 1) * 512], in0=pj[par][0:fw, :],
                        scalar1=1.0, scalar2=vbias_col[0:fw, hc:hc + 1],
                        op0=ALU.mult, op1=ALU.add), waits=wv2)
                else:
                    P.do("vector", lambda v, tt=tt, par=par: v.tensor_copy(
                        out=vfm[0:fw, tt * 512:(tt + 1) * 512], in_=pj[par][0:fw, :]),
                        waits=wv2)
                ev[par] = cur(self.s_v)
            vfm_done = cur(self.s_v)
            for mc in range(NCH):
                wt = [vfm_done, self.ident_ready]
                if mc == 0 and prev_evd is not None:
                    wt.append(prev_evd)
                P.do("tensor", lambda t, mc=mc: t.transpose(
                    pv[:, mc * fw:(mc + 1) * fw],
                    vfm[0:fw, mc * 128:(mc + 1) * 128],
                    self.identb[0:fw, 0:fw]),
                    waits=wt if mc == 0 else [],
                    incs=[(self.s_t, 1)] if mc == NCH - 1 else [])
            prev_tr = cur(self.s_t)
            wv3 = [prev_tr, ones_done]
            if prev_dma is not None:
                wv3.append(prev_dma)
            pvv = pv[:].rearrange("p (mc c) -> p mc c", c=fw)
            P.do("vector", lambda v: v.tensor_copy(
                out=vsv[:, :, 0:64], in_=pvv[:, :, 0:64]), waits=wv3)
            if stride == 130:
                P.do("vector", lambda v: v.tensor_copy(
                    out=vsv[:, :, 65:129], in_=pvv[:, :, 64:128]))
            prev_evd = cur(self.s_v)
            P.dma("scalar", vshard.rearrange("p (hc c) -> p hc c",
                                             c=NCH * stride)[:, hc, :],
                  vstg[:, 0:NCH * stride], self.s_sc, waits=[prev_evd])
            prev_dma = cur(self.s_sc)
        return prev_dma, prev_tr

    def emit_attention(self, nm, causal, shared_kv, kdim_rows,
                       prep_q, prep_k, prep_v, prep_o,
                       q_bias=None, k_bias=None, v_bias=None, o_bias=None,
                       latent=None):
        P = self.P
        nc = self.nc
        stride = 65 if shared_kv else 130
        vsh_cols = NCH * stride * (1 if shared_kv else 8)
        kshard = nc.dram_tensor(f"ksh_{nm}", [kdim_rows, TLOC], BF16)
        vshard = nc.dram_tensor(f"vsh_{nm}", [128, vsh_cols], BF16)
        kg = nc.dram_tensor(f"kg_{nm}", [8 * kdim_rows, TLOC], BF16)
        vg = nc.dram_tensor(f"vg_{nm}", [8 * 128, vsh_cols], BF16)

        q_sb = self.sb([128, KC * TLOC], BF16, "q")
        warr = self.sb([128, KC * D], BF16, "wa")
        wstage = self.sb([128, D], BF16, "ws")
        k_tile = self.sb([128, 8192], BF16, "kt")
        vt = self.sb([128, 64 * stride], BF16, "vt")
        pst = self.sb([128, 2048], BF16, "P")
        rec_sb = self.sb([1, 512], BF16, "rec")
        rb_sb = self.sb([128, 512], BF16, "rb")
        if causal:
            mask8 = self.sb([128, 32 * QB], mybir.dt.float8e4, "mk8")
        qcol = self.sb([128, 8], F32, "qc8")
        kcol = self.sb([128, 8], F32, "kc8")
        vcol = self.sb([128, 8], F32, "vc8")
        vfm = k_tile[:, 0:2048]
        vstg = k_tile[:, 2048:2048 + NCH * stride]

        if DEBUG_BLOCK:
            wdb = [cur(self.s_v), cur(self.s_t), cur(self.s_s), cur(self.s_g),
                   cur(self.s_ld), cur(self.s_sc), cur(self.s_gp)]
            dt_ = nc.dram_tensor(f"dbg_r_start_{nm}", [128, NCH * D], F32,
                                 kind="ExternalOutput")
            P.dma("gpsimd", dt_.ap(), self.r_sb[:], self.s_gp, waits=wdb)
            snap = cur(self.s_gp)
            for engx in ("vector", "scalar", "tensor", "sync", "gpsimd"):
                P.do(engx, lambda e: e.nop(), waits=[snap], chain=False)
        self.emit_ln_stats()
        with self.subphase():
            self.emit_xlnT(quant=False)
        xv = self.xlnT[:].rearrange("p (k t) -> p k t", t=TLOC)

        qb_ready = kb_ready = vb_ready = None
        if q_bias is not None:
            qb_ready = self.emit_bias_col(q_bias, qcol, scale=0.125)
        if k_bias is not None:
            kb_ready = self.emit_bias_col(k_bias, kcol)
        if v_bias is not None:
            vb_ready = self.emit_bias_col(v_bias, vcol)
        if DEBUG_ATTN and q_bias is not None:
            dt_ = nc.dram_tensor(f"dbg_qcol_early_{nm}", [128, 8], F32,
                                 kind="ExternalOutput")
            P.dma("gpsimd", dt_.ap(), qcol[:], self.s_gp, waits=[qb_ready])
            dt2_ = nc.dram_tensor(f"dbg_r_postb_{nm}", [128, NCH * D], F32,
                                  kind="ExternalOutput")
            P.dma("gpsimd", dt2_.ap(), self.r_sb[:], self.s_gp,
                  waits=[qb_ready, vb_ready])

        if causal:
            P.dma("scalar", mask8[:], self.maskd.ap(), self.s_sc,
                  waits=[cur(self.s_gp)])
            mask_ready = cur(self.s_sc)

        if latent is not None:
            latT = vt[:, 0:2 * TLOC]
            lat_view = latT.rearrange("p (k t) -> p k t", t=TLOC)
            wkvd = warr[:, 0:2048]
            wku = warr[:, 2048:4096]
            wvu = warr[:, 4096:6144]
            kvd_ready = latent["prep_down"](wkvd)
            kup_ready = latent["prep_kup"](wku)
            vup_ready = latent["prep_vup"](wvu)
            with self.subphase():
                lat_ready = self.emit_proj_fm_view(
                    wkvd.rearrange("p (k f) -> p k f", f=256), 256,
                    latT, TLOC, kvd_ready, xv, KC, self.xlnT_ready)
            ksrc_view, ksrc_kc, ksrc_ready = lat_view, 2, lat_ready
            kw_view = wku.rearrange("p (k f) -> p k f", f=D)
            vw_view = wvu.rearrange("p (k f) -> p k f", f=D)
            kw_ready, vw_ready = kup_ready, vup_ready
        else:
            kw_ready = prep_k(warr)
            ksrc_view, ksrc_kc, ksrc_ready = xv, KC, self.xlnT_ready
            kw_view = warr[:, 0:KC * self.kv_outF].rearrange(
                "p (k f) -> p k f", f=self.kv_outF)

        koutF = 64 if shared_kv else D
        with self.subphase():
            kfm_ready = self.emit_proj_fm_view(
                kw_view, koutF, q_sb, TLOC, kw_ready, ksrc_view, ksrc_kc,
                ksrc_ready, biascol=kcol if k_bias is not None else None,
                bias_ready=kb_ready)
        if shared_kv:
            P.dma("scalar", kshard.ap(), q_sb[0:64, 0:TLOC], self.s_sc,
                  waits=[kfm_ready])
        else:
            P.dma("scalar", kshard.ap().rearrange("(hc p) t -> p hc t", p=128),
                  q_sb[:].rearrange("p (hc t) -> p hc t", t=TLOC), self.s_sc,
                  waits=[kfm_ready])
        kshard_dma = cur(self.s_sc)

        if latent is not None:
            vsrc_view, vsrc_kc, vsrc_ready = lat_view, 2, lat_ready
        else:
            vw_ready = prep_v(warr)
            vw_view = warr[:, 0:KC * self.kv_outF].rearrange(
                "p (k f) -> p k f", f=self.kv_outF)
            vsrc_view, vsrc_kc, vsrc_ready = xv, KC, self.xlnT_ready
        with self.subphase():
            vshard_dma, vtr_done = self.emit_v_pipeline(
                vw_view, koutF, vw_ready, vsrc_view, vsrc_kc, vsrc_ready,
                vfm, vstg, vshard.ap(), vcol if v_bias is not None else None,
                vb_ready, stride)

        P.do("gpsimd", lambda g: g.collective_compute(
            "AllGather", ALU.bypass, replica_groups=[list(range(8))],
            ins=[kshard.ap().opt()], outs=[kg.ap().opt()]),
            waits=[kshard_dma], incs=[(self.s_cc, 1)], chain=False)
        P.do("gpsimd", lambda g: g.collective_compute(
            "AllGather", ALU.bypass, replica_groups=[list(range(8))],
            ins=[vshard.ap().opt()], outs=[vg.ap().opt()]),
            waits=[vshard_dma], incs=[(self.s_cc, 1)], chain=False)
        gather_done = cur(self.s_cc)

        qw_ready = prep_q(warr, extra_waits=[kfm_ready, vtr_done, cur(self.s_v)])
        with self.subphase():
            q_ready = self.emit_proj_fm_view(
                warr[:].rearrange("p (k f) -> p k f", f=D), D, q_sb, TLOC,
                qw_ready, xv, KC, self.xlnT_ready, scale=0.125,
                biascol=qcol if q_bias is not None else None, bias_ready=qb_ready,
                evac_extra=[kshard_dma])
        qv = q_sb[:].rearrange("p (oc t) -> p oc t", t=TLOC)
        if DEBUG_ATTN:
            wdb = [q_ready, cur(self.s_t)]
            for nmb, buf in (("q0", q_sb), ("wq", warr), ("xlt", self.xlnT),
                             ("qcol", qcol), ("resid", self.r_sb)):
                dt_ = nc.dram_tensor(f"dbg_{nmb}_{nm}", list(buf.shape),
                                     buf[:].dtype, kind="ExternalOutput")
                P.dma("gpsimd", dt_.ap(), buf[:], self.s_gp, waits=wdb)
            P.do("gpsimd", lambda g: g.nop(), waits=[cur(self.s_gp)])

        ow_ready = prep_o(warr, extra_waits=[q_ready])
        if o_bias is not None:
            self.biasrow = self.sb([1, D], BF16, "biasrow")
            P.dma("gpsimd", self.biasrow[:], o_bias, self.s_gp)
            ob_ready = cur(self.s_gp)
        else:
            ob_ready = None

        with self.subphase():
            ps_s = [self.ps([128, 1024], F32, "pss"), self.ps([128, 1024], F32, "pss")]
            ps_o = [self.ps([65, 512], F32, "pso"), self.ps([65, 512], F32, "pso")]
            recB = self.ps([128, 512], F32, "prb")
            ktv = k_tile
            vtv = vt[:].rearrange("p (mc c) -> p mc c", c=stride)
            exp0_done = [None, None]
            exp1_done = [None, None]
            av_done = [None, None]
            pmask = [None, None]
            opsum_free = None
            kv_free = None
            kv_free0 = None
            kv_half = [None, None]
            fresh_load = False
            for b in range(2):
                for pair in range(8):
                    xw = NCH // 4 * stride
                    fresh_load = (not shared_kv) or pair == 0
                    if shared_kv:
                        if pair == 0:
                            src = kg.ap().rearrange(
                                "(r d) (l o) -> d l r o", d=64, o=512)
                            vsrc = vg.ap().rearrange(
                                "(r p) (l x) -> p l r x", p=128, x=xw)
                            for kk in range(2):
                                if kk == 0:
                                    lw = [kv_free0] if kv_free0 is not None \
                                        else [gather_done, vshard_dma]
                                else:
                                    lw = [kv_free] if kv_free is not None else []
                                P.dma("sync", ktv[0:64, :].rearrange(
                                    "d (kk r o) -> d kk r o", kk=2, r=8)[:, kk],
                                    src[:, 2 * b + kk, :, :], self.s_ld,
                                    waits=lw)
                                P.dma("sync", ktv[64:128, :].rearrange(
                                    "d (kk r o) -> d kk r o", kk=2, r=8)[:, kk],
                                    src[:, 2 * b + kk, :, :], self.s_ld)
                                P.dma("sync", vt[:].rearrange(
                                    "p (kk r x) -> p kk r x", kk=2, r=8)[:, kk],
                                    vsrc[:, 2 * b + kk, :, :], self.s_ld)
                                kv_half[kk] = cur(self.s_ld)
                    else:
                        src = kg.ap().rearrange(
                            "(r hc p) (l o) -> p hc l r o", r=8, p=128, o=512)
                        vsrc = vg.ap().rearrange(
                            "(r p) (pr l x) -> p pr l r x", p=128, pr=8, x=xw)
                        for kk in range(2):
                            if kk == 0:
                                lw = [kv_free0] if kv_free0 is not None \
                                    else [gather_done, vshard_dma]
                            else:
                                lw = [kv_free] if kv_free is not None else []
                            P.dma("sync", ktv[:].rearrange(
                                "p (kk r o) -> p kk r o", kk=2, r=8)[:, kk],
                                src[:, pair, 2 * b + kk, :, :], self.s_ld,
                                waits=lw)
                            P.dma("sync", vt[:].rearrange(
                                "p (kk r x) -> p kk r x", kk=2, r=8)[:, kk],
                                vsrc[:, pair, 2 * b + kk, :, :], self.s_ld)
                            kv_half[kk] = cur(self.s_ld)
                    for l in range(2):
                        M = (32 * (l + 1)) if causal else 64
                        qoff = b * 1024 + l * 512
                        for j in range(M + 1):
                            par = j % 2
                            if j < M:
                                wa = []
                                if exp1_done[par] is not None:
                                    wa.append(exp1_done[par])
                                if b == 0 and pair == 0 and l == 0 and j == 0:
                                    wa += [ow_ready, q_ready]
                                    wa += self.psum_guard
                                if j == 0 and l == 0 and fresh_load:
                                    wa.append(kv_half[0])
                                if j == 32 and fresh_load and \
                                        (l == (1 if causal else 0)):
                                    wa.append(kv_half[1])
                                for hh in range(2):
                                    P.do("tensor",
                                         lambda t, j=j, hh=hh, pair=pair, par=par, qoff=qoff:
                                         t.matmul(
                                             ps_s[par][:, hh * 512:(hh + 1) * 512],
                                             ktv[hh * 64:hh * 64 + 64,
                                                 j * 128:(j + 1) * 128],
                                             qv[hh * 64:hh * 64 + 64, pair,
                                                qoff:qoff + 512],
                                             start=True, stop=True),
                                         waits=wa if hh == 0 else [],
                                         incs=[(self.s_t, 1)] if hh == 1 else [])
                                sc_done = cur(self.s_t)
                                P.do("scalar", lambda s, par=par: s.activation(
                                    out=pst[:, par * 1024:(par + 1) * 1024],
                                    in_=ps_s[par][:], func=AF.Exp),
                                    waits=[sc_done])
                                exp0_done[par] = cur(self.s_s)
                                exp1_done[par] = exp0_done[par]
                                masked = causal and j >= M - 32
                                if masked:
                                    o = j - (M - 32)
                                    mk = mask8[:, o * 512:(o + 1) * 512]
                                    mka = mask8[:, o * 512:o * 512 + 256]
                                    mkb = mask8[:, o * 512 + 256:(o + 1) * 512]
                                    P.do("vector", lambda v, par=par, mk=mk:
                                         v.tensor_tensor(
                                             out=pst[:, par * 1024:par * 1024 + 512],
                                             in0=pst[:, par * 1024:par * 1024 + 512],
                                             in1=mk, op=ALU.mult),
                                         waits=[exp0_done[par], mask_ready])
                                    pm0 = cur(self.s_v)
                                    P.do("vector", lambda v, par=par, mka=mka:
                                         v.tensor_tensor(
                                             out=pst[:, par * 1024 + 512:
                                                     par * 1024 + 768],
                                             in0=pst[:, par * 1024 + 512:
                                                     par * 1024 + 768],
                                             in1=mka, op=ALU.mult))
                                    pm1v = cur(self.s_v)
                                    P.do("gpsimd", lambda g, par=par, mkb=mkb:
                                         g.tensor_tensor(
                                             out=pst[:, par * 1024 + 768:
                                                     par * 1024 + 1024],
                                             in0=pst[:, par * 1024 + 768:
                                                     par * 1024 + 1024],
                                             in1=mkb, op=ALU.mult),
                                         waits=[exp1_done[par], mask_ready])
                                    pmask[par] = (pm0, pm1v, cur(self.s_g))
                                else:
                                    pmask[par] = None
                            if j >= 1:
                                jj = j - 1
                                pj_ = jj % 2
                                for hh in range(2):
                                    if pmask[pj_] is not None:
                                        wa2 = [pmask[pj_][0]] if hh == 0 else \
                                            [pmask[pj_][1], pmask[pj_][2]]
                                    elif hh == 0:
                                        wa2 = [exp0_done[pj_]]
                                    else:
                                        wa2 = []
                                    if hh == 0 and jj == 0 and \
                                            opsum_free is not None:
                                        wa2.append(opsum_free)
                                    voff = jj * stride + \
                                        (hh * 65 if not shared_kv else 0)
                                    P.do("tensor",
                                         lambda t, jj=jj, hh=hh, pj_=pj_, voff=voff:
                                         t.matmul(
                                             ps_o[hh][:],
                                             vt[:, voff:voff + 65],
                                             pst[:, pj_ * 1024 + hh * 512:
                                                 pj_ * 1024 + (hh + 1) * 512],
                                             start=(jj == 0), stop=(jj == M - 1)),
                                         waits=wa2,
                                         incs=[(self.s_t, 1)] if hh == 1 else [])
                                av_done[pj_] = cur(self.s_t)
                                if l == 1 and jj == 31:
                                    kv_free0 = cur(self.s_t)
                        avf = cur(self.s_t)
                        for hh in range(2):
                            P.do("vector", lambda v, hh=hh: v.reciprocal(
                                out=rec_sb[0:1, :], in_=ps_o[hh][64:65, :]),
                                waits=[avf] if hh == 0 else [cur(self.s_t)])
                            rd = cur(self.s_v)
                            P.do("tensor", lambda t: t.matmul(
                                recB[:], self.ones1b[:], rec_sb[0:1, :],
                                start=True, stop=True),
                                waits=[rd], incs=[(self.s_t, 1)])
                            bc = cur(self.s_t)
                            P.do("vector", lambda v: v.tensor_copy(
                                out=rb_sb[:], in_=recB[:]), waits=[bc])
                            oco = b * 8192 + pair * 1024 + l * 512
                            P.do("vector", lambda v, hh=hh, oco=oco:
                                 v.tensor_tensor(
                                     out=self.xlnT[hh * 64:(hh + 1) * 64,
                                                   oco:oco + 512],
                                     in0=ps_o[hh][0:64, :], in1=rb_sb[0:64, :],
                                     op=ALU.mult))
                        opsum_free = cur(self.s_v)
                    kv_free = cur(self.s_t)
            ocat_ready = cur(self.s_v)
        if DEBUG_ATTN:
            wdb = [cur(self.s_v), cur(self.s_t), cur(self.s_s)]
            for nmb, buf in (("ocat", self.xlnT), ("qsb", q_sb), ("ktl", k_tile),
                             ("vtl", vt), ("pstd", pst)):
                dt_ = nc.dram_tensor(f"dbg_{nmb}_{nm}", list(buf.shape),
                                     buf[:].dtype, kind="ExternalOutput")
                P.dma("gpsimd", dt_.ap(), buf[:], self.s_gp, waits=wdb)
            if causal:
                dtm = nc.dram_tensor(f"dbg_mask_{nm}", list(mask8.shape),
                                     mybir.dt.float8e4, kind="ExternalOutput")
                P.dma("gpsimd", dtm.ap(), mask8[:], self.s_gp, waits=wdb)
            P.do("gpsimd", lambda g: g.nop(), waits=[cur(self.s_gp)])
        for b in range(2):
            with self.subphase():
                self.emit_tokproj_add(
                    None, 1024, None, D, KC,
                    [(b * 8 + tc, tc * 128) for tc in range(8)],
                    ocat_ready, ow_ready, bias_ready=ob_ready,
                    lhs_view=self.xlnT[:, b * 8192:(b + 1) * 8192]
                        .rearrange("p (k t) -> p k t", t=1024),
                    rhs_view=warr[:].rearrange("p (k f) -> p k f", f=D))

    def emit_proj_fm_view(self, wv, outF, out_sb, out_cols, wready, src_view,
                          src_kc, src_ready, scale=1.0, biascol=None,
                          bias_ready=None, evac_extra=()):
        P = self.P
        fw = min(outF, 128)
        fcn = (outF + 127) // 128
        pj = [self.ps([128, 512], F32, "pj"), self.ps([128, 512], F32, "pj")]
        ev = [None, None]
        idx = 0
        for f in range(fcn):
            for tt in range(0, TLOC, 512):
                par = idx % 2
                idx += 1
                wa = ([wready, src_ready] + self.psum_guard) if idx == 1 else []
                if ev[par] is not None:
                    wa.append(ev[par])
                for k in range(src_kc):
                    P.do("tensor", lambda t, k=k, f=f, tt=tt, par=par: t.matmul(
                        pj[par][0:fw, :], wv[:, k, f * fw:f * fw + fw],
                        src_view[:, k, tt:tt + 512],
                        start=(k == 0), stop=(k == src_kc - 1)),
                        waits=wa if k == 0 else [],
                        incs=[(self.s_t, 1)] if k == src_kc - 1 else [])
                mmd = cur(self.s_t)
                o = out_sb[0:fw, f * out_cols + tt:f * out_cols + tt + 512]
                wv2 = [mmd] + list(evac_extra)
                if biascol is not None:
                    if bias_ready is not None:
                        wv2.append(bias_ready)
                    P.do("vector", lambda v, o=o, par=par, f=f: v.tensor_scalar(
                        out=o, in0=pj[par][0:fw, :], scalar1=scale,
                        scalar2=biascol[0:fw, f:f + 1], op0=ALU.mult, op1=ALU.add),
                        waits=wv2)
                elif scale != 1.0:
                    P.do("vector", lambda v, o=o, par=par: v.tensor_scalar_mul(
                        o, pj[par][0:fw, :], scale), waits=wv2)
                else:
                    P.do("vector", lambda v, o=o, par=par: v.tensor_copy(
                        out=o, in_=pj[par][0:fw, :]), waits=wv2)
                ev[par] = cur(self.s_v)
        return cur(self.s_v)


    def sub_l0a(self):
        self.kv_outF = 64
        w = self.w

        def prep_q(warr, extra_waits=()):
            ws = self.sb([128, 2 * D], BF16, "wsq") if False else None
            with self.subphase():
                wst = self.sb([128, D], BF16, "wst")
                pstA = self.ps([128, KC * 128], BF16, "pt")
                pstB = self.ps([128, KC * 128], BF16, "pt")
                return self.prep_transposed(
                    warr, w["L0_attn_a_query_proj"].rearrange("h k d -> (h k) d"),
                    D, D, wst, [pstA, pstB], extra_waits)

        def prep_k(warr):
            return self.prep_direct_into(warr[:, 0:KC * 64],
                                         w["L0_attn_a_key_proj"], D, 64)

        def prep_v(warr):
            return self.prep_direct_into(warr[:, 0:KC * 64],
                                         w["L0_attn_a_value_proj"], D, 64,
                                         waits=[cur(self.s_t)])

        def prep_o(warr, extra_waits=()):
            with self.subphase():
                wst = self.sb([128, D], BF16, "wst")
                pstA = self.ps([128, KC * 128], BF16, "pt")
                pstB = self.ps([128, KC * 128], BF16, "pt")
                return self.prep_transposed(
                    warr, w["L0_attn_a_output_proj"].rearrange("d h v -> d (h v)"),
                    D, D, wst, [pstA, pstB], extra_waits)

        self.emit_attention("l0a", causal=False, shared_kv=True, kdim_rows=64,
                            prep_q=prep_q, prep_k=prep_k, prep_v=prep_v,
                            prep_o=prep_o)

    def sub_l0b(self):
        self.kv_outF = D
        w = self.w

        def mk_prep(key, extra=False):
            def prep(warr, extra_waits=()):
                with self.subphase():
                    wst = self.sb([128, D], BF16, "wst")
                    pstA = self.ps([128, KC * 128], BF16, "pt")
                    pstB = self.ps([128, KC * 128], BF16, "pt")
                    src = w[key]
                    if key == "L0_attn_b_w_o":
                        src2 = src
                    else:
                        src2 = src.rearrange("h k d -> (h k) d")
                    return self.prep_transposed(warr, src2, D, D, wst,
                                                [pstA, pstB], extra_waits)
            return prep

        def prep_v(warr):
            return mk_prep("L0_attn_b_w_v")(warr, extra_waits=[cur(self.s_t)])

        self.emit_attention("l0b", causal=True, shared_kv=False, kdim_rows=D,
                            prep_q=mk_prep("L0_attn_b_w_q"),
                            prep_k=mk_prep("L0_attn_b_w_k"),
                            prep_v=prep_v,
                            prep_o=mk_prep("L0_attn_b_w_o"))

    def sub_l1a(self):
        self.kv_outF = D
        w = self.w

        def prep_down(wkvd):
            return self.prep_direct_into(wkvd, w["L1_attn_a_Wkv_down"], D, 256)

        def prep_kup(wku):
            return self.prep_direct_into(
                wku, w["L1_attn_a_Wk_up"].rearrange("c h k -> c (h k)"), 256, D)

        def prep_vup(wvu):
            return self.prep_direct_into(
                wvu, w["L1_attn_a_Wv_up"].rearrange("c h k -> c (h k)"), 256, D)

        def prep_q(warr, extra_waits=()):
            return self.prep_direct_into(
                warr, w["L1_attn_a_Wq"].rearrange("d h k -> d (h k)"), D, D,
                waits=list(extra_waits))

        def prep_o(warr, extra_waits=()):
            return self.prep_direct_into(
                warr, w["L1_attn_a_Wout"].rearrange("h v d -> (h v) d"), D, D,
                waits=list(extra_waits))

        self.emit_attention("l1a", causal=True, shared_kv=False, kdim_rows=D,
                            prep_q=prep_q, prep_k=None, prep_v=None,
                            prep_o=prep_o,
                            latent=dict(prep_down=prep_down, prep_kup=prep_kup,
                                        prep_vup=prep_vup))

    def sub_l1b(self):
        self.kv_outF = D
        w = self.w

        def mk_prep(key):
            def prep(warr, extra_waits=()):
                sems = list(extra_waits)
                for h in range(NH):
                    self.P.dma(
                        "gpsimd",
                        warr[:].rearrange("p (k f) -> p k f", f=D)
                        [:, :, h * HD:(h + 1) * HD],
                        w[key].rearrange("h (k p) f -> h p k f", p=128)[h],
                        self.s_gp, waits=sems if h == 0 else ())
                return cur(self.s_gp)
            return prep

        def prep_v(warr):
            return mk_prep("L1_attn_b_V_w")(warr, extra_waits=[cur(self.s_t)])

        def prep_o(warr, extra_waits=()):
            return self.prep_direct_into(
                warr, w["L1_attn_b_O_w"].rearrange("h v d -> (h v) d"), D, D,
                waits=list(extra_waits))

        self.emit_attention(
            "l1b", causal=True, shared_kv=False, kdim_rows=D,
            prep_q=mk_prep("L1_attn_b_Q_w"),
            prep_k=mk_prep("L1_attn_b_K_w"),
            prep_v=prep_v,
            prep_o=prep_o,
            q_bias=w["L1_attn_b_Q_b"].rearrange("(a hh) k -> a (hh k)", a=8),
            k_bias=w["L1_attn_b_K_b"].rearrange("(a hh) k -> a (hh k)", a=8),
            v_bias=w["L1_attn_b_V_b"].rearrange("(a hh) k -> a (hh k)", a=8),
            o_bias=w["L1_attn_b_O_b"].rearrange("(one d) -> one d", one=1))

    def prep_direct_into(self, dst, src_hbm, kdim, fdim, waits=()):
        kcn = kdim // 128
        self.P.dma("gpsimd",
                   dst[:, 0:kcn * fdim].rearrange("p (k f) -> p k f", f=fdim),
                   src_hbm.rearrange("(k p) f -> p k f", p=128), self.s_gp,
                   waits=list(waits))
        return cur(self.s_gp)

    def emit_glu(self, wg, wu, wready, act_out, dcn, fdim, src_view, src_kc,
                 src_ready, coefT=None, e=None, esel=None, coef_ready=None,
                 gs=None, act_war=()):
        P = self.P
        with self.subphase():
            psA = self.ps([128, 512], F32, "pga")
            psB = self.ps([128, 512], F32, "pgb")
            pcb = self.ps([128, 512], F32, "pcb") if coefT is not None else None
            wgv = wg[:, 0:src_kc * fdim].rearrange("p (k f) -> p k f", f=fdim)
            wuv = wu[:, 0:src_kc * fdim].rearrange("p (k f) -> p k f", f=fdim)
            silu_prev = None
            ttA_prev = None
            ttB_prev = None
            scale_prev = None
            first = True
            for tt in range(4):
                if coefT is not None:
                    wv = [coef_ready]
                    if scale_prev is not None:
                        wv.append(scale_prev)
                    if first:
                        wv += self.psum_guard
                    P.do("tensor", lambda t, tt=tt: t.matmul(
                        pcb[:], esel[:, e * 128:(e + 1) * 128],
                        coefT[:, tt * 512:(tt + 1) * 512],
                        start=True, stop=True),
                        waits=wv, incs=[(self.s_t, 1)])
                    pcb_mmd = cur(self.s_t)
                for dc in range(dcn):
                    wa = [wready, src_ready]
                    if silu_prev is not None:
                        wa.append(silu_prev)
                    if first:
                        wa += self.psum_guard
                        first = False
                    for k in range(src_kc):
                        P.do("tensor", lambda t, k=k, dc=dc, tt=tt: t.matmul(
                            psA[:], wgv[:, k, dc * 128:(dc + 1) * 128],
                            src_view[:, k, tt * 512:(tt + 1) * 512],
                            start=(k == 0), stop=(k == src_kc - 1)),
                            waits=wa if k == 0 else [],
                            incs=[(self.s_t, 1)] if k == src_kc - 1 else [])
                    gmmd = cur(self.s_t)
                    P.do("scalar", lambda s: s.activation(
                        out=gs[:], in_=psA[:], func=AF.Silu),
                        waits=[gmmd] + ([ttA_prev] if ttA_prev else []))
                    silu_prev = cur(self.s_s)
                    wb = [ttB_prev] if ttB_prev is not None else []
                    for k in range(src_kc):
                        P.do("tensor", lambda t, k=k, dc=dc, tt=tt: t.matmul(
                            psB[:], wuv[:, k, dc * 128:(dc + 1) * 128],
                            src_view[:, k, tt * 512:(tt + 1) * 512],
                            start=(k == 0), stop=(k == src_kc - 1)),
                            waits=wb if k == 0 else [],
                            incs=[(self.s_t, 1)] if k == src_kc - 1 else [])
                    ummd = cur(self.s_t)
                    o = act_out[:, dc * TLOC + tt * 512:dc * TLOC + tt * 512 + 512]
                    P.do("vector", lambda v, o=o: v.tensor_tensor(
                        out=o, in0=gs[:], in1=psB[:], op=ALU.mult),
                        waits=[ummd, silu_prev] + list(act_war))
                    act_war = ()
                    ttA_prev = cur(self.s_v)
                    ttB_prev = cur(self.s_v)
                    if coefT is not None:
                        P.do("vector", lambda v, o=o: v.tensor_tensor(
                            out=o, in0=o, in1=pcb[:], op=ALU.mult),
                            waits=[pcb_mmd])
                        scale_prev = cur(self.s_v)
                        ttB_prev = scale_prev
        return cur(self.s_v)

    def sub_l1moe(self):
        P = self.P
        w = self.w
        rt32 = self.sb([128, D], F32, "rt")
        c32T = self.sb([128, D], F32, "rtT")
        rsb = self.sb([128, KC * 8], F32, "rw")
        coefT = self.sb([8, TLOC], BF16, "coefT")
        esel = self.sb([8, 1024], BF16, "esel")
        gs = self.sb([128, 512], BF16, "gs")
        act = self.sb([128, 8 * TLOC], BF16, "act")
        wbuf = self.sb([128, 24576], BF16, "wb")
        pr1 = self.sb([128, 8], F32, "pr1")
        pr2 = self.sb([128, 8], F32, "pr2")
        pr3 = self.sb([128, 8], F32, "pr3")
        st1 = self.sb([128, 1], F32, "st1")

        P.do("gpsimd", lambda g: g.memset(esel[:], 0.0))
        P.do("gpsimd", lambda g: g.affine_select(
            out=esel[:].rearrange("j (e p) -> j e p", p=128),
            in_=esel[:].rearrange("j (e p) -> j e p", p=128),
            compare_op=ALU.not_equal, fill=1.0,
            base=0, pattern=[[1, 8], [0, 128]], channel_multiplier=-1),
            waits=[cur(self.s_g)])
        esel_ready = cur(self.s_g)
        P.dma("sync", rsb[:].rearrange("p (k f) -> p k f", f=8),
              w["L1_ffn_router"].rearrange("(k p) f -> p k f", p=128), self.s_ld)
        rw_ready = cur(self.s_ld)

        self.emit_ln_stats()
        with self.subphase():
            self.emit_xlnT(quant=False)
        xv = self.xlnT[:].rearrange("p (k t) -> p k t", t=TLOC)

        with self.subphase():
            pt32 = self.ps([128, D], F32, "pt32")
            plg = self.ps([128, 8], F32, "plg")
            ptc = self.ps([8, 128], F32, "ptc")
            prev_tr = None
            prev_lg = None
            prev_ctc = None
            for i in range(NCH):
                self.emit_ln_chunk(i, rt32[:],
                                   waits=([prev_tr] if prev_tr else
                                          [self.ln_ready]))
                lnr = cur(self.s_v)
                for kc in range(KC):
                    P.do("tensor", lambda t, kc=kc: t.transpose(
                        pt32[:, kc * 128:(kc + 1) * 128],
                        rt32[:, kc * 128:(kc + 1) * 128], self.identf[:]),
                        waits=([lnr, self.ident_ready, cur(self.s_v)] +
                               self.psum_guard) if kc == 0 else [],
                        incs=[(self.s_t, 1)] if kc == KC - 1 else [])
                prev_tr = cur(self.s_t)
                P.do("vector", lambda v: v.tensor_copy(out=c32T[:], in_=pt32[:]),
                     waits=[prev_tr])
                evd = cur(self.s_v)
                wlg = [evd, rw_ready] + ([prev_lg] if prev_lg else [])
                for kc in range(KC):
                    P.do("tensor", lambda t, kc=kc: t.matmul(
                        plg[:], c32T[:, kc * 128:(kc + 1) * 128],
                        rsb[:, kc * 8:(kc + 1) * 8],
                        start=(kc == 0), stop=(kc == KC - 1)),
                        waits=wlg if kc == 0 else [],
                        incs=[(self.s_t, 1)] if kc == KC - 1 else [])
                mmd = cur(self.s_t)
                P.do("vector", lambda v: v.reduce_max(
                    out=st1[:], in_=plg[:], axis=X, negate=True), waits=[mmd])
                P.do("scalar", lambda s: s.activation(
                    out=pr1[:], in_=plg[:], func=AF.Exp, bias=st1[:, 0:1],
                    accum_out=pr3[:, 0:1]), waits=[cur(self.s_v)])
                prev_lg = cur(self.s_s)
                P.do("vector", lambda v: v.reciprocal(
                    out=st1[:], in_=pr3[:, 0:1]), waits=[prev_lg])
                P.do("vector", lambda v: v.tensor_scalar_mul(
                    pr1[:], pr1[:], st1[:, 0:1]))
                P.do("vector", lambda v: v.reduce_max(out=st1[:], in_=pr1[:], axis=X))
                P.do("vector", lambda v: v.tensor_single_scalar(
                    out=pr2[:], in_=pr1[:], scalar=st1[:, 0:1], op=ALU.is_ge))
                P.do("vector", lambda v: v.tensor_tensor(
                    out=pr3[:], in0=pr1[:], in1=pr2[:], op=ALU.mult))
                P.do("vector", lambda v: v.tensor_tensor(
                    out=pr3[:], in0=pr1[:], in1=pr3[:], op=ALU.subtract))
                P.do("vector", lambda v: v.reduce_max(out=st1[:], in_=pr3[:], axis=X))
                P.do("vector", lambda v: v.tensor_single_scalar(
                    out=pr3[:], in_=pr3[:], scalar=st1[:, 0:1], op=ALU.is_ge))
                P.do("vector", lambda v: v.tensor_tensor(
                    out=pr2[:], in0=pr2[:], in1=pr3[:], op=ALU.add))
                P.do("vector", lambda v: v.tensor_tensor(
                    out=pr1[:], in0=pr1[:], in1=pr2[:], op=ALU.mult))
                P.do("vector", lambda v: v.tensor_copy(
                    out=rt32[:, 0:8], in_=pr1[:]),
                    waits=[prev_ctc] if prev_ctc else [])
                cpd = cur(self.s_v)
                P.do("tensor", lambda t: t.transpose(
                    ptc[:], rt32[:, 0:8], self.identf[:]),
                    waits=[cpd] + ([prev_ctc] if prev_ctc else []),
                    incs=[(self.s_t, 1)])
                trc = cur(self.s_t)
                P.do("vector", lambda v, i=i: v.tensor_copy(
                    out=coefT[:, i * 128:(i + 1) * 128], in_=ptc[:]),
                    waits=[trc])
                prev_ctc = cur(self.s_v)
                prev_tr = trc
        coef_ready = cur(self.s_v)

        wsg = wbuf[:, 0:8192]
        wsu = wbuf[:, 8192:16384]
        wsd = wbuf[:, 16384:24576]
        self.prep_direct_into(wsg, w["L1_ffn_shared_gate"], D, D)
        g2 = self.prep_direct_into(wsu, w["L1_ffn_shared_up"], D, D)
        g3 = self.prep_direct_into(wsd, w["L1_ffn_shared_down"], D, D)
        act_ready = self.emit_glu(wsg, wsu, g2, act, 8, D, xv, KC,
                                  self.xlnT_ready, gs=gs)
        with self.subphase():
            self.emit_tokproj_add(act, TLOC, None, D, KC,
                                  [(i, i * 128) for i in range(NCH)],
                                  act_ready, g3,
                                  rhs_view=wsd.rearrange("p (k f) -> p k f", f=D))
        shared_t = cur(self.s_t)
        shared_v = cur(self.s_v)

        we_done = [(shared_t, shared_v), (shared_t, shared_v)]
        wready = [None, None]
        prev_down_t = shared_t
        for e in range(8):
            par = e % 2
            we = wbuf[:, par * 12288:(par + 1) * 12288]
            lw = list(we_done[par])
            self.P.dma("gpsimd",
                       we[:, 0:4096].rearrange("p (k f) -> p k f", f=512),
                       w["L1_ffn_experts_gate"][e].rearrange(
                           "(k p) f -> p k f", p=128), self.s_gp, waits=lw)
            self.P.dma("gpsimd",
                       we[:, 4096:8192].rearrange("p (k f) -> p k f", f=512),
                       w["L1_ffn_experts_up"][e].rearrange(
                           "(k p) f -> p k f", p=128), self.s_gp)
            self.P.dma("gpsimd",
                       we[:, 8192:12288].rearrange("p (k f) -> p k f", f=1024),
                       w["L1_ffn_experts_down"][e].rearrange(
                           "(k p) f -> p k f", p=128), self.s_gp)
            wready[par] = cur(self.s_gp)
            act_ready = self.emit_glu(
                we[:, 0:4096], we[:, 4096:8192], wready[par], act, 4, 512,
                xv, KC, self.xlnT_ready, coefT=coefT, e=e, esel=esel,
                coef_ready=coef_ready, gs=gs, act_war=[prev_down_t])
            with self.subphase():
                self.emit_tokproj_add(
                    act, TLOC, None, D, 4,
                    [(i, i * 128) for i in range(NCH)],
                    act_ready, wready[par],
                    rhs_view=we[:, 8192:12288].rearrange(
                        "p (k f) -> p k f", f=D))
            we_done[par] = (cur(self.s_t), cur(self.s_v))
            prev_down_t = cur(self.s_t)



def shard_x(x):
    xs = x.reshape(2, 16, 512, D)
    out = []
    for c in range(8):
        out.append(np.ascontiguousarray(np.concatenate(
            [xs[0, c], xs[0, 8 + c], xs[1, c], xs[1, 8 + c]], axis=0)))
    return out


def unshard_y(shards):
    y = np.empty((2, 16, 512, D), dtype=shards[0].dtype)
    for c in range(8):
        s = shards[c].reshape(4, 512, D)
        y[0, c] = s[0]
        y[0, 8 + c] = s[1]
        y[1, c] = s[2]
        y[1, 8 + c] = s[3]
    return y.reshape(2, 8192, D)


def build_nc(subs=tuple(range(8))):
    return Build(subs).build()


def make_in_maps(inputs):
    xsh = shard_x(np.ascontiguousarray(inputs["L0_attn_a_x"], dtype=np.float32))
    in_maps = []
    for c in range(8):
        m = {"xs": xsh[c],
             "cconst": np.array([[512.0 * c, float(c), 1e-5, 0, 0, 0, 0, 0]],
                                dtype=np.float32)}
        for k, v in inputs.items():
            if k == "L0_attn_a_x":
                continue
            m[k] = np.ascontiguousarray(v, dtype=np.float32)
        in_maps.append(m)
    return in_maps


def kernel(**inputs):
    nc = build_nc(tuple(range(8)))
    in_maps = make_in_maps(inputs)
    res = run_bass_kernel_spmd(nc, in_maps, core_ids=list(range(8)))
    return unshard_y([r["ys"] for r in res.results]).astype(np.float32)
